# version 01
# baseline (speedup 1.0000x reference)
import contextlib
import numpy as np
import ml_dtypes

import concourse.bass as bass
import concourse.mybir as mybir
from concourse.bass_utils import run_bass_kernel_spmd

F32 = mybir.dt.float32
BF16 = mybir.dt.bfloat16
AF = mybir.ActivationFunctionType
ALU = mybir.AluOpType
AX = mybir.AxisListType

MAGIC = float(np.float32(1.5 * 2.0 ** 23))
D = 1024
T = 4096
NDT = 8
NCC = T // 512
LN_EPS = 1e-5
K_STEEP = 10.0
V_TH_MIN = 0.1
QEPS = 1e-8

ENGS = ("sync", "tensor", "scalar", "vector", "gpsimd")


class SemC:
    __slots__ = ("h", "v")

    def __init__(self, h):
        self.h = h
        self.v = 0


class Prog:
    def __init__(self, nc):
        self.nc = nc
        self.es = contextlib.ExitStack()
        self.es.enter_context(nc.cleanup_on_exit())
        self.scopes = [self.es]
        self.ops = {e: [] for e in ENGS}
        self._fence = {"sync": 0, "gpsimd": 0}
        self.s = {e: SemC(self.es.enter_context(nc.semaphore(f"sem_{e}")))
                  for e in ENGS}
        self._setl = self.es.enter_context(
            nc.sbuf_tensor("settle_buf", [128, 512], F32))

    def sbuf(self, name, shape, dt=F32):
        self._uid = getattr(self, "_uid", 0) + 1
        return self.scopes[-1].enter_context(
            self.nc.sbuf_tensor(f"{name}_{self._uid}", shape, dt))

    def psum(self, name, shape, dt=F32):
        self._uid = getattr(self, "_uid", 0) + 1
        return self.scopes[-1].enter_context(
            self.nc.psum_tensor(f"{name}_{self._uid}", shape, dt))

    @contextlib.contextmanager
    def scope(self):
        st = contextlib.ExitStack()
        self.scopes.append(st)
        try:
            yield
        finally:
            self.barrier()
            self.scopes.pop()
            st.close()

    def on(self, eng, fn):
        self.ops[eng].append(fn)

    def wait(self, eng, *prods):
        for pe in prods:
            sc = self.s[pe]
            if sc.v > 0:
                self.on(eng, lambda e, h=sc.h, t=sc.v: e.wait_ge(h, t))
                if pe in ("sync", "gpsimd"):
                    self._fence[pe] = sc.v

    _fence = None

    def dma(self, eng, out, in_, slow=False):
        sc = self.s[eng]
        fv = self._fence.get(eng)
        if fv:
            self.on(eng, lambda e, h=sc.h, t=fv: e.wait_ge(h, t))
            self._fence[eng] = 0
        sc.v += 16
        if slow:
            self.on(eng, lambda e, o=out, i=in_, h=sc.h:
                    e.dma_start(out=o, in_=i,
                                allow_slow_non_contiguous=True).then_inc(h, 16))
        else:
            self.on(eng, lambda e, o=out, i=in_, h=sc.h:
                    e.dma_start(out=o, in_=i).then_inc(h, 16))

    def run(self, eng, fn):
        sc = self.s[eng]
        if eng in ("vector", "scalar", "gpsimd") and sc.v > 0:
            self.on(eng, lambda e, h=sc.h, t=sc.v: e.wait_ge(h, t))
        sc.v += 1
        self.on(eng, lambda e, f=fn, h=sc.h: f(e).then_inc(h, 1))

    def runw(self, eng, fn):
        sc = self.s[eng]
        if sc.v > 0:
            self.on(eng, lambda e, h=sc.h, t=sc.v: e.wait_ge(h, t))
        sc.v += 1
        self.on(eng, lambda e, f=fn, h=sc.h: f(e).then_inc(h, 1))

    def settle(self):
        sc = self.s["vector"]
        if sc.v > 0:
            self.on("vector", lambda e, h=sc.h, t=sc.v: e.wait_ge(h, t))
        sc.v += 1
        self.on("vector", lambda e, h=sc.h, t=self._setl:
                e.memset(t[:, :], 0.0).then_inc(h, 1))

    def sbuf_root(self, name, shape, dt=F32):
        self._uid = getattr(self, "_uid", 0) + 1
        return self.es.enter_context(
            self.nc.sbuf_tensor(f"{name}_{self._uid}", shape, dt))

    def barrier(self):
        for eng in ENGS:
            self.wait(eng, *[o for o in ENGS if o != eng])

    def finalize(self, final_eng="gpsimd"):
        self.wait(final_eng, *[o for o in ENGS if o != final_eng])
        sc = self.s[final_eng]
        if sc.v > 0:
            self.on(final_eng, lambda e, h=sc.h, t=sc.v: e.wait_ge(h, t))
        with self.nc.Block() as blk:
            for name in ENGS:
                fns = self.ops[name]
                if not fns:
                    continue

                def runner(e, fns=fns):
                    for f in fns:
                        f(e)
                getattr(blk, name)(runner)
        self.es.close()


CONST = {}


def _consts():
    if not CONST:
        CONST["ones_col_bf"] = np.ones((128, 1), ml_dtypes.bfloat16)
        CONST["ones_row"] = np.ones((1, 128), np.float32)
        CONST["eye_bf"] = np.eye(128, dtype=ml_dtypes.bfloat16)
        CONST["eye_f32"] = np.eye(128, dtype=np.float32)
        CONST["ramp"] = np.arange(512, dtype=np.float32)[None, :]
        CONST["revramp"] = np.arange(127, -1, -1, dtype=np.float32)[None, :]
        oh16 = np.zeros((16, 16 * 128), ml_dtypes.bfloat16)
        for n in range(16):
            oh16[n, 128 * n:128 * (n + 1)] = 1.0
        CONST["oh16"] = oh16
        oh8 = np.zeros((8, 8 * 128), ml_dtypes.bfloat16)
        for n in range(8):
            oh8[n, 128 * n:128 * (n + 1)] = 1.0
        CONST["oh8"] = oh8
        oh32f = np.zeros((32, 32 * 128), np.float32)
        for n in range(32):
            oh32f[n, 128 * n:128 * (n + 1)] = 1.0
        CONST["oh32f"] = oh32f
        CONST["oh8f"] = oh8.astype(np.float32)
    return CONST


def decl_consts(nc):
    out = {}
    for k, v in _consts().items():
        dt = BF16 if v.dtype == ml_dtypes.bfloat16 else F32
        out[k] = nc.dram_tensor("c_" + k, list(v.shape), dt,
                                kind="ExternalInput").ap()
    return out


def const_inputs():
    return {"c_" + k: np.ascontiguousarray(v) for k, v in _consts().items()}


CT_SHAPES = {"ones_col_bf": ([128, 1], BF16), "ones_row": ([1, 128], F32),
             "eye_bf": ([128, 128], BF16), "eye_f32": ([128, 128], F32),
             "ramp": ([1, 512], F32), "revramp": ([1, 128], F32),
             "oh16": ([16, 2048], BF16), "oh8": ([8, 1024], BF16),
             "oh32f": ([32, 4096], F32), "oh8f": ([8, 1024], F32)}


def load_const_tiles(p, cd, names=("ones_col_bf", "ones_row", "eye_bf",
                                   "eye_f32", "ramp", "revramp", "oh16",
                                   "oh8")):
    t = {}
    for k in names:
        sh, dt = CT_SHAPES[k]
        t[k] = p.sbuf("ct_" + k, sh, dt)
        p.dma("sync", t[k][:, :], cd[k][:, :])
    return t


def load_w_bf(p, name, dram, kdim, mdim, stage):
    nk = -(-kdim // 128)
    tiles = []
    for i in range(nk):
        rows = min(128, kdim - i * 128)
        bft = p.sbuf(f"{name}{i}", [128, mdim], BF16)
        p.wait("sync", "vector")
        p.dma("sync", stage[0:rows, 0:mdim], dram[i * 128:i * 128 + rows, :])
        p.wait("vector", "sync")
        p.run("vector", lambda e, a=bft, b=stage, r=rows, m=mdim:
              e.tensor_copy(out=a[0:r, 0:m], in_=b[0:r, 0:m]))
        tiles.append(bft)
    return tiles


def load_vec_cols(p, name, dram, n, ct=None, ps=None):
    nj = n // 128
    t = p.sbuf(name, [128, nj], F32)
    tr = p.sbuf(name + "r", [nj, 128], F32)
    p.dma("sync", tr[0:nj, 0:128], dram.rearrange("(j q) -> j q", q=128))
    p.wait("tensor", "sync", "vector", "scalar")
    p.run("tensor", lambda e: e.transpose(
        out=ps[0:128, 0:nj], in_=tr[0:nj, 0:128],
        identity=ct["eye_f32"][0:nj, 0:nj]))
    p.wait("vector", "tensor")
    p.run("vector", lambda e: e.tensor_copy(out=t[:, :], in_=ps[0:128, 0:nj]))
    p.settle()
    return t


def emit_ln(p, ct, src_dram, dst_bf_dram, sb, PS, wfull_sb=None,
            gate_tiles=None, dst_f32_dram=None, rmax=None, gb=None,
            ut_dram=None, psb=None):
    xa, xb, xq, o32 = sb["fa"], sb["fb"], sb["fq"], sb["fo"]
    srow, tmax = sb["srow"], sb["tmax"]
    ps_stat, ps_arep, ps_brep, ps_g = PS[3], PS[4], PS[5], PS[6]

    for cc in range(NCC):
        csl = slice(cc * 512, (cc + 1) * 512)
        p.barrier()
        for k in range(NDT):
            p.dma("sync", xa[k][:, :], src_dram[128 * k:128 * (k + 1), csl])
        p.wait("vector", "sync")
        p.wait("scalar", "sync")
        for k in range(NDT):
            p.run("vector", lambda e, a=xb[k], b=xa[k]:
                  e.tensor_copy(out=a[:, :], in_=b[:, :]))
            p.run("scalar", lambda e, a=xq[k], b=xa[k]:
                  e.activation(out=a[:, :], in_=b[:, :], func=AF.Square))
        p.wait("tensor", "vector", "scalar")
        for k in range(NDT):
            p.run("tensor", lambda e, r=xb[k], w=ct["ones_col_bf"],
                  st=(k == 0), sp=(k == NDT - 1):
                  e.matmul(out=ps_stat[0:1, 0:512], lhsT=w[:, :], rhs=r[:, :],
                           start=st, stop=sp))
        for k in range(NDT):
            p.run("tensor", lambda e, r=xq[k], w=ct["ones_col_bf"],
                  st=(k == 0), sp=(k == NDT - 1):
                  e.matmul(out=ps_stat[32:33, 0:512], lhsT=w[:, :], rhs=r[:, :],
                           start=st, stop=sp))
        m_row = srow[0:1, 0:512]
        v_row = srow[0:1, 512:1024]
        a_row = srow[0:1, 1024:1536]
        b_row = srow[0:1, 1536:2048]
        p.wait("vector", "tensor")
        p.run("vector", lambda e: e.tensor_scalar(
            out=m_row, in0=ps_stat[0:1, 0:512], scalar1=1.0 / D, scalar2=None,
            op0=ALU.mult))
        p.run("vector", lambda e: e.tensor_tensor(
            out=v_row, in0=m_row, in1=m_row, op=ALU.mult))
        p.run("vector", lambda e: e.scalar_tensor_tensor(
            out=v_row, in0=ps_stat[32:33, 0:512], scalar=1.0 / D, in1=v_row,
            op0=ALU.mult, op1=ALU.subtract))
        p.run("vector", lambda e: e.tensor_scalar(
            out=v_row, in0=v_row, scalar1=float(LN_EPS), scalar2=None,
            op0=ALU.add))
        p.wait("scalar", "vector")
        p.run("scalar", lambda e: e.activation(
            out=v_row, in_=v_row, func=AF.Sqrt))
        p.wait("vector", "scalar")
        p.run("vector", lambda e: e.reciprocal(out=a_row, in_=v_row))
        p.run("vector", lambda e: e.scalar_tensor_tensor(
            out=b_row, in0=m_row, scalar=-1.0, in1=a_row,
            op0=ALU.mult, op1=ALU.mult))
        p.wait("tensor", "vector")
        p.run("tensor", lambda e: e.matmul(
            out=ps_arep[:, 0:512], lhsT=ct["ones_row"][:, :], rhs=a_row,
            start=True, stop=True))
        p.run("tensor", lambda e: e.matmul(
            out=ps_brep[:, 0:512], lhsT=ct["ones_row"][:, :], rhs=b_row,
            start=True, stop=True))
        p.wait("vector", "tensor", "gpsimd")
        for k in range(NDT):
            p.run("vector", lambda e, xk=xa[k], o=o32[k]:
                  e.tensor_tensor(out=o[:, :], in0=xk[:, :],
                                  in1=ps_arep[:, 0:512], op=ALU.mult))
            p.run("vector", lambda e, o=o32[k]:
                  e.tensor_tensor(out=o[:, :], in0=o[:, :],
                                  in1=ps_brep[:, 0:512], op=ALU.add))
        if gb is not None:
            g_c, b_c = gb
            for k in range(NDT):
                p.run("vector", lambda e, o=o32[k], k=k: e.tensor_scalar(
                    out=o[:, :], in0=o[:, :], scalar1=g_c[:, k:k + 1],
                    scalar2=b_c[:, k:k + 1], op0=ALU.mult, op1=ALU.add))
        p.wait("scalar", "vector")
        for k in range(NDT):
            p.run("scalar", lambda e, o=o32[k], b=xb[k]:
                  e.activation(out=b[:, :], in_=o[:, :], func=AF.Copy))
        if ut_dram is not None:
            for k in range(NDT):
                for s4 in range(4):
                    b = (k * 4 + s4) % 2
                    p.wait("tensor", "scalar", "vector", "gpsimd")
                    p.run("tensor", lambda e, k=k, s4=s4, b=b: e.transpose(
                        out=psb[:, 128 * b:128 * b + 128],
                        in_=xb[k][:, 128 * s4:128 * (s4 + 1)],
                        identity=ct["eye_bf"][:, :]))
                    p.wait("vector", "tensor")
                    p.run("vector", lambda e, b=b: e.tensor_copy(
                        out=sb["utt"][:, 0:128],
                        in_=psb[:, 128 * b:128 * b + 128]))
                    p.wait("gpsimd", "vector")
                    p.dma("gpsimd",
                          ut_dram[cc * 512 + 128 * s4:cc * 512 + 128 * s4 + 128,
                                  128 * k:128 * (k + 1)],
                          sb["utt"][:, 0:128])
        p.wait("gpsimd", "vector", "scalar")
        for k in range(NDT):
            p.dma("gpsimd", dst_bf_dram[128 * k:128 * (k + 1), csl], xb[k][:, :])
            if dst_f32_dram is not None:
                p.dma("gpsimd", dst_f32_dram[128 * k:128 * (k + 1), csl],
                      o32[k][:, :])
        if rmax is not None:
            for k in range(NDT):
                p.runw("vector", lambda e, o=o32[k]:
                       e.reduce_max(out=tmax[:, 0:1], in_=o[:, :], axis=AX.X))
                if cc == 0 and k == 0:
                    p.runw("vector", lambda e:
                           e.tensor_copy(out=rmax[:, 0:1], in_=tmax[:, 0:1]))
                else:
                    p.runw("vector", lambda e:
                           e.tensor_tensor(out=rmax[:, 0:1], in0=rmax[:, 0:1],
                                           in1=tmax[:, 0:1], op=ALU.max))
        if wfull_sb is not None:
            gl, gt, gm, gw, gs = sb["gl"], sb["gt"], sb["gm"], sb["gw"], sb["gs"]
            p.wait("tensor", "vector")
            for k in range(NDT):
                p.run("tensor", lambda e, w=gate_tiles[k], r=o32[k],
                      st=(k == 0), sp=(k == NDT - 1):
                      e.matmul(out=ps_g[0:8, 0:512], lhsT=w[:, 0:8],
                               rhs=r[:, :], start=st, stop=sp))
            p.wait("vector", "tensor")
            p.run("vector", lambda e: e.tensor_copy(out=gl[:, :],
                                                    in_=ps_g[0:8, 0:512]))
            p.wait("tensor", "vector")
            for j in range(4):
                p.run("tensor", lambda e, j=j: e.transpose(
                    out=ps_g[0:128, 8 * j:8 * j + 8],
                    in_=gl[0:8, 128 * j:128 * (j + 1)],
                    identity=ct["eye_f32"][0:8, 0:8]))
            p.wait("vector", "tensor")
            p.run("vector", lambda e: e.tensor_copy(out=gt[:, :],
                                                    in_=ps_g[0:128, 0:32]))
            for j in range(4):
                jsl = slice(8 * j, 8 * j + 8)
                p.runw("vector", lambda e, jsl=jsl: e.max(
                    out=gm[:, jsl], in_=gt[:, jsl]))
            for j in range(4):
                m1 = gm[:, 8 * j + 0:8 * j + 1]
                m2 = gm[:, 8 * j + 1:8 * j + 2]
                d12, w1, w2 = gs[:, 0:1], gs[:, 1:2], gs[:, 2:3]
                t8 = gs[:, 8:16]
                jsl = slice(8 * j, 8 * j + 8)
                p.runw("vector", lambda e, m1=m1, m2=m2, d12=d12:
                       e.tensor_tensor(out=d12, in0=m1, in1=m2,
                                       op=ALU.subtract))
                p.wait("scalar", "vector")
                p.runw("scalar", lambda e, d12=d12, w1=w1:
                       e.activation(out=w1, in_=d12, func=AF.Sigmoid))
                p.wait("vector", "scalar")
                p.runw("vector", lambda e, w1=w1, w2=w2: e.tensor_scalar(
                    out=w2, in0=w1, scalar1=-1.0, scalar2=1.0,
                    op0=ALU.mult, op1=ALU.add))
                p.runw("vector", lambda e, jsl=jsl, m1=m1, w1=w1:
                       e.tensor_scalar(out=gw[:, jsl], in0=gt[:, jsl],
                                       scalar1=m1, scalar2=w1,
                                       op0=ALU.is_equal, op1=ALU.mult))
                p.runw("vector", lambda e, jsl=jsl, m2=m2, w2=w2, t8=t8:
                       e.tensor_scalar(out=t8, in0=gt[:, jsl],
                                       scalar1=m2, scalar2=w2,
                                       op0=ALU.is_equal, op1=ALU.mult))
                p.runw("vector", lambda e, jsl=jsl, t8=t8:
                       e.tensor_tensor(out=gw[:, jsl], in0=gw[:, jsl],
                                       in1=t8, op=ALU.add))
            p.wait("tensor", "vector")
            for j in range(4):
                p.run("tensor", lambda e, j=j: e.transpose(
                    out=ps_g[0:8, 128 * j:128 * (j + 1)],
                    in_=gw[:, 8 * j:8 * j + 8],
                    identity=ct["eye_f32"][:, :]))
            p.wait("vector", "tensor")
            p.run("vector", lambda e, csl=csl: e.tensor_copy(
                out=wfull_sb[0:8, csl], in_=ps_g[0:8, 0:512]))


def emit_uv(p, ct, xbf_dram, wu_t, wv_t, wfull_sb, sw_dram, sb, PS):
    xcc, su, sw = sb["xbcc"], sb["su"], sb["sw"]
    ps_u, ps_v, ps_w = PS[0], PS[1], PS[2]
    for cc in range(NCC):
        csl = slice(cc * 512, (cc + 1) * 512)
        p.barrier()
        for k in range(NDT):
            p.dma("sync", xcc[k][:, :], xbf_dram[128 * k:128 * (k + 1), csl])
        for h in range(16):
            e_idx = h // 2
            p.wait("tensor", "sync", "vector", "scalar")
            for k in range(NDT):
                p.run("tensor", lambda e, w=wu_t[h][k], r=xcc[k],
                      st=(k == 0), sp=(k == NDT - 1):
                      e.matmul(out=ps_u[:, 0:512], lhsT=w[:, :], rhs=r[:, :],
                               start=st, stop=sp))
            for k in range(NDT):
                p.run("tensor", lambda e, w=wv_t[h][k], r=xcc[k],
                      st=(k == 0), sp=(k == NDT - 1):
                      e.matmul(out=ps_v[:, 0:512], lhsT=w[:, :], rhs=r[:, :],
                               start=st, stop=sp))
            p.run("tensor", lambda e, ei=e_idx, csl=csl: e.matmul(
                out=ps_w[:, 0:512], lhsT=ct["oh8"][:, 128 * ei:128 * (ei + 1)],
                rhs=wfull_sb[0:8, csl], start=True, stop=True))
            p.wait("scalar", "tensor")
            p.run("scalar", lambda e: e.activation(
                out=su[:, :], in_=ps_u[:, 0:512], func=AF.Silu))
            p.wait("vector", "scalar", "gpsimd")
            p.run("vector", lambda e: e.tensor_tensor(
                out=su[:, :], in0=su[:, :], in1=ps_v[:, 0:512], op=ALU.mult))
            p.run("vector", lambda e: e.tensor_tensor(
                out=sw[:, :], in0=su[:, :], in1=ps_w[:, 0:512], op=ALU.mult))
            p.wait("gpsimd", "vector")
            p.dma("gpsimd", sw_dram[128 * h:128 * (h + 1), csl], sw[:, :])


def emit_matvec_out(p, ct, h_dram, n_h, w_t, x_dram, out_dram, sb, PS,
                    scale=0.5, sub_dram=None):
    hcc = sb["hcc"]
    xcc1 = sb["fa"]
    occ = sb["fo"]
    for cc in range(NCC):
        csl = slice(cc * 512, (cc + 1) * 512)
        p.barrier()
        for h in range(n_h):
            p.dma("sync", hcc[h][:, :], h_dram[128 * h:128 * (h + 1), csl])
        for k in range(NDT):
            p.dma("sync", xcc1[k][:, :], x_dram[128 * k:128 * (k + 1), csl])
        p.wait("tensor", "sync")
        for j in range(NDT):
            pj = PS[j % 7]
            if j >= 7:
                p.wait("tensor", "vector")
            for h in range(n_h):
                p.run("tensor", lambda e, w=w_t[j][h], r=hcc[h], pj=pj,
                      st=(h == 0), sp=(h == n_h - 1):
                      e.matmul(out=pj[:, 0:512], lhsT=w, rhs=r[:, :],
                               start=st, stop=sp))
            p.wait("vector", "tensor", "sync")
            p.run("vector", lambda e, j=j, pj=pj: e.scalar_tensor_tensor(
                out=occ[j][:, :], in0=pj[:, 0:512], scalar=float(scale),
                in1=xcc1[j][:, :], op0=ALU.mult, op1=ALU.add))
        if sub_dram is not None:
            p.wait("sync", "vector")
            for j in range(NDT):
                p.dma("sync", xcc1[j][:, :],
                      sub_dram[128 * j:128 * (j + 1), csl])
            p.wait("vector", "sync")
            for j in range(NDT):
                p.run("vector", lambda e, j=j: e.tensor_tensor(
                    out=occ[j][:, :], in0=occ[j][:, :], in1=xcc1[j][:, :],
                    op=ALU.subtract))
        p.wait("gpsimd", "vector")
        for j in range(NDT):
            p.dma("gpsimd", out_dram[128 * j:128 * (j + 1), csl], occ[j][:, :])


def emit_linear(p, ct, xbf_dram, w_t, n_out, sb, PS, evac):
    xcc = sb["xbcc"]
    nps = 4
    for cc in range(NCC):
        p.barrier()
        csl = slice(cc * 512, (cc + 1) * 512)
        for k in range(NDT):
            p.dma("sync", xcc[k][:, :], xbf_dram[128 * k:128 * (k + 1), csl])
        p.wait("tensor", "sync")
        for j in range(n_out):
            pj = PS[j % nps]
            if j >= nps:
                p.wait("tensor", "vector", "scalar")
            for k in range(NDT):
                p.run("tensor", lambda e, w=w_t[j][k], r=xcc[k], pj=pj,
                      st=(k == 0), sp=(k == NDT - 1):
                      e.matmul(out=pj[:, 0:512], lhsT=w, rhs=r[:, :],
                               start=st, stop=sp))
            evac(p, cc, j, pj)



def ln_scratch(p):
    sb = {}
    sb["fa"] = [p.sbuf(f"fa{k}", [128, 512], F32) for k in range(NDT)]
    sb["fb"] = [p.sbuf(f"fb{k}", [128, 512], BF16) for k in range(NDT)]
    sb["fq"] = [p.sbuf(f"fq{k}", [128, 512], BF16) for k in range(NDT)]
    sb["fo"] = [p.sbuf(f"fo{k}", [128, 512], F32) for k in range(NDT)]
    sb["srow"] = p.sbuf("srow", [1, 2048], F32)
    sb["tmax"] = p.sbuf("tmax", [128, 1], F32)
    return sb


def gate_scratch(p, sb):
    sb["gl"] = p.sbuf("gl", [8, 512], F32)
    sb["gt"] = p.sbuf("gt", [128, 32], F32)
    sb["gm"] = p.sbuf("gm", [128, 32], F32)
    sb["gw"] = p.sbuf("gw", [128, 32], F32)
    sb["gs"] = p.sbuf("gs", [128, 16], F32)
    return sb


def uv_scratch(p):
    sb = {}
    sb["xbcc"] = [p.sbuf(f"xbcc{k}", [128, 512], BF16) for k in range(NDT)]
    sb["su"] = p.sbuf("su", [128, 512], BF16)
    sb["sw"] = p.sbuf("sw", [128, 512], BF16)
    return sb


def mv_scratch(p, n_h):
    sb = {}
    sb["hcc"] = [p.sbuf(f"hcc{h}", [128, 512], BF16) for h in range(n_h)]
    sb["fa"] = [p.sbuf(f"va{k}", [128, 512], F32) for k in range(NDT)]
    sb["fo"] = [p.sbuf(f"vo{k}", [128, 512], F32) for k in range(NDT)]
    return sb


def build_launch_A(upto=99, dbg=()):
    nc = bass.Bass("TRN2", target_bir_lowering=False)
    cd = decl_consts(nc)
    x_in = nc.dram_tensor("x_in", [D, T], F32, kind="ExternalInput").ap()
    w = {}
    for nm, sh in [("gate_w", [D, 8]), ("Wu", [8, D, 256]), ("Wv", [8, D, 256]),
                   ("Wo", [8, 256, D]), ("W_xz", [D, 2 * D]), ("W_dt", [D, D]),
                   ("b_dt", [D]), ("A_log", [D, 16]), ("W_B", [D, 16]),
                   ("W_C", [D, 16]), ("D_skip", [D]), ("W_out", [D, D]),
                   ("v_th", [D])]:
        w[nm] = nc.dram_tensor(nm, sh, F32, kind="ExternalInput").ap()
    x3_out = nc.dram_tensor("x3", [D, T], F32, kind="ExternalOutput").ap()
    xln1_out = nc.dram_tensor("xln1", [D, T], F32, kind="ExternalOutput").ap()
    mx_out = nc.dram_tensor("mx", [1, 1], F32, kind="ExternalOutput").ap()

    dr = {}
    for nm, sh, dt in [("xln0", [D, T], BF16), ("sw", [2 * D, T], BF16),
                       ("x1", [D, T], F32), ("xlnm", [D, T], BF16),
                       ("xlnm32", [D, T], F32), ("xssm", [D, T], BF16),
                       ("z", [D, T], BF16), ("dt", [D, T], F32),
                       ("dtx", [D, T], BF16), ("yg", [D, T], BF16),
                       ("x2", [D, T], F32), ("xlnjunk", [D, T], BF16),
                       ("wf", [8, T], BF16)]:
        kind = "ExternalOutput" if nm in dbg else "Internal"
        dr[nm] = nc.dram_tensor("i_" + nm, sh, dt, kind=kind).ap()

    p = Prog(nc)
    ct = load_const_tiles(p, cd)

    psA = p.psum("psA", [128, 3584], F32)
    PS = [psA[:, 512 * i:512 * (i + 1)] for i in range(7)]
    psb16 = p.psum("psAb", [128, 1024], BF16)

    gate_f = []
    for i in range(NDT):
        gt_ = p.sbuf(f"gwf{i}", [128, 8], F32)
        p.dma("sync", gt_[:, :], w["gate_w"][128 * i:128 * (i + 1), :])
        gate_f.append(gt_)
    rmax = p.sbuf("rmax", [128, 1], F32)
    mrow = p.sbuf("mrow", [1, 8], F32)

    def ffn_block(x_dram, sw_dram, out_dram, pfix):
        with p.scope():
            wfull = p.sbuf("wfull", [8, T], BF16)
            with p.scope():
                sb = gate_scratch(p, ln_scratch(p))
                emit_ln(p, ct, x_dram, dr["xln0"], sb, PS, wfull_sb=wfull,
                        gate_tiles=gate_f)
            p.wait("gpsimd", "vector")
            p.dma("gpsimd", dr["wf"][0:8, :], wfull[0:8, :])
            with p.scope():
                stage = p.sbuf("wstage", [128, 2048], F32)
                sb = uv_scratch(p)
                wu_sb = [load_w_bf(p, f"{pfix}u{h}",
                                   w["Wu"][h // 2, :, (h % 2) * 128:
                                           (h % 2) * 128 + 128],
                                   D, 128, stage) for h in range(16)]
                wv_sb = [load_w_bf(p, f"{pfix}v{h}",
                                   w["Wv"][h // 2, :, (h % 2) * 128:
                                           (h % 2) * 128 + 128],
                                   D, 128, stage) for h in range(16)]
                emit_uv(p, ct, dr["xln0"], wu_sb, wv_sb, wfull, sw_dram, sb, PS)
        with p.scope():
            stage = p.sbuf("wstage", [128, 2048], F32)
            sb = mv_scratch(p, 16)
            wo_sb = [load_w_bf(p, f"{pfix}o{h}",
                               w["Wo"][h // 2, (h % 2) * 128:
                                       (h % 2) * 128 + 128, :],
                               128, D, stage)[0] for h in range(16)]
            wo_t = [[wo_sb[h][:, 128 * j:128 * (j + 1)] for h in range(16)]
                    for j in range(NDT)]
            emit_matvec_out(p, ct, sw_dram, 16, wo_t, x_dram, out_dram, sb, PS,
                            scale=0.5)

    if upto >= 1:
        ffn_block(x_in, dr["sw"], dr["x1"], "f")

    if upto < 2:
        p.finalize()
        return nc
    with p.scope():
        sb = ln_scratch(p)
        emit_ln(p, ct, dr["x1"], dr["xlnm"], sb, PS, dst_f32_dram=dr["xlnm32"])
    if upto < 3:
        p.finalize()
        return nc
    with p.scope():
        stage = p.sbuf("wstage", [128, 2048], F32)
        sb = uv_scratch(p)
        wxz_sb = load_w_bf(p, "wxz", w["W_xz"], D, 2 * D, stage)
        wxz_t = [[wxz_sb[k][:, 128 * j:128 * (j + 1)] for k in range(NDT)]
                 for j in range(16)]

        def evac_xz(p_, cc, j, pj):
            csl = slice(cc * 512, (cc + 1) * 512)
            dst = dr["xssm"] if j < 8 else dr["z"]
            jj = j % 8
            tile = sb["su"]
            p_.wait("vector", "tensor", "gpsimd")
            p_.run("vector", lambda e: e.tensor_copy(out=tile[:, :],
                                                     in_=pj[:, 0:512]))
            p_.wait("gpsimd", "vector")
            p_.dma("gpsimd", dst[128 * jj:128 * (jj + 1), csl], tile[:, :])

        emit_linear(p, ct, dr["xlnm"], wxz_t, 16, sb, PS, evac_xz)

    if upto < 4:
        p.finalize()
        return nc
    with p.scope():
        stage = p.sbuf("wstage", [128, 1024], F32)
        sb = uv_scratch(p)
        t32l = [p.sbuf(f"t32{j}", [128, 512], F32) for j in range(2)]
        tbfl = [p.sbuf(f"tbf{j}", [128, 512], BF16) for j in range(2)]
        wdt_sb = load_w_bf(p, "wdt", w["W_dt"], D, D, stage)
        wdt_t = [[wdt_sb[k][:, 128 * j:128 * (j + 1)] for k in range(NDT)]
                 for j in range(NDT)]
        bdt_c = load_vec_cols(p, "bdt", w["b_dt"], D, ct, PS[6])

        def evac_dt(p_, cc, j, pj):
            csl = slice(cc * 512, (cc + 1) * 512)
            t32 = t32l[j % 2]
            tbf = tbfl[j % 2]
            p_.wait("scalar", "tensor", "gpsimd")
            p_.run("scalar", lambda e: e.activation(
                out=t32[:, :], in_=pj[:, 0:512], func=AF.Exp,
                bias=bdt_c[:, j:j + 1]))
            p_.run("scalar", lambda e: e.activation(
                out=t32[:, :], in_=t32[:, :], func=AF.Ln, bias=1.0))
            p_.wait("vector", "scalar", "gpsimd")
            xj = sb["xbcc"][j]
            p_.run("vector", lambda e, xj=xj: e.tensor_tensor(
                out=tbf[:, :], in0=t32[:, :], in1=xj[:, :],
                op=ALU.mult))
            p_.wait("gpsimd", "vector", "scalar")
            p_.dma("gpsimd", dr["dt"][128 * j:128 * (j + 1), csl], t32[:, :])
            p_.dma("gpsimd", dr["dtx"][128 * j:128 * (j + 1), csl], tbf[:, :])

        emit_linear(p, ct, dr["xssm"], wdt_t, NDT, sb, PS, evac_dt)

    a_sb = []
    for k in range(NDT):
        at = p.sbuf(f"aA{k}", [128, 16], F32)
        p.dma("sync", at[:, :], w["A_log"][128 * k:128 * (k + 1), :])
        p.wait("scalar", "sync")
        p.run("scalar", lambda e, at=at: e.activation(
            out=at[:, :], in_=at[:, :], func=AF.Exp))
        p.wait("vector", "scalar")
        p.run("vector", lambda e, at=at: e.tensor_scalar(
            out=at[:, :], in0=at[:, :], scalar1=-1.0, scalar2=None,
            op0=ALU.mult))
        a_sb.append(at)
    dsk_c = load_vec_cols(p, "dsk", w["D_skip"], D, ct, PS[6])
    vth_c = load_vec_cols(p, "vth", w["v_th"], D, ct, PS[6])
    p.wait("vector", "sync")
    p.run("vector", lambda e: e.tensor_scalar(
        out=vth_c[:, :], in0=vth_c[:, :], scalar1=float(V_TH_MIN),
        scalar2=-K_STEEP, op0=ALU.max, op1=ALU.mult))

    if upto < 5:
        p.finalize()
        return nc
    with p.scope():
        bm_sb = p.sbuf("bm", [16, T], BF16)
        cm_sb = p.sbuf("cm", [16, T], BF16)
        with p.scope():
            stage = p.sbuf("wstage", [128, 1024], F32)
            sbx = uv_scratch(p)
            wb_sb = load_w_bf(p, "wB", w["W_B"], D, 16, stage)
            wc_sb = load_w_bf(p, "wC", w["W_C"], D, 16, stage)
            for cc in range(NCC):
                csl = slice(cc * 512, (cc + 1) * 512)
                p.barrier()
                for k in range(NDT):
                    p.dma("sync", sbx["xbcc"][k][:, :],
                          dr["xlnm"][128 * k:128 * (k + 1), csl])
                p.wait("tensor", "sync")
                for k in range(NDT):
                    p.run("tensor", lambda e, k=k, st=(k == 0),
                          sp=(k == NDT - 1):
                          e.matmul(out=PS[0][0:16, 0:512],
                                   lhsT=wb_sb[k][:, 0:16],
                                   rhs=sbx["xbcc"][k][:, :], start=st, stop=sp))
                for k in range(NDT):
                    p.run("tensor", lambda e, k=k, st=(k == 0),
                          sp=(k == NDT - 1):
                          e.matmul(out=PS[1][0:16, 0:512],
                                   lhsT=wc_sb[k][:, 0:16],
                                   rhs=sbx["xbcc"][k][:, :], start=st, stop=sp))
                p.wait("vector", "tensor")
                p.run("vector", lambda e, csl=csl: e.tensor_copy(
                    out=bm_sb[0:16, csl], in_=PS[0][0:16, 0:512]))
                p.run("vector", lambda e, csl=csl: e.tensor_copy(
                    out=cm_sb[0:16, csl], in_=PS[1][0:16, 0:512]))

        TC = 1024
        NSC = T // TC
        if upto >= 5.5:
          with p.scope():
            dtc = [p.sbuf(f"dtc{k}", [128, TC], F32) for k in range(NDT)]
            dxc = [p.sbuf(f"dxc{k}", [128, TC], BF16) for k in range(NDT)]
            brep = [p.sbuf(f"brep{n}", [128, TC], BF16) for n in range(16)]
            crep = [p.sbuf(f"crep{n}", [128, TC], BF16) for n in range(16)]
            dec = p.sbuf("dec", [128, TC], F32)
            inp = p.sbuf("inp", [128, TC], F32)
            scn = p.sbuf("scn", [128, TC], F32)
            yk = [p.sbuf(f"ykk{k}", [128, TC], F32) for k in range(NDT)]
            st8 = p.sbuf("st8", [128, 128], F32)
            xsk = p.sbuf("xsk", [128, TC], BF16)
            zk = p.sbuf("zk", [128, TC], BF16)
            spk = p.sbuf("spk", [128, TC], F32)
            p.wait("vector", "sync")
            p.run("vector", lambda e: e.memset(st8[:, :], 0.0))
            for cc in range(NSC):
                csl = slice(cc * TC, (cc + 1) * TC)
                p.barrier()
                for k in range(NDT):
                    p.dma("sync", dtc[k][:, :],
                          dr["dt"][128 * k:128 * (k + 1), csl])
                    p.dma("sync", dxc[k][:, :],
                          dr["dtx"][128 * k:128 * (k + 1), csl])
                p.wait("tensor", "vector")
                for n in range(16 if upto >= 5.58 else 0):
                    for half in range(2):
                        hs = slice(cc * TC + half * 512,
                                   cc * TC + half * 512 + 512)
                        p.wait("tensor", "vector")
                        p.run("tensor", lambda e, n=n, hs=hs, b=(n % 3):
                              e.matmul(out=PS[b][:, 0:512],
                                       lhsT=ct["oh16"][:, 128 * n:128 * (n + 1)],
                                       rhs=bm_sb[0:16, hs],
                                       start=True, stop=True))
                        p.run("tensor", lambda e, n=n, hs=hs, b=3 + (n % 3):
                              e.matmul(out=PS[b][:, 0:512],
                                       lhsT=ct["oh16"][:, 128 * n:128 * (n + 1)],
                                       rhs=cm_sb[0:16, hs],
                                       start=True, stop=True))
                        p.wait("vector", "tensor")
                        p.run("vector", lambda e, n=n, half=half, b=(n % 3):
                              e.tensor_copy(
                                  out=brep[n][:, 512 * half:512 * half + 512],
                                  in_=PS[b][:, 0:512]))
                        p.run("vector", lambda e, n=n, half=half, b=3 + (n % 3):
                              e.tensor_copy(
                                  out=crep[n][:, 512 * half:512 * half + 512],
                                  in_=PS[b][:, 0:512]))
                for k in range(NDT):
                    if upto < 5.7:
                        break
                    for n in range(16):
                        sc_col = k * 16 + n
                        p.wait("scalar", "sync", "vector")
                        p.run("scalar", lambda e, k=k, n=n: e.activation(
                            out=dec[:, :], in_=dtc[k][:, :], func=AF.Exp,
                            scale=a_sb[k][:, n:n + 1]))
                        p.wait("vector", "scalar")
                        p.run("vector", lambda e, k=k, n=n: e.tensor_tensor(
                            out=inp[:, :], in0=dxc[k][:, :], in1=brep[n][:, :],
                            op=ALU.mult))
                        p.run("vector", lambda e, c=sc_col:
                              e.tensor_tensor_scan(
                                  out=scn[:, :], data0=dec[:, :],
                                  data1=inp[:, :], initial=st8[:, c:c + 1],
                                  op0=ALU.mult, op1=ALU.add))
                        p.runw("vector", lambda e, c=sc_col: e.tensor_copy(
                            out=st8[:, c:c + 1], in_=scn[:, TC - 1:TC]))
                        if upto < 5.8:
                            continue
                        p.run("vector", lambda e, n=n: e.tensor_tensor(
                            out=scn[:, :], in0=scn[:, :], in1=crep[n][:, :],
                            op=ALU.mult))
                        if n == 0:
                            p.run("vector", lambda e, k=k: e.tensor_copy(
                                out=yk[k][:, :], in_=scn[:, :]))
                        else:
                            p.run("vector", lambda e, k=k: e.tensor_tensor(
                                out=yk[k][:, :], in0=yk[k][:, :],
                                in1=scn[:, :], op=ALU.add))
                    if upto < 5.9:
                        continue
                    p.wait("sync", "vector", "gpsimd")
                    p.dma("sync", xsk[:, :],
                          dr["xssm"][128 * k:128 * (k + 1), csl])
                    p.dma("sync", zk[:, :], dr["z"][128 * k:128 * (k + 1), csl])
                    p.wait("vector", "sync")
                    p.run("vector", lambda e, k=k: e.scalar_tensor_tensor(
                        out=yk[k][:, :], in0=xsk[:, :],
                        scalar=dsk_c[:, k:k + 1], in1=yk[k][:, :],
                        op0=ALU.mult, op1=ALU.add))
                    p.wait("scalar", "vector")
                    p.run("scalar", lambda e, k=k: e.activation(
                        out=spk[:, :], in_=yk[k][:, :], func=AF.Sigmoid,
                        scale=K_STEEP, bias=vth_c[:, k:k + 1]))
                    p.run("scalar", lambda e: e.activation(
                        out=zk[:, :], in_=zk[:, :], func=AF.Silu))
                    p.wait("vector", "scalar", "gpsimd")
                    p.run("vector", lambda e, k=k: e.tensor_tensor(
                        out=yk[k][:, :], in0=yk[k][:, :], in1=spk[:, :],
                        op=ALU.mult))
                    p.run("vector", lambda e, k=k: e.tensor_tensor(
                        out=xsk[:, :], in0=yk[k][:, :], in1=zk[:, :],
                        op=ALU.mult))
                    p.wait("gpsimd", "vector")
                    p.dma("gpsimd", dr["yg"][128 * k:128 * (k + 1), csl],
                          xsk[:, :])

    if upto < 6:
        p.finalize()
        return nc
    with p.scope():
        stage = p.sbuf("wstage", [128, 1024], F32)
        sb = mv_scratch(p, NDT)
        wout_sb = load_w_bf(p, "wout", w["W_out"], D, D, stage)
        wout_t = [[wout_sb[k][:, 128 * j:128 * (j + 1)] for k in range(NDT)]
                  for j in range(NDT)]
        emit_matvec_out(p, ct, dr["yg"], NDT, wout_t, dr["x1"], dr["x2"],
                        sb, PS, scale=1.0, sub_dram=dr["xlnm32"])

    if upto < 7:
        p.finalize()
        return nc
    ffn_block(dr["x2"], dr["sw"], x3_out, "g")

    if upto < 8:
        p.finalize()
        return nc
    with p.scope():
        sb = ln_scratch(p)
        emit_ln(p, ct, x3_out, dr["xlnjunk"], sb, PS, dst_f32_dram=xln1_out,
                rmax=rmax)
        p.wait("tensor", "vector")
        p.run("tensor", lambda e: e.transpose(
            out=PS[0][0:1, 0:128], in_=rmax[:, 0:1],
            identity=ct["eye_f32"][:, :]))
        p.wait("vector", "tensor")
        p.runw("vector", lambda e: e.reduce_max(
            out=mrow[0:1, 0:1], in_=PS[0][0:1, 0:128], axis=AX.X))
        p.wait("gpsimd", "vector")
        p.dma("gpsimd", mx_out[0:1, 0:1], mrow[0:1, 0:1])

    p.finalize()
    return nc





def emit_scalar_prep(p, ct, gmax_sb, srow, ps):
    s_col = p.sbuf("s_col", [128, 2], F32)
    p.wait("tensor", "sync", "vector")
    p.run("tensor", lambda e: e.transpose(
        out=ps[0:1, 0:8], in_=gmax_sb[0:8, 0:1],
        identity=ct["eye_f32"][0:8, 0:8]))
    p.wait("vector", "tensor")
    p.runw("vector", lambda e: e.reduce_max(
        out=srow[0:1, 0:1], in_=ps[0:1, 0:8], axis=AX.X))
    p.runw("vector", lambda e: e.tensor_scalar(
        out=srow[0:1, 0:1], in0=srow[0:1, 0:1], scalar1=float(QEPS),
        scalar2=1.0 / 255.0, op0=ALU.max, op1=ALU.mult))
    p.runw("vector", lambda e: e.reciprocal(
        out=srow[0:1, 1:2], in_=srow[0:1, 0:1]))
    p.wait("tensor", "vector")
    p.run("tensor", lambda e: e.matmul(
        out=ps[0:128, 8:10], lhsT=ct["ones_row"][:, :], rhs=srow[0:1, 0:2],
        start=True, stop=True))
    p.wait("vector", "tensor")
    p.runw("vector", lambda e: e.tensor_copy(out=s_col[:, 0:2],
                                             in_=ps[0:128, 8:10]))
    p.settle()
    return s_col[:, 0:1], s_col[:, 1:2]


def build_launch_B(uptoB=99):
    F = 4 * D
    NFT = F // 128
    nc = bass.Bass("TRN2", target_bir_lowering=False)
    cd = decl_consts(nc)
    xln_in = nc.dram_tensor("xln", [D, T], F32, kind="ExternalInput").ap()
    w1_in = nc.dram_tensor("w1", [F, D], F32, kind="ExternalInput").ap()
    w1t_in = nc.dram_tensor("w1t", [D, F], F32, kind="ExternalInput").ap()
    b1_in = nc.dram_tensor("b1", [F], F32, kind="ExternalInput").ap()
    gmax_in = nc.dram_tensor("gmax", [8, 1], F32, kind="ExternalInput").ap()
    h_out = nc.dram_tensor("h", [F, T], F32, kind="ExternalOutput").ap()
    mx_out = nc.dram_tensor("mx", [1, 1], F32, kind="ExternalOutput").ap()
    dbg_out = nc.dram_tensor("dbgB", [128, 128], F32, kind="ExternalOutput").ap()
    dbq_out = nc.dram_tensor("dbqB", [128, 512], F32, kind="ExternalOutput").ap()


    p = Prog(nc)
    ct = load_const_tiles(p, cd, ("ones_col_bf", "ones_row", "eye_f32",
                                  "oh32f"))
    psA = p.psum("psB", [128, 3584], F32)
    PS = [psA[:, 512 * i:512 * (i + 1)] for i in range(7)]
    psb16 = p.psum("psBb", [128, 1024], BF16)
    srow = p.sbuf("srowB", [1, 16], F32)
    gmax_sb = p.sbuf("gmaxB", [8, 1], F32)
    p.dma("sync", gmax_sb[:, :], gmax_in[:, :])
    s_colB, rs_col = emit_scalar_prep(p, ct, gmax_sb, srow, PS[6])

    sw_c = p.sbuf("sw_c", [128, NFT], F32)
    swr_c = p.sbuf("swr_c", [128, NFT], F32)
    sbq = p.sbuf("sbq", [128, NFT], F32)
    bterm = p.sbuf("bterm", [128, NFT], F32)
    b1c = load_vec_cols(p, "b1c", b1_in, F, ct, PS[6])
    with p.scope():
        stage = p.sbuf("wstageB", [128, 1024], F32)
        for i in range(NFT):
            p.barrier()
            p.dma("sync", stage[:, 0:D], w1_in[128 * i:128 * (i + 1), :])
            p.barrier()
            if i == 0:
                p.dma("gpsimd", dbq_out[:, :], stage[:, 0:512])
                p.barrier()
            p.runw("vector", lambda e, i=i, st=stage: e.reduce_max(
                out=sw_c[:, i:i + 1], in_=st[:, 0:D], axis=AX.X,
                apply_absolute_value=True))
        tq1 = p.sbuf("tq1", [128, NFT], F32)
        tq2 = p.sbuf("tq2", [128, NFT], F32)
        p.wait("vector", "sync")
        p.runw("vector", lambda e: e.tensor_scalar(
            out=tq1[:, :], in0=sw_c[:, :], scalar1=float(QEPS),
            scalar2=1.0 / 127.0, op0=ALU.max, op1=ALU.mult))
        p.runw("vector", lambda e: e.tensor_copy(out=sw_c[:, :],
                                                 in_=tq1[:, :]))
        p.wait("gpsimd", "vector")
        p.dma("gpsimd", dbg_out[0:128, 98:128], sw_c[:, 0:30])
        p.runw("vector", lambda e: e.reciprocal(out=swr_c[:, :],
                                                in_=sw_c[:, :]))
        p.runw("vector", lambda e: e.tensor_scalar(
            out=sbq[:, :], in0=sw_c[:, :], scalar1=s_colB, scalar2=None,
            op0=ALU.mult))
        p.runw("vector", lambda e: e.reciprocal(out=tq1[:, :],
                                                in_=sbq[:, :]))
        p.runw("vector", lambda e: e.tensor_tensor(
            out=tq2[:, :], in0=b1c[:, :], in1=tq1[:, :], op=ALU.mult))
        p.runw("vector", lambda e: e.tensor_scalar(
            out=tq1[:, :], in0=tq2[:, :], scalar1=MAGIC, scalar2=MAGIC,
            op0=ALU.add, op1=ALU.subtract))
        p.runw("vector", lambda e: e.tensor_tensor(
            out=bterm[:, :], in0=sbq[:, :], in1=tq1[:, :], op=ALU.mult))

    if uptoB < 2:
        p.finalize()
        return nc
    qw = [p.sbuf(f"qw1t{i}", [128, F], BF16) for i in range(NDT)]
    with p.scope():
        stage = p.sbuf("wstageB2", [128, F], F32)
        rep = p.sbuf("repB", [128, 512], F32)
        r32 = p.sbuf("r32B", [NFT, 128], F32)
        if uptoB >= 2.2:
            p.wait("tensor", "vector", "sync")
            p.run("tensor", lambda e: e.transpose(
                out=PS[0][0:NFT, 0:128], in_=swr_c[:, :],
                identity=ct["eye_f32"][:, :]))
            p.wait("vector", "tensor")
            p.run("vector", lambda e: e.tensor_copy(out=r32[:, :],
                                                    in_=PS[0][0:NFT, 0:128]))
            p.settle()
        for i in range(NDT):
            t = qw[i]
            p.wait("sync", "vector")
            for q4 in range(4):
                p.dma("sync", stage[:, 1024 * q4:1024 * (q4 + 1)],
                      w1t_in[128 * i:128 * (i + 1),
                             1024 * q4:1024 * (q4 + 1)])
            for fc in range(F // 512):
                p.barrier()
                if uptoB >= 2.4:
                    for s4 in range(4):
                        fr = fc * 4 + s4
                        p.run("tensor", lambda e, fr=fr, s4=s4: e.matmul(
                            out=PS[1][:, 128 * s4:128 * (s4 + 1)],
                            lhsT=ct["oh32f"][:, 128 * fr:128 * (fr + 1)],
                            rhs=r32[0:NFT, :], start=True, stop=True))
                    p.wait("vector", "tensor")
                if uptoB >= 2.6:
                    fsl = slice(512 * fc, 512 * (fc + 1))
                    p.run("vector", lambda e, fsl=fsl, st=stage:
                          e.tensor_tensor(
                              out=rep[:, :], in0=st[:, fsl],
                              in1=PS[1][:, 0:512], op=ALU.mult))
                    p.run("vector", lambda e, t=t, fsl=fsl: e.tensor_scalar(
                        out=t[:, fsl], in0=rep[:, :], scalar1=MAGIC,
                        scalar2=MAGIC, op0=ALU.add, op1=ALU.subtract))

    if uptoB < 3:
        p.finalize()
        return nc
    hmax = p.sbuf("hmax", [128, 1], F32)
    p.wait("vector", "sync")
    p.run("vector", lambda e: e.memset(hmax[:, :], 0.0))
    with p.scope():
        xcc = [p.sbuf(f"qx{k}", [128, 512], F32) for k in range(NDT)]
        qx = [p.sbuf(f"qxb{k}", [128, 512], BF16) for k in range(NDT)]
        hcc = p.sbuf("hccB", [128, 512], F32)
        tm = p.sbuf("tmB", [128, 1], F32)
        for cc in range(NCC):
            csl = slice(cc * 512, (cc + 1) * 512)
            p.barrier()
            for k in range(NDT):
                p.dma("sync", xcc[k][:, :], xln_in[128 * k:128 * (k + 1), csl])
            p.barrier()
            for k in range(NDT):
                p.run("vector", lambda e, k=k: e.tensor_scalar(
                    out=xcc[k][:, :], in0=xcc[k][:, :], scalar1=rs_col,
                    scalar2=MAGIC, op0=ALU.mult, op1=ALU.add))
                p.run("vector", lambda e, k=k: e.tensor_scalar(
                    out=xcc[k][:, :], in0=xcc[k][:, :], scalar1=MAGIC,
                    scalar2=0.0, op0=ALU.subtract, op1=ALU.max))
                p.run("vector", lambda e, k=k: e.tensor_scalar(
                    out=qx[k][:, :], in0=xcc[k][:, :], scalar1=255.0,
                    scalar2=None, op0=ALU.min))

            for j in range(NFT):
                pj = PS[j % 7]
                p.wait("tensor", "vector", "scalar")
                for k in range(NDT):
                    p.run("tensor", lambda e, w=qw[k], j=j, k=k, pj=pj,
                          st=(k == 0), sp=(k == NDT - 1):
                          e.matmul(out=pj[:, 0:512],
                                   lhsT=w[:, 128 * j:128 * (j + 1)],
                                   rhs=qx[k][:, :], start=st, stop=sp))
                p.wait("vector", "tensor", "gpsimd")
                p.run("vector", lambda e, j=j, pj=pj: e.tensor_scalar(
                    out=hcc[:, :], in0=pj[:, 0:512], scalar1=sbq[:, j:j + 1],
                    scalar2=bterm[:, j:j + 1], op0=ALU.mult, op1=ALU.add))
                p.run("vector", lambda e: e.tensor_scalar(
                    out=hcc[:, :], in0=hcc[:, :], scalar1=0.0, scalar2=None,
                    op0=ALU.max))
                p.runw("vector", lambda e: e.reduce_max(
                    out=tm[:, 0:1], in_=hcc[:, :], axis=AX.X))
                p.runw("vector", lambda e: e.tensor_tensor(
                    out=hmax[:, 0:1], in0=hmax[:, 0:1], in1=tm[:, 0:1],
                    op=ALU.max))
                p.wait("gpsimd", "scalar", "vector")
                p.dma("gpsimd", h_out[128 * j:128 * (j + 1), csl], hcc[:, :])
    p.wait("gpsimd", "vector", "scalar", "tensor", "sync")
    p.dma("gpsimd", dbg_out[0:128, 0:2], s_colB.tensor.ap()[:, 0:2])
    p.dma("gpsimd", dbg_out[0:128, 2:34], sbq[:, :])
    p.dma("gpsimd", dbg_out[0:128, 34:66], bterm[:, :])
    p.dma("gpsimd", dbg_out[0:128, 66:98], sw_c[:, :])
    p.wait("tensor", "vector", "sync")
    p.run("tensor", lambda e: e.transpose(
        out=PS[0][0:1, 0:128], in_=hmax[:, 0:1], identity=ct["eye_f32"][:, :]))
    p.wait("vector", "tensor")
    p.runw("vector", lambda e: e.reduce_max(
        out=srow[0:1, 2:3], in_=PS[0][0:1, 0:128], axis=AX.X))
    p.wait("gpsimd", "vector")
    p.dma("gpsimd", mx_out[0:1, 0:1], srow[0:1, 2:3])
    p.finalize()
    return nc


def emit_w2_quant(p, ct, w2_in, w2t_in, b2_in, gmax_sb, srow, PS):
    F = 4 * D
    NFT = F // 128
    s_col, rs_col = emit_scalar_prep(p, ct, gmax_sb, srow, PS[6])
    sw_c = p.sbuf("sw2c", [128, NDT], F32)
    swr_c = p.sbuf("sw2r", [128, NDT], F32)
    sbq = p.sbuf("sbq2", [128, NDT], F32)
    bterm = p.sbuf("bterm2", [128, NDT], F32)
    b2c = load_vec_cols(p, "b2c", b2_in, D, ct, PS[6])
    with p.scope():
        stage = p.sbuf("wstageC", [128, F], F32)
        for i in range(NDT):
            p.wait("sync", "vector")
            p.dma("sync", stage[:, :], w2_in[128 * i:128 * (i + 1), :])
            p.wait("vector", "sync")
            p.runw("vector", lambda e, i=i, st=stage: e.reduce_max(
                out=sw_c[:, i:i + 1], in_=st[:, :], axis=AX.X,
                apply_absolute_value=True))
        p.wait("vector", "sync")
        p.runw("vector", lambda e: e.tensor_scalar(
            out=sw_c[:, :], in0=sw_c[:, :], scalar1=float(QEPS),
            scalar2=1.0 / 127.0, op0=ALU.max, op1=ALU.mult))
        p.runw("vector", lambda e: e.reciprocal(out=swr_c[:, :],
                                                in_=sw_c[:, :]))
        p.runw("vector", lambda e: e.tensor_scalar(
            out=sbq[:, :], in0=sw_c[:, :], scalar1=s_col, scalar2=None,
            op0=ALU.mult))
        p.runw("vector", lambda e: e.reciprocal(out=bterm[:, :],
                                                in_=sbq[:, :]))
        p.runw("vector", lambda e: e.tensor_tensor(
            out=bterm[:, :], in0=b2c[:, :], in1=bterm[:, :], op=ALU.mult))
        p.runw("vector", lambda e: e.tensor_scalar(
            out=bterm[:, :], in0=bterm[:, :], scalar1=MAGIC, scalar2=MAGIC,
            op0=ALU.add, op1=ALU.subtract))
        p.runw("vector", lambda e: e.tensor_tensor(
            out=bterm[:, :], in0=sbq[:, :], in1=bterm[:, :], op=ALU.mult))
        p.settle()

    qw = [p.sbuf(f"qw2t{i}", [128, D], BF16) for i in range(F // 128)]
    with p.scope():
        stage = p.sbuf("wstageC2", [128, D], F32)
        rep = p.sbuf("repC", [128, D], F32)
        r8 = p.sbuf("r8C", [NDT, 128], F32)
        p.wait("tensor", "vector", "sync")
        p.run("tensor", lambda e: e.transpose(
            out=PS[0][0:NDT, 0:128], in_=swr_c[:, :],
            identity=ct["eye_f32"][:, :]))
        p.wait("vector", "tensor")
        p.run("vector", lambda e: e.tensor_copy(out=r8[:, :],
                                                in_=PS[0][0:NDT, 0:128]))
        p.settle()
        p.wait("tensor", "vector")
        for s8 in range(NDT):
            b = s8 % 2
            p.wait("tensor", "vector")
            p.run("tensor", lambda e, s8=s8, b=b: e.matmul(
                out=PS[b][:, 0:128],
                lhsT=ct["oh8f"][:, 128 * s8:128 * (s8 + 1)],
                rhs=r8[0:NDT, :], start=True, stop=True))
            p.wait("vector", "tensor")
            p.run("vector", lambda e, s8=s8, b=b: e.tensor_copy(
                out=rep[:, 128 * s8:128 * (s8 + 1)], in_=PS[b][:, 0:128]))
        for i in range(F // 128):
            t = qw[i]
            p.wait("sync", "vector")
            p.dma("sync", stage[:, :], w2t_in[128 * i:128 * (i + 1), :])
            p.wait("vector", "sync")
            p.run("vector", lambda e, t=t, st=stage: e.tensor_tensor(
                out=st[:, :], in0=st[:, :], in1=rep[:, :], op=ALU.mult))
            p.run("vector", lambda e, t=t, st=stage: e.tensor_scalar(
                out=t[:, :], in0=st[:, :], scalar1=MAGIC, scalar2=MAGIC,
                op0=ALU.add, op1=ALU.subtract))
    return qw, sbq, bterm, s_col, rs_col


def emit_qh_w2_resid(p, ct, h_in, x_in, out_dram, qw, sbq, bterm, s_q, PS,
                     scale=0.5):
    F = 4 * D
    NFT = F // 128
    with p.scope():
        hcc = [p.sbuf(f"hC{k}", [128, 512], F32) for k in range(NFT)]
        qh = [p.sbuf(f"qhC{k}", [128, 512], BF16) for k in range(NFT)]
        occ = [p.sbuf(f"oC{j}", [128, 512], F32) for j in range(NDT)]
        xcc = [p.sbuf(f"xC{j}", [128, 512], F32) for j in range(NDT)]
        for cc in range(NCC):
            csl = slice(cc * 512, (cc + 1) * 512)
            p.barrier()
            for k in range(NFT):
                p.dma("sync", hcc[k][:, :], h_in[128 * k:128 * (k + 1), csl])
            for j in range(NDT):
                p.dma("sync", xcc[j][:, :], x_in[128 * j:128 * (j + 1), csl])
            p.wait("vector", "sync")
            for k in range(NFT):
                p.run("vector", lambda e, k=k: e.tensor_scalar(
                    out=hcc[k][:, :], in0=hcc[k][:, :], scalar1=s_q,
                    scalar2=MAGIC, op0=ALU.mult, op1=ALU.add))
                p.run("vector", lambda e, k=k: e.tensor_scalar(
                    out=qh[k][:, :], in0=hcc[k][:, :], scalar1=MAGIC,
                    scalar2=None, op0=ALU.subtract))
            for j in range(NDT):
                pj = PS[j % 7]
                p.wait("tensor", "vector", "scalar")
                for k in range(NFT):
                    p.run("tensor", lambda e, w=qw[k], j=j, k=k, pj=pj,
                          st=(k == 0), sp=(k == NFT - 1):
                          e.matmul(out=pj[:, 0:512],
                                   lhsT=w[:, 128 * j:128 * (j + 1)],
                                   rhs=qh[k][:, :], start=st, stop=sp))
                p.wait("vector", "tensor")
                p.run("vector", lambda e, j=j, pj=pj: e.tensor_scalar(
                    out=occ[j][:, :], in0=pj[:, 0:512],
                    scalar1=sbq[:, j:j + 1], scalar2=bterm[:, j:j + 1],
                    op0=ALU.mult, op1=ALU.add))
                p.run("vector", lambda e, j=j: e.scalar_tensor_tensor(
                    out=occ[j][:, :], in0=occ[j][:, :], scalar=float(scale),
                    in1=xcc[j][:, :], op0=ALU.mult, op1=ALU.add))
            p.wait("gpsimd", "vector")
            for j in range(NDT):
                p.dma("gpsimd", out_dram[128 * j:128 * (j + 1), csl],
                      occ[j][:, :])


def build_launch_E():
    F = 4 * D
    nc = bass.Bass("TRN2", target_bir_lowering=False)
    cd = decl_consts(nc)
    x_in = nc.dram_tensor("x", [D, T], F32, kind="ExternalInput").ap()
    h_in = nc.dram_tensor("h", [F, T], F32, kind="ExternalInput").ap()
    w2_in = nc.dram_tensor("w2", [D, F], F32, kind="ExternalInput").ap()
    w2t_in = nc.dram_tensor("w2t", [F, D], F32, kind="ExternalInput").ap()
    b2_in = nc.dram_tensor("b2", [D], F32, kind="ExternalInput").ap()
    gmax_in = nc.dram_tensor("gmax", [8, 1], F32, kind="ExternalInput").ap()
    y_out = nc.dram_tensor("y", [D, T], F32, kind="ExternalOutput").ap()

    p = Prog(nc)
    ct = load_const_tiles(p, cd, ("ones_col_bf", "ones_row", "eye_f32",
                                  "oh8f"))
    psA = p.psum("psE", [128, 3584], F32)
    PS = [psA[:, 512 * i:512 * (i + 1)] for i in range(7)]
    psb16 = p.psum("psEb", [128, 1024], BF16)
    srow = p.sbuf("srowE", [1, 16], F32)
    gmax_sb = p.sbuf("gmaxE", [8, 1], F32)
    p.dma("sync", gmax_sb[:, :], gmax_in[:, :])
    qw, sbq, bterm, s_col, rs_col = emit_w2_quant(
        p, ct, w2_in, w2t_in, b2_in, gmax_sb, srow, PS)
    emit_qh_w2_resid(p, ct, h_in, x_in, y_out, qw, sbq, bterm, rs_col, PS)
    p.finalize()
    return nc




TWOPI = float(np.float32(2.0 * np.pi))
NS4 = 512
NNT = NS4 // 128
GCH = 128
NCH = T // GCH


def row_frac_sincos(p, rowv, tmp, out_sin, out_cos, width=512):
    a, b = tmp[0:1, 0:width], tmp[0:1, width:2 * width]
    p.run("vector", lambda e: e.tensor_scalar(
        out=a, in0=rowv, scalar1=MAGIC, scalar2=MAGIC, op0=ALU.add,
        op1=ALU.subtract))
    p.run("vector", lambda e: e.tensor_tensor(
        out=a, in0=rowv, in1=a, op=ALU.subtract))
    p.run("vector", lambda e: e.tensor_scalar(
        out=a, in0=a, scalar1=TWOPI, scalar2=None, op0=ALU.mult))
    p.wait("scalar", "vector")
    p.run("scalar", lambda e: e.activation(out=out_sin, in_=a, func=AF.Sin))
    p.wait("vector", "scalar")
    p.run("vector", lambda e: e.tensor_scalar(
        out=b, in0=rowv, scalar1=0.25, scalar2=None, op0=ALU.add))
    p.run("vector", lambda e: e.tensor_scalar(
        out=a, in0=b, scalar1=MAGIC, scalar2=MAGIC, op0=ALU.add,
        op1=ALU.subtract))
    p.run("vector", lambda e: e.tensor_tensor(
        out=a, in0=b, in1=a, op=ALU.subtract))
    p.run("vector", lambda e: e.tensor_scalar(
        out=a, in0=a, scalar1=TWOPI, scalar2=None, op0=ALU.mult))
    p.wait("scalar", "vector")
    p.run("scalar", lambda e: e.activation(out=out_cos, in_=a, func=AF.Sin))
    p.wait("vector", "scalar")


def rows_to_cols(p, ct, row_ap, out_cols, nj, ps, bounce):
    n = nj * 128
    assert n == 512
    p.wait("gpsimd", "vector", "scalar", "tensor", "sync")
    p.dma("gpsimd", bounce.rearrange("(a b) -> a b", a=1), row_ap[0:1, 0:n])
    p.wait("sync", "gpsimd")
    p.dma("sync", out_cols[:, 0:nj],
          bounce.rearrange("(j q) -> q j", q=128), slow=True)


def build_launch_C(dbg=()):
    F = 4 * D
    nc = bass.Bass("TRN2", target_bir_lowering=False)
    cd = decl_consts(nc)
    x_in = nc.dram_tensor("x", [D, T], F32, kind="ExternalInput").ap()
    h_in = nc.dram_tensor("h", [F, T], F32, kind="ExternalInput").ap()
    w2_in = nc.dram_tensor("w2", [D, F], F32, kind="ExternalInput").ap()
    w2t_in = nc.dram_tensor("w2t", [F, D], F32, kind="ExternalInput").ap()
    b2_in = nc.dram_tensor("b2", [D], F32, kind="ExternalInput").ap()
    gmax_in = nc.dram_tensor("gmax", [8, 1], F32, kind="ExternalInput").ap()
    lr_in = nc.dram_tensor("Lr", [NS4], F32, kind="ExternalInput").ap()
    li_in = nc.dram_tensor("Li", [NS4], F32, kind="ExternalInput").ap()
    cr_in = nc.dram_tensor("Cr", [D, NS4], F32, kind="ExternalInput").ap()
    ci_in = nc.dram_tensor("Ci", [D, NS4], F32, kind="ExternalInput").ap()
    pd_in = nc.dram_tensor("pD", [D], F32, kind="ExternalInput").ap()
    lg_in = nc.dram_tensor("lg", [D], F32, kind="ExternalInput").ap()
    lb_in = nc.dram_tensor("lb", [D], F32, kind="ExternalInput").ap()
    x5_out = nc.dram_tensor("x5", [D, T], F32, kind="ExternalOutput").ap()
    bounce = nc.dram_tensor("i_bounce", [512], F32, kind="Internal").ap()
    xln2_out = nc.dram_tensor("xln2", [D, T], F32, kind="ExternalOutput").ap()
    mx_out = nc.dram_tensor("mx", [1, 1], F32, kind="ExternalOutput").ap()

    dr = {}
    for nm, sh, dt in [("x4", [D, T], F32), ("t1", [D, T], F32),
                       ("u", [D, T], BF16), ("ut", [T, D], BF16),
                       ("ys", [D, T], BF16), ("junk", [D, T], BF16)]:
        kind = "ExternalOutput" if nm in dbg else "Internal"
        dr[nm] = nc.dram_tensor("i_" + nm, sh, dt, kind=kind).ap()
    dbg_out = None
    if dbg:
        dbg_out = nc.dram_tensor("dbgC", [128, 1024], F32,
                                 kind="ExternalOutput").ap()

    p = Prog(nc)
    ct = load_const_tiles(p, cd, ("ones_col_bf", "ones_row", "eye_bf",
                                  "eye_f32", "ramp", "revramp", "oh8f"))
    psA = p.psum("psC", [128, 3584], F32)
    PS = [psA[:, 512 * i:512 * (i + 1)] for i in range(7)]
    psb16 = p.psum("psCb", [128, 1024], BF16)
    srow = p.sbuf("srowC", [1, 16], F32)
    gmax_sb = p.sbuf("gmaxC", [8, 1], F32)
    p.dma("sync", gmax_sb[:, :], gmax_in[:, :])
    rmax = p.sbuf("rmaxC", [128, 1], F32)

    with p.scope():
        qw, sbq, bterm, s_col, rs_col = emit_w2_quant(
            p, ct, w2_in, w2t_in, b2_in, gmax_sb, srow, PS)
        emit_qh_w2_resid(p, ct, h_in, x_in, dr["x4"], qw, sbq, bterm,
                         rs_col, PS)

    lg_c = load_vec_cols(p, "lgc", lg_in, D, ct, PS[6])
    lb_c = load_vec_cols(p, "lbc", lb_in, D, ct, PS[6])
    with p.scope():
        sb = ln_scratch(p)
        emit_ln(p, ct, dr["x4"], dr["junk"], sb, PS, dst_f32_dram=dr["t1"])
    with p.scope():
        sb = ln_scratch(p)
        sb["utt"] = p.sbuf("utt", [128, 128], BF16)
        emit_ln(p, ct, dr["t1"], dr["u"], sb, PS, gb=(lg_c, lb_c),
                ut_dram=dr["ut"], psb=psb16)

    Pre_bf = [p.sbuf(f"Preb{t}", [128, 512], BF16) for t in range(NNT)]
    Pim_bf = [p.sbuf(f"Pimb{t}", [128, 512], BF16) for t in range(NNT)]
    nPre_bf = [p.sbuf(f"nPreb{t}", [128, 512], BF16) for t in range(NNT)]
    nPim_bf = [p.sbuf(f"nPimb{t}", [128, 512], BF16) for t in range(NNT)]
    nlam_c = p.sbuf("nlam_c", [128, 2 * NNT], F32)
    plam_c = p.sbuf("plam_c", [128, 2 * NNT], F32)
    Cer = [p.sbuf(f"Cer{t}", [128, D], BF16) for t in range(NNT)]
    Cei = [p.sbuf(f"Cei{t}", [128, D], BF16) for t in range(NNT)]
    Kb = [p.sbuf(f"Kb{j}", [128, GCH], F32) for j in range(NDT)]
    c0_c = p.sbuf("c0c", [128, NDT], F32)
    WinR = p.sbuf("WinR", [128, 512], BF16)
    WinI = p.sbuf("WinI", [128, 512], BF16)
    pd_c = load_vec_cols(p, "pdc", pd_in, D, ct, PS[6])
    c3s = contextlib.ExitStack()
    p.scopes.append(c3s)
    lam = p.sbuf("lam", [1, 2048], F32)
    lr_row = lam[0:1, 0:512]
    li2p_row = lam[0:1, 512:1024]
    tmp2 = p.sbuf("tmp2", [1, 4096], F32)
    p.dma("sync", lam[0:1, 1024:1536], lr_in.rearrange("(a b) -> a b", a=1))
    p.dma("sync", lam[0:1, 1536:2048], li_in.rearrange("(a b) -> a b", a=1))
    p.wait("scalar", "sync")
    p.run("scalar", lambda e: e.activation(out=tmp2[0:1, 0:512],
                                           in_=lam[0:1, 1024:1536],
                                           func=AF.Exp))
    p.run("scalar", lambda e: e.activation(out=tmp2[0:1, 512:1024],
                                           in_=lam[0:1, 1536:2048],
                                           func=AF.Exp))
    p.wait("vector", "scalar")
    p.run("vector", lambda e: e.tensor_scalar(
        out=lr_row, in0=tmp2[0:1, 0:512], scalar1=-1.0, scalar2=None,
        op0=ALU.mult))
    p.run("vector", lambda e: e.tensor_scalar(
        out=li2p_row, in0=tmp2[0:1, 512:1024], scalar1=float(1.0 / TWOPI),
        scalar2=None, op0=ALU.mult))
    lr_c = p.sbuf("lr_c", [128, NNT], F32)
    li2p_c = p.sbuf("li2p_c", [128, NNT], F32)
    rows_to_cols(p, ct, lr_row, lr_c, NNT, PS[6], bounce)
    rows_to_cols(p, ct, li2p_row, li2p_c, NNT, PS[6], bounce)
    p.wait("vector", "sync")
    p.wait("scalar", "sync")
    p.settle()
    if dbg_out is not None:
        p.wait("gpsimd", "vector", "scalar", "tensor", "sync")
        p.dma("gpsimd", dbg_out[0:1, 156:668], lr_row)
        p.dma("gpsimd", dbg_out[0:128, 136:140], lr_c[:, :])
        p.dma("gpsimd", dbg_out[0:1, 668:1024], lam[0:1, 1024:1380])
        p.wait("gpsimd", "gpsimd")

    ramp_rep = p.sbuf("ramp_rep", [128, 512], F32)
    p.wait("tensor", "sync", "vector")
    p.run("tensor", lambda e: e.matmul(
        out=PS[5][:, 0:512], lhsT=ct["ones_row"][:, :], rhs=ct["ramp"][0:1, :],
        start=True, stop=True))
    p.wait("vector", "tensor")
    p.run("vector", lambda e: e.tensor_copy(out=ramp_rep[:, :],
                                            in_=PS[5][:, 0:512]))

    with p.scope():
        Pre = [p.sbuf(f"Pre{t}", [128, 512], F32) for t in range(NNT)]
        Pim = [p.sbuf(f"Pim{t}", [128, 512], F32) for t in range(NNT)]
        mag = p.sbuf("mag", [128, 512], F32)
        mcy = p.sbuf("mcy", [128, 512], F32)
        tfr = p.sbuf("tfr", [128, 512], F32)
        sn = p.sbuf("sn", [128, 512], F32)
        cn = p.sbuf("cn", [128, 512], F32)
        for t in range(NNT):
            p.run("vector", lambda e, t=t: e.tensor_scalar(
                out=mag[:, :], in0=ramp_rep[:, :], scalar1=lr_c[:, t:t + 1],
                scalar2=None, op0=ALU.mult))
            p.wait("scalar", "vector")
            p.run("scalar", lambda e: e.activation(out=mag[:, :],
                                                   in_=mag[:, :], func=AF.Exp))
            p.run("vector", lambda e, t=t: e.tensor_scalar(
                out=mcy[:, :], in0=ramp_rep[:, :], scalar1=li2p_c[:, t:t + 1],
                scalar2=None, op0=ALU.mult))
            p.run("vector", lambda e: e.tensor_scalar(
                out=tfr[:, :], in0=mcy[:, :], scalar1=MAGIC, scalar2=MAGIC,
                op0=ALU.add, op1=ALU.subtract))
            p.run("vector", lambda e: e.tensor_tensor(
                out=tfr[:, :], in0=mcy[:, :], in1=tfr[:, :], op=ALU.subtract))
            p.run("vector", lambda e: e.tensor_scalar(
                out=tfr[:, :], in0=tfr[:, :], scalar1=TWOPI, scalar2=None,
                op0=ALU.mult))
            p.wait("scalar", "vector")
            p.run("scalar", lambda e: e.activation(out=sn[:, :], in_=tfr[:, :],
                                                   func=AF.Sin))
            p.wait("vector", "scalar")
            p.run("vector", lambda e: e.tensor_scalar(
                out=mcy[:, :], in0=mcy[:, :], scalar1=0.25, scalar2=None,
                op0=ALU.add))
            p.run("vector", lambda e: e.tensor_scalar(
                out=tfr[:, :], in0=mcy[:, :], scalar1=MAGIC, scalar2=MAGIC,
                op0=ALU.add, op1=ALU.subtract))
            p.run("vector", lambda e: e.tensor_tensor(
                out=tfr[:, :], in0=mcy[:, :], in1=tfr[:, :], op=ALU.subtract))
            p.run("vector", lambda e: e.tensor_scalar(
                out=tfr[:, :], in0=tfr[:, :], scalar1=TWOPI, scalar2=None,
                op0=ALU.mult))
            p.wait("scalar", "vector")
            p.run("scalar", lambda e: e.activation(out=cn[:, :], in_=tfr[:, :],
                                                   func=AF.Sin))
            p.wait("vector", "scalar")
            p.run("vector", lambda e, t=t: e.tensor_tensor(
                out=Pre[t][:, :], in0=mag[:, :], in1=cn[:, :], op=ALU.mult))
            p.run("vector", lambda e, t=t: e.tensor_tensor(
                out=Pim[t][:, :], in0=mag[:, :], in1=sn[:, :], op=ALU.mult))
            p.run("vector", lambda e, t=t: e.tensor_copy(
                out=Pre_bf[t][:, :], in_=Pre[t][:, :]))
            p.run("vector", lambda e, t=t: e.tensor_copy(
                out=Pim_bf[t][:, :], in_=Pim[t][:, :]))
            p.run("vector", lambda e, t=t: e.tensor_scalar(
                out=nPre_bf[t][:, :], in0=Pre[t][:, :], scalar1=-1.0,
                scalar2=None, op0=ALU.mult))
            p.run("vector", lambda e, t=t: e.tensor_scalar(
                out=nPim_bf[t][:, :], in0=Pim[t][:, :], scalar1=-1.0,
                scalar2=None, op0=ALU.mult))
            p.runw("vector", lambda e, t=t: e.tensor_scalar(
                out=nlam_c[:, 2 * t:2 * t + 1], in0=Pre[t][:, 128:129],
                scalar1=-1.0, scalar2=None, op0=ALU.mult))
            p.runw("vector", lambda e, t=t: e.tensor_scalar(
                out=nlam_c[:, 2 * t + 1:2 * t + 2], in0=Pim[t][:, 128:129],
                scalar1=-1.0, scalar2=None, op0=ALU.mult))
            p.runw("vector", lambda e, t=t: e.tensor_copy(
                out=plam_c[:, 2 * t:2 * t + 1], in_=Pre[t][:, 128:129]))
            p.runw("vector", lambda e, t=t: e.tensor_copy(
                out=plam_c[:, 2 * t + 1:2 * t + 2], in_=Pim[t][:, 128:129]))
        p.settle()

    wr_c = p.sbuf("wr_c", [128, NNT], F32)
    wi_c = p.sbuf("wi_c", [128, NNT], F32)
    nwi_c = p.sbuf("nwi_c", [128, NNT], F32)
    with p.scope():
        rw = p.sbuf("rw", [1, 4608], F32)
        elr = rw[0:1, 0:512]
        cosi = rw[0:1, 512:1024]
        sini = rw[0:1, 1024:1536]
        er = rw[0:1, 1536:2048]
        ei = rw[0:1, 2048:2560]
        den = rw[0:1, 2560:3072]
        wrr = rw[0:1, 3072:3584]
        wir = rw[0:1, 3584:4096]
        p.wait("scalar", "vector")
        p.run("scalar", lambda e: e.activation(out=elr, in_=lr_row,
                                               func=AF.Exp))
        p.wait("vector", "scalar")
        row_frac_sincos(p, li2p_row, tmp2, sini, cosi)
        p.run("vector", lambda e: e.tensor_tensor(
            out=er, in0=elr, in1=cosi, op=ALU.mult))
        p.run("vector", lambda e: e.tensor_scalar(
            out=er, in0=er, scalar1=-1.0, scalar2=None, op0=ALU.add))
        p.run("vector", lambda e: e.tensor_tensor(
            out=ei, in0=elr, in1=sini, op=ALU.mult))
        p.run("vector", lambda e: e.tensor_scalar(
            out=den, in0=li2p_row, scalar1=TWOPI, scalar2=None, op0=ALU.mult))
        p.run("vector", lambda e: e.tensor_tensor(
            out=wrr, in0=lr_row, in1=lr_row, op=ALU.mult))
        p.run("vector", lambda e: e.scalar_tensor_tensor(
            out=den, in0=den, scalar=1.0, in1=den, op0=ALU.mult,
            op1=ALU.mult))
        p.run("vector", lambda e: e.tensor_tensor(
            out=den, in0=wrr, in1=den, op=ALU.add))
        p.run("vector", lambda e: e.reciprocal(out=den, in_=den))
        li_full = rw[0:1, 4096:4608]
        p.run("vector", lambda e: e.tensor_scalar(
            out=li_full, in0=li2p_row, scalar1=TWOPI, scalar2=None,
            op0=ALU.mult))
        p.run("vector", lambda e: e.tensor_tensor(
            out=wrr, in0=er, in1=lr_row, op=ALU.mult))
        p.run("vector", lambda e: e.tensor_tensor(
            out=tmp2[0:1, 0:512], in0=ei, in1=li_full, op=ALU.mult))
        p.run("vector", lambda e: e.tensor_tensor(
            out=wrr, in0=wrr, in1=tmp2[0:1, 0:512], op=ALU.add))
        p.run("vector", lambda e: e.tensor_tensor(
            out=wrr, in0=wrr, in1=den, op=ALU.mult))
        p.run("vector", lambda e: e.tensor_tensor(
            out=wir, in0=ei, in1=lr_row, op=ALU.mult))
        p.run("vector", lambda e: e.tensor_tensor(
            out=tmp2[0:1, 0:512], in0=er, in1=li_full, op=ALU.mult))
        p.run("vector", lambda e: e.tensor_tensor(
            out=wir, in0=wir, in1=tmp2[0:1, 0:512], op=ALU.subtract))
        p.run("vector", lambda e: e.tensor_tensor(
            out=wir, in0=wir, in1=den, op=ALU.mult))
        rows_to_cols(p, ct, wrr, wr_c, NNT, PS[6], bounce)
        rows_to_cols(p, ct, wir, wi_c, NNT, PS[6], bounce)
        p.wait("vector", "sync")
        p.runw("vector", lambda e: e.tensor_scalar(
            out=nwi_c[:, :], in0=wi_c[:, :], scalar1=-1.0, scalar2=None,
            op0=ALU.mult))
        p.settle()

    with p.scope():
        crs = p.sbuf("crs", [128, 512], F32)
        cis = p.sbuf("cis", [128, 512], F32)
        v1 = p.sbuf("v1C", [128, 128], F32)
        for dt_ in range(NDT):
            p.wait("sync", "tensor", "vector")
            p.dma("sync", crs[:, :], cr_in[128 * dt_:128 * (dt_ + 1), :])
            p.dma("sync", cis[:, :], ci_in[128 * dt_:128 * (dt_ + 1), :])
            for t in range(NNT):
                p.wait("tensor", "sync", "vector")
                p.run("tensor", lambda e, t=t: e.transpose(
                    out=PS[0][0:128, 0:128], in_=crs[:, 128 * t:128 * (t + 1)],
                    identity=ct["eye_f32"][:, :]))
                p.run("tensor", lambda e, t=t: e.transpose(
                    out=PS[1][0:128, 0:128], in_=cis[:, 128 * t:128 * (t + 1)],
                    identity=ct["eye_f32"][:, :]))
                p.wait("vector", "tensor")
                dsl = slice(128 * dt_, 128 * (dt_ + 1))
                p.run("vector", lambda e, t=t: e.tensor_scalar(
                    out=v1[:, :], in0=PS[0][0:128, 0:128],
                    scalar1=wr_c[:, t:t + 1], scalar2=None, op0=ALU.mult))
                p.run("vector", lambda e, t=t, dsl=dsl: e.scalar_tensor_tensor(
                    out=Cer[t][:, dsl], in0=PS[1][0:128, 0:128],
                    scalar=nwi_c[:, t:t + 1], in1=v1[:, :], op0=ALU.mult,
                    op1=ALU.add))
                p.run("vector", lambda e, t=t: e.tensor_scalar(
                    out=v1[:, :], in0=PS[0][0:128, 0:128],
                    scalar1=wi_c[:, t:t + 1], scalar2=None, op0=ALU.mult))
                p.run("vector", lambda e, t=t, dsl=dsl: e.scalar_tensor_tensor(
                    out=Cei[t][:, dsl], in0=PS[1][0:128, 0:128],
                    scalar=wr_c[:, t:t + 1], in1=v1[:, :], op0=ALU.mult,
                    op1=ALU.add))

    for j in range(NDT):
        p.wait("tensor", "vector")
        for t in range(NNT):
            p.run("tensor", lambda e, t=t, j=j, st=(t == 0):
                  e.matmul(out=PS[2][:, 0:GCH],
                           lhsT=Cer[t][:, 128 * j:128 * (j + 1)],
                           rhs=Pre_bf[t][:, 0:GCH], start=st, stop=False))
            p.run("tensor", lambda e, t=t, j=j, sp=(t == NNT - 1):
                  e.matmul(out=PS[2][:, 0:GCH],
                           lhsT=Cei[t][:, 128 * j:128 * (j + 1)],
                           rhs=nPim_bf[t][:, 0:GCH], start=False, stop=sp))
        p.wait("vector", "tensor")
        p.run("vector", lambda e, j=j: e.tensor_copy(out=Kb[j][:, :],
                                                     in_=PS[2][:, 0:GCH]))
        p.runw("vector", lambda e, j=j: e.tensor_tensor(
            out=c0_c[:, j:j + 1], in0=Kb[j][:, 0:1], in1=pd_c[:, j:j + 1],
            op=ALU.add))
    p.settle()

    with p.scope():
        mt = p.sbuf("mtW", [128, 512], F32)
        ft = p.sbuf("ftW", [128, 512], F32)
        st_ = p.sbuf("stW", [128, 512], F32)
        p.wait("tensor", "vector", "sync")
        p.run("tensor", lambda e: e.matmul(
            out=PS[3][:, 0:512], lhsT=ct["revramp"][:, :], rhs=lr_row,
            start=True, stop=True))
        p.wait("scalar", "tensor")
        p.run("scalar", lambda e: e.activation(out=mt[:, :],
                                               in_=PS[3][:, 0:512],
                                               func=AF.Exp))
        p.wait("tensor", "scalar")
        p.run("tensor", lambda e: e.matmul(
            out=PS[3][:, 0:512], lhsT=ct["revramp"][:, :], rhs=li2p_row,
            start=True, stop=True))
        p.wait("vector", "tensor", "scalar")
        p.run("vector", lambda e: e.tensor_scalar(
            out=ft[:, :], in0=PS[3][:, 0:512], scalar1=MAGIC, scalar2=MAGIC,
            op0=ALU.add, op1=ALU.subtract))
        p.run("vector", lambda e: e.tensor_tensor(
            out=ft[:, :], in0=PS[3][:, 0:512], in1=ft[:, :], op=ALU.subtract))
        p.run("vector", lambda e: e.tensor_scalar(
            out=ft[:, :], in0=ft[:, :], scalar1=TWOPI, scalar2=None,
            op0=ALU.mult))
        p.wait("scalar", "vector")
        p.run("scalar", lambda e: e.activation(out=st_[:, :], in_=ft[:, :],
                                               func=AF.Sin))
        p.wait("vector", "scalar")
        p.run("vector", lambda e: e.tensor_tensor(
            out=WinI[:, :], in0=mt[:, :], in1=st_[:, :], op=ALU.mult))
        p.run("vector", lambda e: e.tensor_scalar(
            out=ft[:, :], in0=PS[3][:, 0:512], scalar1=0.25, scalar2=None,
            op0=ALU.add))
        p.run("vector", lambda e: e.tensor_scalar(
            out=st_[:, :], in0=ft[:, :], scalar1=MAGIC, scalar2=MAGIC,
            op0=ALU.add, op1=ALU.subtract))
        p.run("vector", lambda e: e.tensor_tensor(
            out=ft[:, :], in0=ft[:, :], in1=st_[:, :], op=ALU.subtract))
        p.run("vector", lambda e: e.tensor_scalar(
            out=ft[:, :], in0=ft[:, :], scalar1=TWOPI, scalar2=None,
            op0=ALU.mult))
        p.wait("scalar", "vector")
        p.run("scalar", lambda e: e.activation(out=st_[:, :], in_=ft[:, :],
                                               func=AF.Sin))
        p.wait("vector", "scalar")
        p.run("vector", lambda e: e.tensor_tensor(
            out=WinR[:, :], in0=mt[:, :], in1=st_[:, :], op=ALU.mult))

    p.barrier()

    if dbg_out is not None:
        p.wait("gpsimd", "vector", "scalar", "tensor", "sync")
        p.dma("gpsimd", dbg_out[0:128, 140:144], wr_c[:, :])
        p.dma("gpsimd", dbg_out[0:128, 144:148], wi_c[:, :])
        p.dma("gpsimd", dbg_out[0:128, 284:412], Kb[0][:, :])
        p.wait("gpsimd", "gpsimd")

    p.barrier()
    p.scopes.pop()
    c3s.close()

    with p.scope():
        utc = p.sbuf("utc", [128, D], BF16)
        Psb = [[p.sbuf(f"Psb{t}{ri}", [128, D], F32) for ri in range(2)]
               for t in range(NNT)]
        Sa = [[p.sbuf(f"Sa{t}{ri}", [128, D], F32) for ri in range(2)]
              for t in range(NNT)]
        Sb = [[p.sbuf(f"Sb{t}{ri}", [128, D], F32) for ri in range(2)]
              for t in range(NNT)]
        Mb = [[p.sbuf(f"Mb{t}{c4}", [128, D], BF16) for c4 in range(4)]
              for t in range(NNT)]
        ytb = p.sbuf("ytb", [128, D], BF16)
        yev = p.sbuf("yev", [128, 128], BF16)
        ztile = p.sbuf("ztile", [128, 128], BF16)
        p.run("vector", lambda e: e.memset(ztile[:, :], 0.0))
        for t in range(NNT):
            for ri in range(2):
                p.run("vector", lambda e, t=t, ri=ri:
                      e.memset(Sa[t][ri][:, :], 0.0))
        comp_w = None
        for c in range(NCH):
            S_old = Sa if c % 2 == 0 else Sb
            S_new = Sb if c % 2 == 0 else Sa
            p.barrier()
            p.dma("sync", utc[:, :], dr["ut"][GCH * c:GCH * (c + 1), :])
            for t in range(NNT):
                for ri, W in ((0, WinR), (1, WinI)):
                    p.wait("tensor", "sync", "vector")
                    for hf in range(2):
                        p.run("tensor", lambda e, t=t, W=W, hf=hf:
                              e.matmul(out=PS[4 + hf][:, 0:512],
                                       lhsT=W[:, 128 * t:128 * (t + 1)],
                                       rhs=utc[:, 512 * hf:512 * (hf + 1)],
                                       start=True, stop=True))
                    p.wait("vector", "tensor")
                    for hf in range(2):
                        p.run("vector", lambda e, t=t, ri=ri, hf=hf:
                              e.tensor_copy(
                                  out=Psb[t][ri][:, 512 * hf:512 * (hf + 1)],
                                  in_=PS[4 + hf][:, 0:512]))
            if c > 0:
                for t in range(NNT):
                    p.run("vector", lambda e, t=t, S=S_old: e.tensor_tensor(
                        out=Mb[t][0][:, :], in0=S[t][0][:, :],
                        in1=Cer[t][:, :], op=ALU.mult))
                    p.run("vector", lambda e, t=t, S=S_old: e.tensor_tensor(
                        out=Mb[t][1][:, :], in0=S[t][1][:, :],
                        in1=Cei[t][:, :], op=ALU.mult))
                    p.run("vector", lambda e, t=t, S=S_old: e.tensor_tensor(
                        out=Mb[t][2][:, :], in0=S[t][1][:, :],
                        in1=Cer[t][:, :], op=ALU.mult))
                    p.run("vector", lambda e, t=t, S=S_old: e.tensor_tensor(
                        out=Mb[t][3][:, :], in0=S[t][0][:, :],
                        in1=Cei[t][:, :], op=ALU.mult))
                p.wait("tensor", "vector")
                for t in range(NNT):
                    for c4, Wt in ((0, Pre_bf[t]), (1, nPre_bf[t]),
                                   (2, nPim_bf[t]), (3, nPim_bf[t])):
                        for hf in range(2):
                            p.run("tensor", lambda e, t=t, c4=c4, Wt=Wt,
                                  hf=hf, st=(t == 0 and c4 == 0),
                                  sp=(t == NNT - 1 and c4 == 3):
                                  e.matmul(
                                      out=PS[2 + hf][:, 0:512],
                                      lhsT=Wt[:, 1:129],
                                      rhs=Mb[t][c4][:, 512 * hf:512 * (hf + 1)],
                                      start=st, stop=sp))
                p.wait("vector", "tensor")
                for hf in range(2):
                    p.run("vector", lambda e, hf=hf: e.tensor_copy(
                        out=ytb[:, 512 * hf:512 * (hf + 1)],
                        in_=PS[2 + hf][:, 0:512]))
                for dt_ in range(NDT):
                    b = dt_ % 2
                    p.wait("tensor", "vector", "gpsimd")
                    p.run("tensor", lambda e, dt_=dt_, b=b: e.transpose(
                        out=psb16[:, 128 * b:128 * b + 128],
                        in_=ytb[:, 128 * dt_:128 * (dt_ + 1)],
                        identity=ct["eye_bf"][:, :]))
                    p.wait("vector", "tensor")
                    p.run("vector", lambda e, b=b: e.tensor_copy(
                        out=yev[:, :], in_=psb16[:, 128 * b:128 * b + 128]))
                    p.wait("gpsimd", "vector")
                    p.dma("gpsimd",
                          dr["ys"][128 * dt_:128 * (dt_ + 1),
                                   GCH * c:GCH * (c + 1)], yev[:, :])
            else:
                p.wait("gpsimd", "vector")
                for dt_ in range(NDT):
                    p.dma("gpsimd",
                          dr["ys"][128 * dt_:128 * (dt_ + 1), 0:GCH],
                          ztile[:, :])
            for t in range(NNT):
                p.run("vector", lambda e, t=t, S=S_old, N=S_new:
                      e.scalar_tensor_tensor(
                          out=N[t][0][:, :], in0=S[t][0][:, :],
                          scalar=plam_c[:, 2 * t:2 * t + 1],
                          in1=Psb[t][0][:, :],
                          op0=ALU.mult, op1=ALU.add))
                p.run("vector", lambda e, t=t, S=S_old, N=S_new:
                      e.scalar_tensor_tensor(
                          out=N[t][0][:, :], in0=S[t][1][:, :],
                          scalar=nlam_c[:, 2 * t + 1:2 * t + 2],
                          in1=N[t][0][:, :], op0=ALU.mult, op1=ALU.add))
                p.run("vector", lambda e, t=t, S=S_old, N=S_new:
                      e.scalar_tensor_tensor(
                          out=N[t][1][:, :], in0=S[t][1][:, :],
                          scalar=plam_c[:, 2 * t:2 * t + 1],
                          in1=Psb[t][1][:, :],
                          op0=ALU.mult, op1=ALU.add))
                p.run("vector", lambda e, t=t, S=S_old, N=S_new:
                      e.scalar_tensor_tensor(
                          out=N[t][1][:, :], in0=S[t][0][:, :],
                          scalar=plam_c[:, 2 * t + 1:2 * t + 2],
                          in1=N[t][1][:, :],
                          op0=ALU.mult, op1=ALU.add))

    if dbg_out is not None:
        p.wait("gpsimd", "vector", "scalar", "tensor", "sync")
        p.dma("gpsimd", dbg_out[0:128, 0:128], Kb[0][:, :])
        p.dma("gpsimd", dbg_out[0:128, 128:136], c0_c[:, :])
        p.dma("gpsimd", dbg_out[0:128, 148:156], nlam_c[:, :])
        p.wait("vector", "gpsimd")
        p.runw("vector", lambda e: e.memset(srow[0:1, 8:9], 42.0))
        p.wait("gpsimd", "vector")
        p.dma("gpsimd", dbg_out[0:1, 5:6], srow[0:1, 8:9])

    with p.scope():
        accv = p.sbuf("accv", [128, T], F32)
        accg = p.sbuf("accg", [128, T], F32)
        uk = p.sbuf("ukC", [128, T], BF16)
        x4k = p.sbuf("x4k", [128, T], F32)
        ysk = p.sbuf("ysk", [128, T], BF16)
        for k in range(NDT):
            p.barrier()
            p.dma("sync", uk[:, :], dr["u"][128 * k:128 * (k + 1), :])
            p.dma("sync", x4k[:, :], dr["x4"][128 * k:128 * (k + 1), :])
            p.dma("sync", ysk[:, :], dr["ys"][128 * k:128 * (k + 1), :])
            p.wait("vector", "sync")
            p.wait("gpsimd", "sync")
            p.run("vector", lambda e, k=k: e.scalar_tensor_tensor(
                out=accv[:, :], in0=uk[:, :], scalar=c0_c[:, k:k + 1],
                in1=x4k[:, :], op0=ALU.mult, op1=ALU.add))
            p.run("vector", lambda e: e.tensor_tensor(
                out=accv[:, :], in0=accv[:, :], in1=ysk[:, :], op=ALU.add))

            useg = uk[:, :].rearrange("p (c q) -> p c q", q=GCH)
            av = accv[:, :].rearrange("p (c q) -> p c q", q=GCH)
            ag = accg[:, :].rearrange("p (c q) -> p c q", q=GCH)
            for l in range(1, GCH):
                w_ = GCH - l
                eng = "vector"
                acc_ = av if eng == "vector" else ag
                p.run(eng, lambda e, l=l, w_=w_, acc_=acc_, k=k:
                      e.scalar_tensor_tensor(
                          out=acc_[:, :, l:GCH], in0=useg[:, :, 0:w_],
                          scalar=Kb[k][:, l:l + 1], in1=acc_[:, :, l:GCH],
                          op0=ALU.mult, op1=ALU.add))
            p.wait("gpsimd", "vector")
            p.dma("gpsimd", x5_out[128 * k:128 * (k + 1), :], accv[:, :])

    with p.scope():
        sb = ln_scratch(p)
        emit_ln(p, ct, x5_out, dr["junk"], sb, PS, dst_f32_dram=xln2_out,
                rmax=rmax)
        p.wait("tensor", "vector", "sync")
        p.run("tensor", lambda e: e.transpose(
            out=PS[0][0:1, 0:128], in_=rmax[:, 0:1],
            identity=ct["eye_f32"][:, :]))
        p.wait("vector", "tensor")
        p.runw("vector", lambda e: e.reduce_max(
            out=srow[0:1, 4:5], in_=PS[0][0:1, 0:128], axis=AX.X))
        p.wait("gpsimd", "vector")
        p.dma("gpsimd", mx_out[0:1, 0:1], srow[0:1, 4:5])
    p.finalize()
    return nc

_CACHE = {}


def _inputs_A(inputs, b):
    x = inputs["L0_attn_hidden_states"]
    m = {"x_in": np.ascontiguousarray(x[b].T)}
    m.update(const_inputs())
    m["gate_w"] = np.ascontiguousarray(inputs["L0_ffn_gate_w"])
    m["Wu"] = np.ascontiguousarray(inputs["L0_ffn_Wu"])
    m["Wv"] = np.ascontiguousarray(inputs["L0_ffn_Wv"])
    m["Wo"] = np.ascontiguousarray(inputs["L0_ffn_Wo"])
    for nm in ["W_xz", "W_dt", "b_dt", "A_log", "W_B", "W_C", "D_skip",
               "W_out", "v_th"]:
        m[nm] = np.ascontiguousarray(inputs["L0_attn_" + nm])
    return m


def kernel(**inputs):
    n = 8
    cores = list(range(n))
    if "A" not in _CACHE:
        _CACHE["A"] = build_launch_A()
        _CACHE["B"] = build_launch_B()
        _CACHE["C"] = build_launch_C()
        _CACHE["E"] = build_launch_E()
    ci = const_inputs()
    w1 = np.ascontiguousarray(inputs["L1_ffn_w1"])
    w1t = np.ascontiguousarray(w1.T)
    b1 = np.ascontiguousarray(inputs["L1_ffn_b1"])
    w2 = np.ascontiguousarray(inputs["L1_ffn_w2"])
    w2t = np.ascontiguousarray(w2.T)
    b2 = np.ascontiguousarray(inputs["L1_ffn_b2"])

    resA = run_bass_kernel_spmd(_CACHE["A"],
                                [_inputs_A(inputs, b) for b in range(n)],
                                core_ids=cores)
    m1 = np.stack([resA.results[b]["mx"][0] for b in range(n)])

    resB = run_bass_kernel_spmd(_CACHE["B"], [
        {"xln": resA.results[b]["xln1"], "w1": w1, "w1t": w1t, "b1": b1,
         "gmax": m1, **ci} for b in range(n)], core_ids=cores)
    m2 = np.stack([resB.results[b]["mx"][0] for b in range(n)])

    resC = run_bass_kernel_spmd(_CACHE["C"], [
        {"x": resA.results[b]["x3"], "h": resB.results[b]["h"],
         "w2": w2, "w2t": w2t, "b2": b2, "gmax": m2,
         "Lr": inputs["L1_attn_Lambda_real"],
         "Li": inputs["L1_attn_Lambda_imag"],
         "Cr": inputs["L1_attn_C_real"], "Ci": inputs["L1_attn_C_imag"],
         "pD": inputs["L1_attn_param_D"], "lg": inputs["L1_attn_ln_gamma"],
         "lb": inputs["L1_attn_ln_beta"], **ci} for b in range(n)],
        core_ids=cores)
    m3 = np.stack([resC.results[b]["mx"][0] for b in range(n)])

    resD = run_bass_kernel_spmd(_CACHE["B"], [
        {"xln": resC.results[b]["xln2"], "w1": w1, "w1t": w1t, "b1": b1,
         "gmax": m3, **ci} for b in range(n)], core_ids=cores)
    m4 = np.stack([resD.results[b]["mx"][0] for b in range(n)])

    resE = run_bass_kernel_spmd(_CACHE["E"], [
        {"x": resC.results[b]["x5"], "h": resD.results[b]["h"],
         "w2": w2, "w2t": w2t, "b2": b2, "gmax": m4, **ci}
        for b in range(n)], core_ids=cores)
    return np.stack([resE.results[b]["y"].T for b in range(n)])


# version 02
# speedup vs baseline: 1.3443x; 1.3443x over previous
import contextlib
import numpy as np
import ml_dtypes

import concourse.bass as bass
import concourse.mybir as mybir
from concourse.bass_utils import run_bass_kernel_spmd

F32 = mybir.dt.float32
BF16 = mybir.dt.bfloat16
AF = mybir.ActivationFunctionType
ALU = mybir.AluOpType
AX = mybir.AxisListType

MAGIC = float(np.float32(1.5 * 2.0 ** 23))
D = 1024
T = 4096
NDT = 8
NCC = T // 512
LN_EPS = 1e-5
K_STEEP = 10.0
V_TH_MIN = 0.1
QEPS = 1e-8

ENGS = ("sync", "tensor", "scalar", "vector", "gpsimd")


class SemC:
    __slots__ = ("h", "v")

    def __init__(self, h):
        self.h = h
        self.v = 0


class Prog:
    def __init__(self, nc):
        self.nc = nc
        self.es = contextlib.ExitStack()
        self.es.enter_context(nc.cleanup_on_exit())
        self.scopes = [self.es]
        self.ops = {e: [] for e in ENGS}
        self._fence = {"sync": 0, "gpsimd": 0}
        self.s = {e: SemC(self.es.enter_context(nc.semaphore(f"sem_{e}")))
                  for e in ENGS}
        self._setl = self.es.enter_context(
            nc.sbuf_tensor("settle_buf", [128, 512], F32))

    def sbuf(self, name, shape, dt=F32):
        self._uid = getattr(self, "_uid", 0) + 1
        return self.scopes[-1].enter_context(
            self.nc.sbuf_tensor(f"{name}_{self._uid}", shape, dt))

    def psum(self, name, shape, dt=F32):
        self._uid = getattr(self, "_uid", 0) + 1
        return self.scopes[-1].enter_context(
            self.nc.psum_tensor(f"{name}_{self._uid}", shape, dt))

    @contextlib.contextmanager
    def scope(self):
        st = contextlib.ExitStack()
        self.scopes.append(st)
        try:
            yield
        finally:
            self.barrier()
            self.scopes.pop()
            st.close()

    def on(self, eng, fn):
        self.ops[eng].append(fn)

    def wait(self, eng, *prods):
        for pe in prods:
            sc = self.s[pe]
            if sc.v > 0:
                self.on(eng, lambda e, h=sc.h, t=sc.v: e.wait_ge(h, t))
                if pe in ("sync", "gpsimd"):
                    self._fence[pe] = sc.v

    _fence = None

    def dma(self, eng, out, in_, slow=False):
        sc = self.s[eng]
        fv = self._fence.get(eng)
        if fv:
            self.on(eng, lambda e, h=sc.h, t=fv: e.wait_ge(h, t))
            self._fence[eng] = 0
        sc.v += 16
        if slow:
            self.on(eng, lambda e, o=out, i=in_, h=sc.h:
                    e.dma_start(out=o, in_=i,
                                allow_slow_non_contiguous=True).then_inc(h, 16))
        else:
            self.on(eng, lambda e, o=out, i=in_, h=sc.h:
                    e.dma_start(out=o, in_=i).then_inc(h, 16))

    def run(self, eng, fn):
        sc = self.s[eng]
        if eng in ("vector", "scalar", "gpsimd") and sc.v > 0:
            self.on(eng, lambda e, h=sc.h, t=sc.v: e.wait_ge(h, t))
        sc.v += 1
        self.on(eng, lambda e, f=fn, h=sc.h: f(e).then_inc(h, 1))

    def wait_at(self, eng, pe, val):
        if val > 0:
            sc = self.s[pe]
            self.on(eng, lambda e, h=sc.h, t=val: e.wait_ge(h, t))

    def runf(self, eng, fn):
        sc = self.s[eng]
        sc.v += 1
        self.on(eng, lambda e, f=fn, h=sc.h: f(e).then_inc(h, 1))

    def runw(self, eng, fn):
        sc = self.s[eng]
        if sc.v > 0:
            self.on(eng, lambda e, h=sc.h, t=sc.v: e.wait_ge(h, t))
        sc.v += 1
        self.on(eng, lambda e, f=fn, h=sc.h: f(e).then_inc(h, 1))

    def settle(self):
        sc = self.s["vector"]
        if sc.v > 0:
            self.on("vector", lambda e, h=sc.h, t=sc.v: e.wait_ge(h, t))
        sc.v += 1
        self.on("vector", lambda e, h=sc.h, t=self._setl:
                e.memset(t[:, :], 0.0).then_inc(h, 1))

    def sbuf_root(self, name, shape, dt=F32):
        self._uid = getattr(self, "_uid", 0) + 1
        return self.es.enter_context(
            self.nc.sbuf_tensor(f"{name}_{self._uid}", shape, dt))

    def barrier(self):
        for eng in ENGS:
            self.wait(eng, *[o for o in ENGS if o != eng])

    def finalize(self, final_eng="gpsimd"):
        self.wait(final_eng, *[o for o in ENGS if o != final_eng])
        sc = self.s[final_eng]
        if sc.v > 0:
            self.on(final_eng, lambda e, h=sc.h, t=sc.v: e.wait_ge(h, t))
        with self.nc.Block() as blk:
            for name in ENGS:
                fns = self.ops[name]
                if not fns:
                    continue

                def runner(e, fns=fns):
                    for f in fns:
                        f(e)
                getattr(blk, name)(runner)
        self.es.close()


CONST = {}


def _consts():
    if not CONST:
        CONST["ones_col_bf"] = np.ones((128, 1), ml_dtypes.bfloat16)
        CONST["ones_row"] = np.ones((1, 128), np.float32)
        CONST["eye_bf"] = np.eye(128, dtype=ml_dtypes.bfloat16)
        CONST["eye_f32"] = np.eye(128, dtype=np.float32)
        CONST["ramp"] = np.arange(512, dtype=np.float32)[None, :]
        CONST["revramp"] = np.arange(127, -1, -1, dtype=np.float32)[None, :]
        oh16 = np.zeros((16, 16 * 128), ml_dtypes.bfloat16)
        for n in range(16):
            oh16[n, 128 * n:128 * (n + 1)] = 1.0
        CONST["oh16"] = oh16
        oh8 = np.zeros((8, 8 * 128), ml_dtypes.bfloat16)
        for n in range(8):
            oh8[n, 128 * n:128 * (n + 1)] = 1.0
        CONST["oh8"] = oh8
        oh32f = np.zeros((32, 32 * 128), np.float32)
        for n in range(32):
            oh32f[n, 128 * n:128 * (n + 1)] = 1.0
        CONST["oh32f"] = oh32f
        CONST["oh8f"] = oh8.astype(np.float32)
    return CONST


def decl_consts(nc):
    out = {}
    for k, v in _consts().items():
        dt = BF16 if v.dtype == ml_dtypes.bfloat16 else F32
        out[k] = nc.dram_tensor("c_" + k, list(v.shape), dt,
                                kind="ExternalInput").ap()
    return out


def const_inputs():
    return {"c_" + k: np.ascontiguousarray(v) for k, v in _consts().items()}


CT_SHAPES = {"ones_col_bf": ([128, 1], BF16), "ones_row": ([1, 128], F32),
             "eye_bf": ([128, 128], BF16), "eye_f32": ([128, 128], F32),
             "ramp": ([1, 512], F32), "revramp": ([1, 128], F32),
             "oh16": ([16, 2048], BF16), "oh8": ([8, 1024], BF16),
             "oh32f": ([32, 4096], F32), "oh8f": ([8, 1024], F32)}


def load_const_tiles(p, cd, names=("ones_col_bf", "ones_row", "eye_bf",
                                   "eye_f32", "ramp", "revramp", "oh16",
                                   "oh8")):
    t = {}
    for k in names:
        sh, dt = CT_SHAPES[k]
        t[k] = p.sbuf("ct_" + k, sh, dt)
        p.dma("sync", t[k][:, :], cd[k][:, :])
    return t


def load_w_bf(p, name, dram, kdim, mdim, stage):
    nk = -(-kdim // 128)
    tiles = []
    for i in range(nk):
        rows = min(128, kdim - i * 128)
        bft = p.sbuf(f"{name}{i}", [128, mdim], BF16)
        p.wait("sync", "vector")
        p.dma("sync", stage[0:rows, 0:mdim], dram[i * 128:i * 128 + rows, :])
        p.wait("vector", "sync")
        p.run("vector", lambda e, a=bft, b=stage, r=rows, m=mdim:
              e.tensor_copy(out=a[0:r, 0:m], in_=b[0:r, 0:m]))
        tiles.append(bft)
    return tiles


def load_vec_cols(p, name, dram, n, ct=None, ps=None):
    nj = n // 128
    t = p.sbuf(name, [128, nj], F32)
    tr = p.sbuf(name + "r", [nj, 128], F32)
    p.dma("sync", tr[0:nj, 0:128], dram.rearrange("(j q) -> j q", q=128))
    p.wait("tensor", "sync", "vector", "scalar")
    p.run("tensor", lambda e: e.transpose(
        out=ps[0:128, 0:nj], in_=tr[0:nj, 0:128],
        identity=ct["eye_f32"][0:nj, 0:nj]))
    p.wait("vector", "tensor")
    p.run("vector", lambda e: e.tensor_copy(out=t[:, :], in_=ps[0:128, 0:nj]))
    p.settle()
    return t


def emit_ln(p, ct, src_dram, dst_bf_dram, sb, PS, wfull_sb=None,
            gate_tiles=None, dst_f32_dram=None, rmax=None, gb=None,
            ut_dram=None, psb=None):
    xa, xb, xq, o32 = sb["fa"], sb["fb"], sb["fq"], sb["fo"]
    srow, tmax = sb["srow"], sb["tmax"]
    ps_stat, ps_arep, ps_brep, ps_g = PS[3], PS[4], PS[5], PS[6]

    for cc in range(NCC):
        csl = slice(cc * 512, (cc + 1) * 512)
        p.barrier()
        for k in range(NDT):
            p.dma("sync", xa[k][:, :], src_dram[128 * k:128 * (k + 1), csl])
        p.wait("vector", "sync")
        p.wait("scalar", "sync")
        for k in range(NDT):
            p.run("vector", lambda e, a=xb[k], b=xa[k]:
                  e.tensor_copy(out=a[:, :], in_=b[:, :]))
            p.run("scalar", lambda e, a=xq[k], b=xa[k]:
                  e.activation(out=a[:, :], in_=b[:, :], func=AF.Square))
        p.wait("tensor", "vector", "scalar")
        for k in range(NDT):
            p.run("tensor", lambda e, r=xb[k], w=ct["ones_col_bf"],
                  st=(k == 0), sp=(k == NDT - 1):
                  e.matmul(out=ps_stat[0:1, 0:512], lhsT=w[:, :], rhs=r[:, :],
                           start=st, stop=sp))
        for k in range(NDT):
            p.run("tensor", lambda e, r=xq[k], w=ct["ones_col_bf"],
                  st=(k == 0), sp=(k == NDT - 1):
                  e.matmul(out=ps_stat[32:33, 0:512], lhsT=w[:, :], rhs=r[:, :],
                           start=st, stop=sp))
        m_row = srow[0:1, 0:512]
        v_row = srow[0:1, 512:1024]
        a_row = srow[0:1, 1024:1536]
        b_row = srow[0:1, 1536:2048]
        p.wait("vector", "tensor")
        p.run("vector", lambda e: e.tensor_scalar(
            out=m_row, in0=ps_stat[0:1, 0:512], scalar1=1.0 / D, scalar2=None,
            op0=ALU.mult))
        p.run("vector", lambda e: e.tensor_tensor(
            out=v_row, in0=m_row, in1=m_row, op=ALU.mult))
        p.run("vector", lambda e: e.scalar_tensor_tensor(
            out=v_row, in0=ps_stat[32:33, 0:512], scalar=1.0 / D, in1=v_row,
            op0=ALU.mult, op1=ALU.subtract))
        p.run("vector", lambda e: e.tensor_scalar(
            out=v_row, in0=v_row, scalar1=float(LN_EPS), scalar2=None,
            op0=ALU.add))
        p.wait("scalar", "vector")
        p.run("scalar", lambda e: e.activation(
            out=v_row, in_=v_row, func=AF.Sqrt))
        p.wait("vector", "scalar")
        p.run("vector", lambda e: e.reciprocal(out=a_row, in_=v_row))
        p.run("vector", lambda e: e.scalar_tensor_tensor(
            out=b_row, in0=m_row, scalar=-1.0, in1=a_row,
            op0=ALU.mult, op1=ALU.mult))
        p.wait("tensor", "vector")
        p.run("tensor", lambda e: e.matmul(
            out=ps_arep[:, 0:512], lhsT=ct["ones_row"][:, :], rhs=a_row,
            start=True, stop=True))
        p.run("tensor", lambda e: e.matmul(
            out=ps_brep[:, 0:512], lhsT=ct["ones_row"][:, :], rhs=b_row,
            start=True, stop=True))
        p.wait("vector", "tensor", "gpsimd")
        for k in range(NDT):
            p.run("vector", lambda e, xk=xa[k], o=o32[k]:
                  e.tensor_tensor(out=o[:, :], in0=xk[:, :],
                                  in1=ps_arep[:, 0:512], op=ALU.mult))
            p.run("vector", lambda e, o=o32[k]:
                  e.tensor_tensor(out=o[:, :], in0=o[:, :],
                                  in1=ps_brep[:, 0:512], op=ALU.add))
        if gb is not None:
            g_c, b_c = gb
            for k in range(NDT):
                p.run("vector", lambda e, o=o32[k], k=k: e.tensor_scalar(
                    out=o[:, :], in0=o[:, :], scalar1=g_c[:, k:k + 1],
                    scalar2=b_c[:, k:k + 1], op0=ALU.mult, op1=ALU.add))
        p.wait("scalar", "vector")
        for k in range(NDT):
            p.run("scalar", lambda e, o=o32[k], b=xb[k]:
                  e.activation(out=b[:, :], in_=o[:, :], func=AF.Copy))
        if ut_dram is not None:
            for k in range(NDT):
                for s4 in range(4):
                    b = (k * 4 + s4) % 2
                    p.wait("tensor", "scalar", "vector", "gpsimd")
                    p.run("tensor", lambda e, k=k, s4=s4, b=b: e.transpose(
                        out=psb[:, 128 * b:128 * b + 128],
                        in_=xb[k][:, 128 * s4:128 * (s4 + 1)],
                        identity=ct["eye_bf"][:, :]))
                    p.wait("vector", "tensor")
                    p.run("vector", lambda e, b=b: e.tensor_copy(
                        out=sb["utt"][:, 0:128],
                        in_=psb[:, 128 * b:128 * b + 128]))
                    p.wait("gpsimd", "vector")
                    p.dma("gpsimd",
                          ut_dram[cc * 512 + 128 * s4:cc * 512 + 128 * s4 + 128,
                                  128 * k:128 * (k + 1)],
                          sb["utt"][:, 0:128])
        p.wait("gpsimd", "vector", "scalar")
        for k in range(NDT):
            p.dma("gpsimd", dst_bf_dram[128 * k:128 * (k + 1), csl], xb[k][:, :])
            if dst_f32_dram is not None:
                p.dma("gpsimd", dst_f32_dram[128 * k:128 * (k + 1), csl],
                      o32[k][:, :])
        if rmax is not None:
            for k in range(NDT):
                p.runw("vector", lambda e, o=o32[k]:
                       e.reduce_max(out=tmax[:, 0:1], in_=o[:, :], axis=AX.X))
                if cc == 0 and k == 0:
                    p.runw("vector", lambda e:
                           e.tensor_copy(out=rmax[:, 0:1], in_=tmax[:, 0:1]))
                else:
                    p.runw("vector", lambda e:
                           e.tensor_tensor(out=rmax[:, 0:1], in0=rmax[:, 0:1],
                                           in1=tmax[:, 0:1], op=ALU.max))
        if wfull_sb is not None:
            gl, gt, gm, gw, gs = sb["gl"], sb["gt"], sb["gm"], sb["gw"], sb["gs"]
            p.wait("tensor", "vector")
            for k in range(NDT):
                p.run("tensor", lambda e, w=gate_tiles[k], r=o32[k],
                      st=(k == 0), sp=(k == NDT - 1):
                      e.matmul(out=ps_g[0:8, 0:512], lhsT=w[:, 0:8],
                               rhs=r[:, :], start=st, stop=sp))
            p.wait("vector", "tensor")
            p.run("vector", lambda e: e.tensor_copy(out=gl[:, :],
                                                    in_=ps_g[0:8, 0:512]))
            p.wait("tensor", "vector")
            for j in range(4):
                p.run("tensor", lambda e, j=j: e.transpose(
                    out=ps_g[0:128, 8 * j:8 * j + 8],
                    in_=gl[0:8, 128 * j:128 * (j + 1)],
                    identity=ct["eye_f32"][0:8, 0:8]))
            p.wait("vector", "tensor")
            p.run("vector", lambda e: e.tensor_copy(out=gt[:, :],
                                                    in_=ps_g[0:128, 0:32]))
            for j in range(4):
                jsl = slice(8 * j, 8 * j + 8)
                p.runw("vector", lambda e, jsl=jsl: e.max(
                    out=gm[:, jsl], in_=gt[:, jsl]))
            for j in range(4):
                m1 = gm[:, 8 * j + 0:8 * j + 1]
                m2 = gm[:, 8 * j + 1:8 * j + 2]
                d12, w1, w2 = gs[:, 0:1], gs[:, 1:2], gs[:, 2:3]
                t8 = gs[:, 8:16]
                jsl = slice(8 * j, 8 * j + 8)
                p.runw("vector", lambda e, m1=m1, m2=m2, d12=d12:
                       e.tensor_tensor(out=d12, in0=m1, in1=m2,
                                       op=ALU.subtract))
                p.wait("scalar", "vector")
                p.runw("scalar", lambda e, d12=d12, w1=w1:
                       e.activation(out=w1, in_=d12, func=AF.Sigmoid))
                p.wait("vector", "scalar")
                p.runw("vector", lambda e, w1=w1, w2=w2: e.tensor_scalar(
                    out=w2, in0=w1, scalar1=-1.0, scalar2=1.0,
                    op0=ALU.mult, op1=ALU.add))
                p.runw("vector", lambda e, jsl=jsl, m1=m1, w1=w1:
                       e.tensor_scalar(out=gw[:, jsl], in0=gt[:, jsl],
                                       scalar1=m1, scalar2=w1,
                                       op0=ALU.is_equal, op1=ALU.mult))
                p.runw("vector", lambda e, jsl=jsl, m2=m2, w2=w2, t8=t8:
                       e.tensor_scalar(out=t8, in0=gt[:, jsl],
                                       scalar1=m2, scalar2=w2,
                                       op0=ALU.is_equal, op1=ALU.mult))
                p.runw("vector", lambda e, jsl=jsl, t8=t8:
                       e.tensor_tensor(out=gw[:, jsl], in0=gw[:, jsl],
                                       in1=t8, op=ALU.add))
            p.wait("tensor", "vector")
            for j in range(4):
                p.run("tensor", lambda e, j=j: e.transpose(
                    out=ps_g[0:8, 128 * j:128 * (j + 1)],
                    in_=gw[:, 8 * j:8 * j + 8],
                    identity=ct["eye_f32"][:, :]))
            p.wait("vector", "tensor")
            p.run("vector", lambda e, csl=csl: e.tensor_copy(
                out=wfull_sb[0:8, csl], in_=ps_g[0:8, 0:512]))


def emit_uv(p, ct, xbf_dram, wu_t, wv_t, wfull_sb, sw_dram, sb, PS):
    xcc, su, sw = sb["xbcc"], sb["su"], sb["sw"]
    su2 = [su, sb["su1"]]
    sw2 = [sw, sb["sw1"]]
    for cc in range(NCC):
        csl = slice(cc * 512, (cc + 1) * 512)
        p.barrier()
        for k in range(NDT):
            p.dma("sync", xcc[k][:, :], xbf_dram[128 * k:128 * (k + 1), csl])
        p.wait("tensor", "sync")
        silu_done = {}
        evac_done = {}
        dma_done = {}
        for h in range(16):
            e_idx = h // 2
            q = h % 2
            ps_u, ps_v, ps_w = PS[3 * q], PS[3 * q + 1], PS[3 * q + 2]
            suh, swh = su2[q], sw2[q]
            p.wait_at("tensor", "scalar", silu_done.get(h - 2, 0))
            p.wait_at("tensor", "vector", evac_done.get(h - 2, 0))
            for k in range(NDT):
                p.run("tensor", lambda e, w=wu_t[h][k], r=xcc[k], ps_u=ps_u,
                      st=(k == 0), sp=(k == NDT - 1):
                      e.matmul(out=ps_u[:, 0:512], lhsT=w[:, :], rhs=r[:, :],
                               start=st, stop=sp))
            for k in range(NDT):
                p.run("tensor", lambda e, w=wv_t[h][k], r=xcc[k], ps_v=ps_v,
                      st=(k == 0), sp=(k == NDT - 1):
                      e.matmul(out=ps_v[:, 0:512], lhsT=w[:, :], rhs=r[:, :],
                               start=st, stop=sp))
            p.run("tensor", lambda e, ei=e_idx, csl=csl, ps_w=ps_w: e.matmul(
                out=ps_w[:, 0:512], lhsT=ct["oh8"][:, 128 * ei:128 * (ei + 1)],
                rhs=wfull_sb[0:8, csl], start=True, stop=True))
            mm_done = p.s["tensor"].v
            p.wait_at("scalar", "tensor", mm_done)
            p.wait_at("scalar", "vector", evac_done.get(h - 2, 0))
            p.run("scalar", lambda e, ps_u=ps_u, suh=suh: e.activation(
                out=suh[:, :], in_=ps_u[:, 0:512], func=AF.Silu))
            silu_done[h] = p.s["scalar"].v
            p.wait_at("vector", "scalar", silu_done[h])
            p.wait_at("vector", "gpsimd", dma_done.get(h - 2, 0))
            p.run("vector", lambda e, ps_v=ps_v, suh=suh: e.tensor_tensor(
                out=suh[:, :], in0=suh[:, :], in1=ps_v[:, 0:512], op=ALU.mult))
            p.run("vector", lambda e, ps_w=ps_w, suh=suh, swh=swh:
                  e.tensor_tensor(
                      out=swh[:, :], in0=suh[:, :], in1=ps_w[:, 0:512],
                      op=ALU.mult))
            evac_done[h] = p.s["vector"].v
            p.wait("gpsimd", "vector")
            p.dma("gpsimd", sw_dram[128 * h:128 * (h + 1), csl], swh[:, :])
            dma_done[h] = p.s["gpsimd"].v


def emit_matvec_out(p, ct, h_dram, n_h, w_t, x_dram, out_dram, sb, PS,
                    scale=0.5, sub_dram=None):
    hcc = sb["hcc"]
    xcc1 = sb["fa"]
    occ = sb["fo"]
    for cc in range(NCC):
        csl = slice(cc * 512, (cc + 1) * 512)
        p.barrier()
        for h in range(n_h):
            p.dma("sync", hcc[h][:, :], h_dram[128 * h:128 * (h + 1), csl])
        for k in range(NDT):
            p.dma("sync", xcc1[k][:, :], x_dram[128 * k:128 * (k + 1), csl])
        p.wait("tensor", "sync")
        evac_done = {}
        for j in range(NDT):
            pj = PS[j % 7]
            p.wait_at("tensor", "vector", evac_done.get(j - 6, 0))
            for h in range(n_h):
                p.run("tensor", lambda e, w=w_t[j][h], r=hcc[h], pj=pj,
                      st=(h == 0), sp=(h == n_h - 1):
                      e.matmul(out=pj[:, 0:512], lhsT=w, rhs=r[:, :],
                               start=st, stop=sp))
            p.wait("vector", "tensor", "sync")
            p.run("vector", lambda e, j=j, pj=pj: e.scalar_tensor_tensor(
                out=occ[j][:, :], in0=pj[:, 0:512], scalar=float(scale),
                in1=xcc1[j][:, :], op0=ALU.mult, op1=ALU.add))
            evac_done[j] = p.s["vector"].v
        if sub_dram is not None:
            p.wait("sync", "vector")
            for j in range(NDT):
                p.dma("sync", xcc1[j][:, :],
                      sub_dram[128 * j:128 * (j + 1), csl])
            p.wait("vector", "sync")
            for j in range(NDT):
                p.run("vector", lambda e, j=j: e.tensor_tensor(
                    out=occ[j][:, :], in0=occ[j][:, :], in1=xcc1[j][:, :],
                    op=ALU.subtract))
        p.wait("gpsimd", "vector")
        for j in range(NDT):
            p.dma("gpsimd", out_dram[128 * j:128 * (j + 1), csl], occ[j][:, :])


def emit_linear(p, ct, xbf_dram, w_t, n_out, sb, PS, evac):
    xcc = sb["xbcc"]
    nps = 4
    for cc in range(NCC):
        p.barrier()
        csl = slice(cc * 512, (cc + 1) * 512)
        for k in range(NDT):
            p.dma("sync", xcc[k][:, :], xbf_dram[128 * k:128 * (k + 1), csl])
        p.wait("tensor", "sync")
        for j in range(n_out):
            pj = PS[j % nps]
            if j >= nps:
                p.wait("tensor", "vector", "scalar")
            for k in range(NDT):
                p.run("tensor", lambda e, w=w_t[j][k], r=xcc[k], pj=pj,
                      st=(k == 0), sp=(k == NDT - 1):
                      e.matmul(out=pj[:, 0:512], lhsT=w, rhs=r[:, :],
                               start=st, stop=sp))
            evac(p, cc, j, pj)



def ln_scratch(p):
    sb = {}
    sb["fa"] = [p.sbuf(f"fa{k}", [128, 512], F32) for k in range(NDT)]
    sb["fb"] = [p.sbuf(f"fb{k}", [128, 512], BF16) for k in range(NDT)]
    sb["fq"] = [p.sbuf(f"fq{k}", [128, 512], BF16) for k in range(NDT)]
    sb["fo"] = [p.sbuf(f"fo{k}", [128, 512], F32) for k in range(NDT)]
    sb["srow"] = p.sbuf("srow", [1, 2048], F32)
    sb["tmax"] = p.sbuf("tmax", [128, 1], F32)
    return sb


def gate_scratch(p, sb):
    sb["gl"] = p.sbuf("gl", [8, 512], F32)
    sb["gt"] = p.sbuf("gt", [128, 32], F32)
    sb["gm"] = p.sbuf("gm", [128, 32], F32)
    sb["gw"] = p.sbuf("gw", [128, 32], F32)
    sb["gs"] = p.sbuf("gs", [128, 16], F32)
    return sb


def uv_scratch(p):
    sb = {}
    sb["xbcc"] = [p.sbuf(f"xbcc{k}", [128, 512], BF16) for k in range(NDT)]
    sb["su"] = p.sbuf("su", [128, 512], BF16)
    sb["sw"] = p.sbuf("sw", [128, 512], BF16)
    sb["su1"] = p.sbuf("su1", [128, 512], BF16)
    sb["sw1"] = p.sbuf("sw1", [128, 512], BF16)
    return sb


def mv_scratch(p, n_h):
    sb = {}
    sb["hcc"] = [p.sbuf(f"hcc{h}", [128, 512], BF16) for h in range(n_h)]
    sb["fa"] = [p.sbuf(f"va{k}", [128, 512], F32) for k in range(NDT)]
    sb["fo"] = [p.sbuf(f"vo{k}", [128, 512], F32) for k in range(NDT)]
    return sb


def build_launch_A(upto=99, dbg=()):
    nc = bass.Bass("TRN2", target_bir_lowering=False)
    cd = decl_consts(nc)
    x_in = nc.dram_tensor("x_in", [D, T], F32, kind="ExternalInput").ap()
    w = {}
    for nm, sh in [("gate_w", [D, 8]), ("Wu", [8, D, 256]), ("Wv", [8, D, 256]),
                   ("Wo", [8, 256, D]), ("W_xz", [D, 2 * D]), ("W_dt", [D, D]),
                   ("b_dt", [D]), ("A_log", [D, 16]), ("W_B", [D, 16]),
                   ("W_C", [D, 16]), ("D_skip", [D]), ("W_out", [D, D]),
                   ("v_th", [D])]:
        w[nm] = nc.dram_tensor(nm, sh, F32, kind="ExternalInput").ap()
    x3_out = nc.dram_tensor("x3", [D, T], F32, kind="ExternalOutput").ap()
    xln1_out = nc.dram_tensor("xln1", [D, T], F32, kind="ExternalOutput").ap()
    mx_out = nc.dram_tensor("mx", [1, 1], F32, kind="ExternalOutput").ap()

    dr = {}
    for nm, sh, dt in [("xln0", [D, T], BF16), ("sw", [2 * D, T], BF16),
                       ("x1", [D, T], F32), ("xlnm", [D, T], BF16),
                       ("xlnm32", [D, T], F32), ("xssm", [D, T], BF16),
                       ("z", [D, T], BF16), ("dt", [D, T], F32),
                       ("dtx", [D, T], BF16), ("yg", [D, T], BF16),
                       ("x2", [D, T], F32), ("xlnjunk", [D, T], BF16),
                       ("wf", [8, T], BF16)]:
        kind = "ExternalOutput" if nm in dbg else "Internal"
        dr[nm] = nc.dram_tensor("i_" + nm, sh, dt, kind=kind).ap()

    p = Prog(nc)
    ct = load_const_tiles(p, cd)

    psA = p.psum("psA", [128, 3584], F32)
    PS = [psA[:, 512 * i:512 * (i + 1)] for i in range(7)]
    psb16 = p.psum("psAb", [128, 1024], BF16)

    gate_f = []
    for i in range(NDT):
        gt_ = p.sbuf(f"gwf{i}", [128, 8], F32)
        p.dma("sync", gt_[:, :], w["gate_w"][128 * i:128 * (i + 1), :])
        gate_f.append(gt_)
    rmax = p.sbuf("rmax", [128, 1], F32)
    mrow = p.sbuf("mrow", [1, 8], F32)

    def ffn_block(x_dram, sw_dram, out_dram, pfix):
        with p.scope():
            wfull = p.sbuf("wfull", [8, T], BF16)
            with p.scope():
                sb = gate_scratch(p, ln_scratch(p))
                emit_ln(p, ct, x_dram, dr["xln0"], sb, PS, wfull_sb=wfull,
                        gate_tiles=gate_f)
            p.wait("gpsimd", "vector")
            p.dma("gpsimd", dr["wf"][0:8, :], wfull[0:8, :])
            with p.scope():
                stage = p.sbuf("wstage", [128, 2048], F32)
                sb = uv_scratch(p)
                wu_sb = [load_w_bf(p, f"{pfix}u{h}",
                                   w["Wu"][h // 2, :, (h % 2) * 128:
                                           (h % 2) * 128 + 128],
                                   D, 128, stage) for h in range(16)]
                wv_sb = [load_w_bf(p, f"{pfix}v{h}",
                                   w["Wv"][h // 2, :, (h % 2) * 128:
                                           (h % 2) * 128 + 128],
                                   D, 128, stage) for h in range(16)]
                emit_uv(p, ct, dr["xln0"], wu_sb, wv_sb, wfull, sw_dram, sb, PS)
        with p.scope():
            stage = p.sbuf("wstage", [128, 2048], F32)
            sb = mv_scratch(p, 16)
            wo_sb = [load_w_bf(p, f"{pfix}o{h}",
                               w["Wo"][h // 2, (h % 2) * 128:
                                       (h % 2) * 128 + 128, :],
                               128, D, stage)[0] for h in range(16)]
            wo_t = [[wo_sb[h][:, 128 * j:128 * (j + 1)] for h in range(16)]
                    for j in range(NDT)]
            emit_matvec_out(p, ct, sw_dram, 16, wo_t, x_dram, out_dram, sb, PS,
                            scale=0.5)

    if upto >= 1:
        ffn_block(x_in, dr["sw"], dr["x1"], "f")

    if upto < 2:
        p.finalize()
        return nc
    with p.scope():
        sb = ln_scratch(p)
        emit_ln(p, ct, dr["x1"], dr["xlnm"], sb, PS, dst_f32_dram=dr["xlnm32"])
    if upto < 3:
        p.finalize()
        return nc
    with p.scope():
        stage = p.sbuf("wstage", [128, 2048], F32)
        sb = uv_scratch(p)
        wxz_sb = load_w_bf(p, "wxz", w["W_xz"], D, 2 * D, stage)
        wxz_t = [[wxz_sb[k][:, 128 * j:128 * (j + 1)] for k in range(NDT)]
                 for j in range(16)]

        def evac_xz(p_, cc, j, pj):
            csl = slice(cc * 512, (cc + 1) * 512)
            dst = dr["xssm"] if j < 8 else dr["z"]
            jj = j % 8
            tile = sb["su"]
            p_.wait("vector", "tensor", "gpsimd")
            p_.run("vector", lambda e: e.tensor_copy(out=tile[:, :],
                                                     in_=pj[:, 0:512]))
            p_.wait("gpsimd", "vector")
            p_.dma("gpsimd", dst[128 * jj:128 * (jj + 1), csl], tile[:, :])

        emit_linear(p, ct, dr["xlnm"], wxz_t, 16, sb, PS, evac_xz)

    if upto < 4:
        p.finalize()
        return nc
    with p.scope():
        stage = p.sbuf("wstage", [128, 1024], F32)
        sb = uv_scratch(p)
        t32l = [p.sbuf(f"t32{j}", [128, 512], F32) for j in range(2)]
        tbfl = [p.sbuf(f"tbf{j}", [128, 512], BF16) for j in range(2)]
        wdt_sb = load_w_bf(p, "wdt", w["W_dt"], D, D, stage)
        wdt_t = [[wdt_sb[k][:, 128 * j:128 * (j + 1)] for k in range(NDT)]
                 for j in range(NDT)]
        bdt_c = load_vec_cols(p, "bdt", w["b_dt"], D, ct, PS[6])

        def evac_dt(p_, cc, j, pj):
            csl = slice(cc * 512, (cc + 1) * 512)
            t32 = t32l[j % 2]
            tbf = tbfl[j % 2]
            p_.wait("scalar", "tensor", "gpsimd")
            p_.run("scalar", lambda e: e.activation(
                out=t32[:, :], in_=pj[:, 0:512], func=AF.Exp,
                bias=bdt_c[:, j:j + 1]))
            p_.run("scalar", lambda e: e.activation(
                out=t32[:, :], in_=t32[:, :], func=AF.Ln, bias=1.0))
            p_.wait("vector", "scalar", "gpsimd")
            xj = sb["xbcc"][j]
            p_.run("vector", lambda e, xj=xj: e.tensor_tensor(
                out=tbf[:, :], in0=t32[:, :], in1=xj[:, :],
                op=ALU.mult))
            p_.wait("gpsimd", "vector", "scalar")
            p_.dma("gpsimd", dr["dt"][128 * j:128 * (j + 1), csl], t32[:, :])
            p_.dma("gpsimd", dr["dtx"][128 * j:128 * (j + 1), csl], tbf[:, :])

        emit_linear(p, ct, dr["xssm"], wdt_t, NDT, sb, PS, evac_dt)

    a_sb = []
    for k in range(NDT):
        at = p.sbuf(f"aA{k}", [128, 16], F32)
        p.dma("sync", at[:, :], w["A_log"][128 * k:128 * (k + 1), :])
        p.wait("scalar", "sync")
        p.run("scalar", lambda e, at=at: e.activation(
            out=at[:, :], in_=at[:, :], func=AF.Exp))
        p.wait("vector", "scalar")
        p.run("vector", lambda e, at=at: e.tensor_scalar(
            out=at[:, :], in0=at[:, :], scalar1=-1.0, scalar2=None,
            op0=ALU.mult))
        a_sb.append(at)
    dsk_c = load_vec_cols(p, "dsk", w["D_skip"], D, ct, PS[6])
    vth_c = load_vec_cols(p, "vth", w["v_th"], D, ct, PS[6])
    p.wait("vector", "sync")
    p.run("vector", lambda e: e.tensor_scalar(
        out=vth_c[:, :], in0=vth_c[:, :], scalar1=float(V_TH_MIN),
        scalar2=-K_STEEP, op0=ALU.max, op1=ALU.mult))

    if upto < 5:
        p.finalize()
        return nc
    with p.scope():
        bm_sb = p.sbuf("bm", [16, T], BF16)
        cm_sb = p.sbuf("cm", [16, T], BF16)
        with p.scope():
            stage = p.sbuf("wstage", [128, 1024], F32)
            sbx = uv_scratch(p)
            wb_sb = load_w_bf(p, "wB", w["W_B"], D, 16, stage)
            wc_sb = load_w_bf(p, "wC", w["W_C"], D, 16, stage)
            for cc in range(NCC):
                csl = slice(cc * 512, (cc + 1) * 512)
                p.barrier()
                for k in range(NDT):
                    p.dma("sync", sbx["xbcc"][k][:, :],
                          dr["xlnm"][128 * k:128 * (k + 1), csl])
                p.wait("tensor", "sync")
                for k in range(NDT):
                    p.run("tensor", lambda e, k=k, st=(k == 0),
                          sp=(k == NDT - 1):
                          e.matmul(out=PS[0][0:16, 0:512],
                                   lhsT=wb_sb[k][:, 0:16],
                                   rhs=sbx["xbcc"][k][:, :], start=st, stop=sp))
                for k in range(NDT):
                    p.run("tensor", lambda e, k=k, st=(k == 0),
                          sp=(k == NDT - 1):
                          e.matmul(out=PS[1][0:16, 0:512],
                                   lhsT=wc_sb[k][:, 0:16],
                                   rhs=sbx["xbcc"][k][:, :], start=st, stop=sp))
                p.wait("vector", "tensor")
                p.run("vector", lambda e, csl=csl: e.tensor_copy(
                    out=bm_sb[0:16, csl], in_=PS[0][0:16, 0:512]))
                p.run("vector", lambda e, csl=csl: e.tensor_copy(
                    out=cm_sb[0:16, csl], in_=PS[1][0:16, 0:512]))

        TC = 1024
        NSC = T // TC
        if upto >= 5.5:
          with p.scope():
            dtc = [p.sbuf(f"dtc{k}", [128, TC], F32) for k in range(NDT)]
            dxc = [p.sbuf(f"dxc{k}", [128, TC], BF16) for k in range(NDT)]
            brep = [p.sbuf(f"brep{n}", [128, TC], BF16) for n in range(16)]
            crep = [p.sbuf(f"crep{n}", [128, TC], BF16) for n in range(16)]
            dec2 = [p.sbuf(f"dec{i}", [128, TC], F32) for i in range(2)]
            inp = p.sbuf("inp", [128, TC], F32)
            scn = p.sbuf("scn", [128, TC], F32)
            yk = [p.sbuf(f"ykk{k}", [128, TC], F32) for k in range(NDT)]
            st8 = p.sbuf("st8", [128, 128], F32)
            xsk = p.sbuf("xsk", [128, TC], BF16)
            zk = p.sbuf("zk", [128, TC], BF16)
            spk = p.sbuf("spk", [128, TC], F32)
            p.wait("vector", "sync")
            p.run("vector", lambda e: e.memset(st8[:, :], 0.0))
            dec_rd = [0, 0]
            for cc in range(NSC):
                csl = slice(cc * TC, (cc + 1) * TC)
                p.barrier()
                for k in range(NDT):
                    p.dma("sync", dtc[k][:, :],
                          dr["dt"][128 * k:128 * (k + 1), csl])
                    p.dma("sync", dxc[k][:, :],
                          dr["dtx"][128 * k:128 * (k + 1), csl])
                p.wait("tensor", "vector")
                for n in range(16 if upto >= 5.58 else 0):
                    for half in range(2):
                        hs = slice(cc * TC + half * 512,
                                   cc * TC + half * 512 + 512)
                        p.wait("tensor", "vector")
                        p.run("tensor", lambda e, n=n, hs=hs, b=(n % 3):
                              e.matmul(out=PS[b][:, 0:512],
                                       lhsT=ct["oh16"][:, 128 * n:128 * (n + 1)],
                                       rhs=bm_sb[0:16, hs],
                                       start=True, stop=True))
                        p.run("tensor", lambda e, n=n, hs=hs, b=3 + (n % 3):
                              e.matmul(out=PS[b][:, 0:512],
                                       lhsT=ct["oh16"][:, 128 * n:128 * (n + 1)],
                                       rhs=cm_sb[0:16, hs],
                                       start=True, stop=True))
                        p.wait("vector", "tensor")
                        p.run("vector", lambda e, n=n, half=half, b=(n % 3):
                              e.tensor_copy(
                                  out=brep[n][:, 512 * half:512 * half + 512],
                                  in_=PS[b][:, 0:512]))
                        p.run("vector", lambda e, n=n, half=half, b=3 + (n % 3):
                              e.tensor_copy(
                                  out=crep[n][:, 512 * half:512 * half + 512],
                                  in_=PS[b][:, 0:512]))
                p.wait("vector", "sync")
                p.wait("scalar", "sync", "vector")
                for k in range(NDT):
                    if upto < 5.7:
                        break
                    for n in range(16):
                        sc_col = k * 16 + n
                        pb = sc_col % 2
                        dec = dec2[pb]
                        p.wait_at("scalar", "vector", dec_rd[pb])
                        p.run("scalar", lambda e, k=k, n=n, dec=dec:
                              e.activation(
                                  out=dec[:, :], in_=dtc[k][:, :], func=AF.Exp,
                                  scale=a_sb[k][:, n:n + 1]))
                        exp_done = p.s["scalar"].v
                        p.runf("vector", lambda e, k=k, n=n: e.tensor_tensor(
                            out=inp[:, :], in0=dxc[k][:, :], in1=brep[n][:, :],
                            op=ALU.mult))
                        p.wait_at("vector", "scalar", exp_done)
                        p.runf("vector", lambda e, c=sc_col, dec=dec:
                              e.tensor_tensor_scan(
                                  out=scn[:, :], data0=dec[:, :],
                                  data1=inp[:, :], initial=st8[:, c:c + 1],
                                  op0=ALU.mult, op1=ALU.add))
                        dec_rd[pb] = p.s["vector"].v
                        p.runw("vector", lambda e, c=sc_col: e.tensor_copy(
                            out=st8[:, c:c + 1], in_=scn[:, TC - 1:TC]))
                        if upto < 5.8:
                            continue
                        p.runf("vector", lambda e, n=n: e.tensor_tensor(
                            out=scn[:, :], in0=scn[:, :], in1=crep[n][:, :],
                            op=ALU.mult))
                        if n == 0:
                            p.runf("vector", lambda e, k=k: e.tensor_copy(
                                out=yk[k][:, :], in_=scn[:, :]))
                        else:
                            p.runf("vector", lambda e, k=k: e.tensor_tensor(
                                out=yk[k][:, :], in0=yk[k][:, :],
                                in1=scn[:, :], op=ALU.add))
                    if upto < 5.9:
                        continue
                    p.wait("sync", "vector", "gpsimd")
                    p.dma("sync", xsk[:, :],
                          dr["xssm"][128 * k:128 * (k + 1), csl])
                    p.dma("sync", zk[:, :], dr["z"][128 * k:128 * (k + 1), csl])
                    p.wait("vector", "sync")
                    p.runf("vector", lambda e, k=k: e.scalar_tensor_tensor(
                        out=yk[k][:, :], in0=xsk[:, :],
                        scalar=dsk_c[:, k:k + 1], in1=yk[k][:, :],
                        op0=ALU.mult, op1=ALU.add))
                    p.wait("scalar", "vector")
                    p.run("scalar", lambda e, k=k: e.activation(
                        out=spk[:, :], in_=yk[k][:, :], func=AF.Sigmoid,
                        scale=K_STEEP, bias=vth_c[:, k:k + 1]))
                    p.runf("scalar", lambda e: e.activation(
                        out=zk[:, :], in_=zk[:, :], func=AF.Silu))
                    p.wait("vector", "scalar", "gpsimd")
                    p.runf("vector", lambda e, k=k: e.tensor_tensor(
                        out=yk[k][:, :], in0=yk[k][:, :], in1=spk[:, :],
                        op=ALU.mult))
                    p.runf("vector", lambda e, k=k: e.tensor_tensor(
                        out=xsk[:, :], in0=yk[k][:, :], in1=zk[:, :],
                        op=ALU.mult))
                    p.wait("gpsimd", "vector")
                    p.dma("gpsimd", dr["yg"][128 * k:128 * (k + 1), csl],
                          xsk[:, :])

    if upto < 6:
        p.finalize()
        return nc
    with p.scope():
        stage = p.sbuf("wstage", [128, 1024], F32)
        sb = mv_scratch(p, NDT)
        wout_sb = load_w_bf(p, "wout", w["W_out"], D, D, stage)
        wout_t = [[wout_sb[k][:, 128 * j:128 * (j + 1)] for k in range(NDT)]
                  for j in range(NDT)]
        emit_matvec_out(p, ct, dr["yg"], NDT, wout_t, dr["x1"], dr["x2"],
                        sb, PS, scale=1.0, sub_dram=dr["xlnm32"])

    if upto < 7:
        p.finalize()
        return nc
    ffn_block(dr["x2"], dr["sw"], x3_out, "g")

    if upto < 8:
        p.finalize()
        return nc
    with p.scope():
        sb = ln_scratch(p)
        emit_ln(p, ct, x3_out, dr["xlnjunk"], sb, PS, dst_f32_dram=xln1_out,
                rmax=rmax)
        p.wait("tensor", "vector")
        p.run("tensor", lambda e: e.transpose(
            out=PS[0][0:1, 0:128], in_=rmax[:, 0:1],
            identity=ct["eye_f32"][:, :]))
        p.wait("vector", "tensor")
        p.runw("vector", lambda e: e.reduce_max(
            out=mrow[0:1, 0:1], in_=PS[0][0:1, 0:128], axis=AX.X))
        p.wait("gpsimd", "vector")
        p.dma("gpsimd", mx_out[0:1, 0:1], mrow[0:1, 0:1])

    p.finalize()
    return nc





def emit_scalar_prep(p, ct, gmax_sb, srow, ps):
    s_col = p.sbuf("s_col", [128, 2], F32)
    p.wait("tensor", "sync", "vector")
    p.run("tensor", lambda e: e.transpose(
        out=ps[0:1, 0:8], in_=gmax_sb[0:8, 0:1],
        identity=ct["eye_f32"][0:8, 0:8]))
    p.wait("vector", "tensor")
    p.runw("vector", lambda e: e.reduce_max(
        out=srow[0:1, 0:1], in_=ps[0:1, 0:8], axis=AX.X))
    p.runw("vector", lambda e: e.tensor_scalar(
        out=srow[0:1, 0:1], in0=srow[0:1, 0:1], scalar1=float(QEPS),
        scalar2=1.0 / 255.0, op0=ALU.max, op1=ALU.mult))
    p.runw("vector", lambda e: e.reciprocal(
        out=srow[0:1, 1:2], in_=srow[0:1, 0:1]))
    p.wait("tensor", "vector")
    p.run("tensor", lambda e: e.matmul(
        out=ps[0:128, 8:10], lhsT=ct["ones_row"][:, :], rhs=srow[0:1, 0:2],
        start=True, stop=True))
    p.wait("vector", "tensor")
    p.runw("vector", lambda e: e.tensor_copy(out=s_col[:, 0:2],
                                             in_=ps[0:128, 8:10]))
    p.settle()
    return s_col[:, 0:1], s_col[:, 1:2]


def build_launch_B(uptoB=99):
    F = 4 * D
    NFT = F // 128
    nc = bass.Bass("TRN2", target_bir_lowering=False)
    cd = decl_consts(nc)
    xln_in = nc.dram_tensor("xln", [D, T], F32, kind="ExternalInput").ap()
    w1_in = nc.dram_tensor("w1", [F, D], F32, kind="ExternalInput").ap()
    w1t_in = nc.dram_tensor("w1t", [D, F], F32, kind="ExternalInput").ap()
    b1_in = nc.dram_tensor("b1", [F], F32, kind="ExternalInput").ap()
    gmax_in = nc.dram_tensor("gmax", [8, 1], F32, kind="ExternalInput").ap()
    h_out = nc.dram_tensor("h", [F, T], F32, kind="ExternalOutput").ap()
    mx_out = nc.dram_tensor("mx", [1, 1], F32, kind="ExternalOutput").ap()
    dbg_out = nc.dram_tensor("dbgB", [128, 128], F32, kind="ExternalOutput").ap()
    dbq_out = nc.dram_tensor("dbqB", [128, 512], F32, kind="ExternalOutput").ap()


    p = Prog(nc)
    ct = load_const_tiles(p, cd, ("ones_col_bf", "ones_row", "eye_f32",
                                  "oh32f"))
    psA = p.psum("psB", [128, 3584], F32)
    PS = [psA[:, 512 * i:512 * (i + 1)] for i in range(7)]
    psb16 = p.psum("psBb", [128, 1024], BF16)
    srow = p.sbuf("srowB", [1, 16], F32)
    gmax_sb = p.sbuf("gmaxB", [8, 1], F32)
    p.dma("sync", gmax_sb[:, :], gmax_in[:, :])
    s_colB, rs_col = emit_scalar_prep(p, ct, gmax_sb, srow, PS[6])

    sw_c = p.sbuf("sw_c", [128, NFT], F32)
    swr_c = p.sbuf("swr_c", [128, NFT], F32)
    sbq = p.sbuf("sbq", [128, NFT], F32)
    bterm = p.sbuf("bterm", [128, NFT], F32)
    b1c = load_vec_cols(p, "b1c", b1_in, F, ct, PS[6])
    with p.scope():
        stage = p.sbuf("wstageB", [128, 1024], F32)
        for i in range(NFT):
            p.barrier()
            p.dma("sync", stage[:, 0:D], w1_in[128 * i:128 * (i + 1), :])
            p.barrier()
            if i == 0:
                p.dma("gpsimd", dbq_out[:, :], stage[:, 0:512])
                p.barrier()
            p.runw("vector", lambda e, i=i, st=stage: e.reduce_max(
                out=sw_c[:, i:i + 1], in_=st[:, 0:D], axis=AX.X,
                apply_absolute_value=True))
        tq1 = p.sbuf("tq1", [128, NFT], F32)
        tq2 = p.sbuf("tq2", [128, NFT], F32)
        p.wait("vector", "sync")
        p.runw("vector", lambda e: e.tensor_scalar(
            out=tq1[:, :], in0=sw_c[:, :], scalar1=float(QEPS),
            scalar2=1.0 / 127.0, op0=ALU.max, op1=ALU.mult))
        p.runw("vector", lambda e: e.tensor_copy(out=sw_c[:, :],
                                                 in_=tq1[:, :]))
        p.wait("gpsimd", "vector")
        p.dma("gpsimd", dbg_out[0:128, 98:128], sw_c[:, 0:30])
        p.runw("vector", lambda e: e.reciprocal(out=swr_c[:, :],
                                                in_=sw_c[:, :]))
        p.runw("vector", lambda e: e.tensor_scalar(
            out=sbq[:, :], in0=sw_c[:, :], scalar1=s_colB, scalar2=None,
            op0=ALU.mult))
        p.runw("vector", lambda e: e.reciprocal(out=tq1[:, :],
                                                in_=sbq[:, :]))
        p.runw("vector", lambda e: e.tensor_tensor(
            out=tq2[:, :], in0=b1c[:, :], in1=tq1[:, :], op=ALU.mult))
        p.runw("vector", lambda e: e.tensor_scalar(
            out=tq1[:, :], in0=tq2[:, :], scalar1=MAGIC, scalar2=MAGIC,
            op0=ALU.add, op1=ALU.subtract))
        p.runw("vector", lambda e: e.tensor_tensor(
            out=bterm[:, :], in0=sbq[:, :], in1=tq1[:, :], op=ALU.mult))

    if uptoB < 2:
        p.finalize()
        return nc
    qw = [p.sbuf(f"qw1t{i}", [128, F], BF16) for i in range(NDT)]
    with p.scope():
        stage = p.sbuf("wstageB2", [128, F], F32)
        rep = p.sbuf("repB", [128, 512], F32)
        r32 = p.sbuf("r32B", [NFT, 128], F32)
        if uptoB >= 2.2:
            p.wait("tensor", "vector", "sync")
            p.run("tensor", lambda e: e.transpose(
                out=PS[0][0:NFT, 0:128], in_=swr_c[:, :],
                identity=ct["eye_f32"][:, :]))
            p.wait("vector", "tensor")
            p.run("vector", lambda e: e.tensor_copy(out=r32[:, :],
                                                    in_=PS[0][0:NFT, 0:128]))
            p.settle()
        for i in range(NDT):
            t = qw[i]
            p.wait("sync", "vector")
            for q4 in range(4):
                p.dma("sync", stage[:, 1024 * q4:1024 * (q4 + 1)],
                      w1t_in[128 * i:128 * (i + 1),
                             1024 * q4:1024 * (q4 + 1)])
            for fc in range(F // 512):
                p.barrier()
                if uptoB >= 2.4:
                    for s4 in range(4):
                        fr = fc * 4 + s4
                        p.run("tensor", lambda e, fr=fr, s4=s4: e.matmul(
                            out=PS[1][:, 128 * s4:128 * (s4 + 1)],
                            lhsT=ct["oh32f"][:, 128 * fr:128 * (fr + 1)],
                            rhs=r32[0:NFT, :], start=True, stop=True))
                    p.wait("vector", "tensor")
                if uptoB >= 2.6:
                    fsl = slice(512 * fc, 512 * (fc + 1))
                    p.run("vector", lambda e, fsl=fsl, st=stage:
                          e.tensor_tensor(
                              out=rep[:, :], in0=st[:, fsl],
                              in1=PS[1][:, 0:512], op=ALU.mult))
                    p.run("vector", lambda e, t=t, fsl=fsl: e.tensor_scalar(
                        out=t[:, fsl], in0=rep[:, :], scalar1=MAGIC,
                        scalar2=MAGIC, op0=ALU.add, op1=ALU.subtract))

    if uptoB < 3:
        p.finalize()
        return nc
    hmax = p.sbuf("hmax", [128, 1], F32)
    p.wait("vector", "sync")
    p.run("vector", lambda e: e.memset(hmax[:, :], 0.0))
    with p.scope():
        xcc = [p.sbuf(f"qx{k}", [128, 512], F32) for k in range(NDT)]
        qx = [p.sbuf(f"qxb{k}", [128, 512], BF16) for k in range(NDT)]
        hcc2 = [p.sbuf(f"hccB{q}", [128, 512], F32) for q in range(2)]
        tm = p.sbuf("tmB", [128, 1], F32)
        for cc in range(NCC):
            csl = slice(cc * 512, (cc + 1) * 512)
            p.barrier()
            for k in range(NDT):
                p.dma("sync", xcc[k][:, :], xln_in[128 * k:128 * (k + 1), csl])
            p.barrier()
            for k in range(NDT):
                p.run("vector", lambda e, k=k: e.tensor_scalar(
                    out=xcc[k][:, :], in0=xcc[k][:, :], scalar1=rs_col,
                    scalar2=MAGIC, op0=ALU.mult, op1=ALU.add))
                p.run("vector", lambda e, k=k: e.tensor_scalar(
                    out=xcc[k][:, :], in0=xcc[k][:, :], scalar1=MAGIC,
                    scalar2=0.0, op0=ALU.subtract, op1=ALU.max))
                p.run("vector", lambda e, k=k: e.tensor_scalar(
                    out=qx[k][:, :], in0=xcc[k][:, :], scalar1=255.0,
                    scalar2=None, op0=ALU.min))

            quant_done = p.s["vector"].v
            p.wait("tensor", "scalar")
            evac_done = {}
            dma_done = {}
            for j in range(NFT):
                pj = PS[j % 7]
                hq = hcc2[j % 2]
                p.wait_at("tensor", "vector",
                          max(quant_done, evac_done.get(j - 6, 0)))
                for k in range(NDT):
                    p.run("tensor", lambda e, w=qw[k], j=j, k=k, pj=pj,
                          st=(k == 0), sp=(k == NDT - 1):
                          e.matmul(out=pj[:, 0:512],
                                   lhsT=w[:, 128 * j:128 * (j + 1)],
                                   rhs=qx[k][:, :], start=st, stop=sp))
                p.wait("vector", "tensor")
                p.wait_at("vector", "gpsimd", dma_done.get(j - 2, 0))
                p.run("vector", lambda e, j=j, pj=pj, hq=hq: e.tensor_scalar(
                    out=hq[:, :], in0=pj[:, 0:512], scalar1=sbq[:, j:j + 1],
                    scalar2=bterm[:, j:j + 1], op0=ALU.mult, op1=ALU.add))
                p.run("vector", lambda e, hq=hq: e.tensor_scalar(
                    out=hq[:, :], in0=hq[:, :], scalar1=0.0, scalar2=None,
                    op0=ALU.max))
                p.runw("vector", lambda e, hq=hq: e.reduce_max(
                    out=tm[:, 0:1], in_=hq[:, :], axis=AX.X))
                p.runw("vector", lambda e: e.tensor_tensor(
                    out=hmax[:, 0:1], in0=hmax[:, 0:1], in1=tm[:, 0:1],
                    op=ALU.max))
                evac_done[j] = p.s["vector"].v
                p.wait("gpsimd", "vector")
                p.dma("gpsimd", h_out[128 * j:128 * (j + 1), csl], hq[:, :])
                dma_done[j] = p.s["gpsimd"].v
    p.wait("gpsimd", "vector", "scalar", "tensor", "sync")
    p.dma("gpsimd", dbg_out[0:128, 0:2], s_colB.tensor.ap()[:, 0:2])
    p.dma("gpsimd", dbg_out[0:128, 2:34], sbq[:, :])
    p.dma("gpsimd", dbg_out[0:128, 34:66], bterm[:, :])
    p.dma("gpsimd", dbg_out[0:128, 66:98], sw_c[:, :])
    p.wait("tensor", "vector", "sync")
    p.run("tensor", lambda e: e.transpose(
        out=PS[0][0:1, 0:128], in_=hmax[:, 0:1], identity=ct["eye_f32"][:, :]))
    p.wait("vector", "tensor")
    p.runw("vector", lambda e: e.reduce_max(
        out=srow[0:1, 2:3], in_=PS[0][0:1, 0:128], axis=AX.X))
    p.wait("gpsimd", "vector")
    p.dma("gpsimd", mx_out[0:1, 0:1], srow[0:1, 2:3])
    p.finalize()
    return nc


def emit_w2_quant(p, ct, w2_in, w2t_in, b2_in, gmax_sb, srow, PS):
    F = 4 * D
    NFT = F // 128
    s_col, rs_col = emit_scalar_prep(p, ct, gmax_sb, srow, PS[6])
    sw_c = p.sbuf("sw2c", [128, NDT], F32)
    swr_c = p.sbuf("sw2r", [128, NDT], F32)
    sbq = p.sbuf("sbq2", [128, NDT], F32)
    bterm = p.sbuf("bterm2", [128, NDT], F32)
    b2c = load_vec_cols(p, "b2c", b2_in, D, ct, PS[6])
    with p.scope():
        stage = p.sbuf("wstageC", [128, F], F32)
        for i in range(NDT):
            p.wait("sync", "vector")
            p.dma("sync", stage[:, :], w2_in[128 * i:128 * (i + 1), :])
            p.wait("vector", "sync")
            p.runw("vector", lambda e, i=i, st=stage: e.reduce_max(
                out=sw_c[:, i:i + 1], in_=st[:, :], axis=AX.X,
                apply_absolute_value=True))
        p.wait("vector", "sync")
        p.runw("vector", lambda e: e.tensor_scalar(
            out=sw_c[:, :], in0=sw_c[:, :], scalar1=float(QEPS),
            scalar2=1.0 / 127.0, op0=ALU.max, op1=ALU.mult))
        p.runw("vector", lambda e: e.reciprocal(out=swr_c[:, :],
                                                in_=sw_c[:, :]))
        p.runw("vector", lambda e: e.tensor_scalar(
            out=sbq[:, :], in0=sw_c[:, :], scalar1=s_col, scalar2=None,
            op0=ALU.mult))
        p.runw("vector", lambda e: e.reciprocal(out=bterm[:, :],
                                                in_=sbq[:, :]))
        p.runw("vector", lambda e: e.tensor_tensor(
            out=bterm[:, :], in0=b2c[:, :], in1=bterm[:, :], op=ALU.mult))
        p.runw("vector", lambda e: e.tensor_scalar(
            out=bterm[:, :], in0=bterm[:, :], scalar1=MAGIC, scalar2=MAGIC,
            op0=ALU.add, op1=ALU.subtract))
        p.runw("vector", lambda e: e.tensor_tensor(
            out=bterm[:, :], in0=sbq[:, :], in1=bterm[:, :], op=ALU.mult))
        p.settle()

    qw = [p.sbuf(f"qw2t{i}", [128, D], BF16) for i in range(F // 128)]
    with p.scope():
        stage = p.sbuf("wstageC2", [128, D], F32)
        rep = p.sbuf("repC", [128, D], F32)
        r8 = p.sbuf("r8C", [NDT, 128], F32)
        p.wait("tensor", "vector", "sync")
        p.run("tensor", lambda e: e.transpose(
            out=PS[0][0:NDT, 0:128], in_=swr_c[:, :],
            identity=ct["eye_f32"][:, :]))
        p.wait("vector", "tensor")
        p.run("vector", lambda e: e.tensor_copy(out=r8[:, :],
                                                in_=PS[0][0:NDT, 0:128]))
        p.settle()
        p.wait("tensor", "vector")
        for s8 in range(NDT):
            b = s8 % 2
            p.wait("tensor", "vector")
            p.run("tensor", lambda e, s8=s8, b=b: e.matmul(
                out=PS[b][:, 0:128],
                lhsT=ct["oh8f"][:, 128 * s8:128 * (s8 + 1)],
                rhs=r8[0:NDT, :], start=True, stop=True))
            p.wait("vector", "tensor")
            p.run("vector", lambda e, s8=s8, b=b: e.tensor_copy(
                out=rep[:, 128 * s8:128 * (s8 + 1)], in_=PS[b][:, 0:128]))
        for i in range(F // 128):
            t = qw[i]
            p.wait("sync", "vector")
            p.dma("sync", stage[:, :], w2t_in[128 * i:128 * (i + 1), :])
            p.wait("vector", "sync")
            p.run("vector", lambda e, t=t, st=stage: e.tensor_tensor(
                out=st[:, :], in0=st[:, :], in1=rep[:, :], op=ALU.mult))
            p.run("vector", lambda e, t=t, st=stage: e.tensor_scalar(
                out=t[:, :], in0=st[:, :], scalar1=MAGIC, scalar2=MAGIC,
                op0=ALU.add, op1=ALU.subtract))
    return qw, sbq, bterm, s_col, rs_col


def emit_qh_w2_resid(p, ct, h_in, x_in, out_dram, qw, sbq, bterm, s_q, PS,
                     scale=0.5):
    F = 4 * D
    NFT = F // 128
    with p.scope():
        hcc = [p.sbuf(f"hC{k}", [128, 512], F32) for k in range(NFT)]
        qh = [p.sbuf(f"qhC{k}", [128, 512], BF16) for k in range(NFT)]
        occ = [p.sbuf(f"oC{j}", [128, 512], F32) for j in range(NDT)]
        xcc = [p.sbuf(f"xC{j}", [128, 512], F32) for j in range(NDT)]
        for cc in range(NCC):
            csl = slice(cc * 512, (cc + 1) * 512)
            p.barrier()
            for k in range(NFT):
                p.dma("sync", hcc[k][:, :], h_in[128 * k:128 * (k + 1), csl])
            for j in range(NDT):
                p.dma("sync", xcc[j][:, :], x_in[128 * j:128 * (j + 1), csl])
            p.wait("vector", "sync")
            for k in range(NFT):
                p.run("vector", lambda e, k=k: e.tensor_scalar(
                    out=hcc[k][:, :], in0=hcc[k][:, :], scalar1=s_q,
                    scalar2=MAGIC, op0=ALU.mult, op1=ALU.add))
                p.run("vector", lambda e, k=k: e.tensor_scalar(
                    out=qh[k][:, :], in0=hcc[k][:, :], scalar1=MAGIC,
                    scalar2=None, op0=ALU.subtract))
            quant_done = p.s["vector"].v
            p.wait("tensor", "scalar")
            evac_done = {}
            for j in range(NDT):
                pj = PS[j % 7]
                p.wait_at("tensor", "vector",
                          max(quant_done, evac_done.get(j - 6, 0)))
                for k in range(NFT):
                    p.run("tensor", lambda e, w=qw[k], j=j, k=k, pj=pj,
                          st=(k == 0), sp=(k == NFT - 1):
                          e.matmul(out=pj[:, 0:512],
                                   lhsT=w[:, 128 * j:128 * (j + 1)],
                                   rhs=qh[k][:, :], start=st, stop=sp))
                p.wait("vector", "tensor")
                p.run("vector", lambda e, j=j, pj=pj: e.tensor_scalar(
                    out=occ[j][:, :], in0=pj[:, 0:512],
                    scalar1=sbq[:, j:j + 1], scalar2=bterm[:, j:j + 1],
                    op0=ALU.mult, op1=ALU.add))
                p.run("vector", lambda e, j=j: e.scalar_tensor_tensor(
                    out=occ[j][:, :], in0=occ[j][:, :], scalar=float(scale),
                    in1=xcc[j][:, :], op0=ALU.mult, op1=ALU.add))
                evac_done[j] = p.s["vector"].v
            p.wait("gpsimd", "vector")
            for j in range(NDT):
                p.dma("gpsimd", out_dram[128 * j:128 * (j + 1), csl],
                      occ[j][:, :])


def build_launch_E():
    F = 4 * D
    nc = bass.Bass("TRN2", target_bir_lowering=False)
    cd = decl_consts(nc)
    x_in = nc.dram_tensor("x", [D, T], F32, kind="ExternalInput").ap()
    h_in = nc.dram_tensor("h", [F, T], F32, kind="ExternalInput").ap()
    w2_in = nc.dram_tensor("w2", [D, F], F32, kind="ExternalInput").ap()
    w2t_in = nc.dram_tensor("w2t", [F, D], F32, kind="ExternalInput").ap()
    b2_in = nc.dram_tensor("b2", [D], F32, kind="ExternalInput").ap()
    gmax_in = nc.dram_tensor("gmax", [8, 1], F32, kind="ExternalInput").ap()
    y_out = nc.dram_tensor("y", [D, T], F32, kind="ExternalOutput").ap()

    p = Prog(nc)
    ct = load_const_tiles(p, cd, ("ones_col_bf", "ones_row", "eye_f32",
                                  "oh8f"))
    psA = p.psum("psE", [128, 3584], F32)
    PS = [psA[:, 512 * i:512 * (i + 1)] for i in range(7)]
    psb16 = p.psum("psEb", [128, 1024], BF16)
    srow = p.sbuf("srowE", [1, 16], F32)
    gmax_sb = p.sbuf("gmaxE", [8, 1], F32)
    p.dma("sync", gmax_sb[:, :], gmax_in[:, :])
    qw, sbq, bterm, s_col, rs_col = emit_w2_quant(
        p, ct, w2_in, w2t_in, b2_in, gmax_sb, srow, PS)
    emit_qh_w2_resid(p, ct, h_in, x_in, y_out, qw, sbq, bterm, rs_col, PS)
    p.finalize()
    return nc




TWOPI = float(np.float32(2.0 * np.pi))
NS4 = 512
NNT = NS4 // 128
GCH = 128
NCH = T // GCH


def row_frac_sincos(p, rowv, tmp, out_sin, out_cos, width=512):
    a, b = tmp[0:1, 0:width], tmp[0:1, width:2 * width]
    p.run("vector", lambda e: e.tensor_scalar(
        out=a, in0=rowv, scalar1=MAGIC, scalar2=MAGIC, op0=ALU.add,
        op1=ALU.subtract))
    p.run("vector", lambda e: e.tensor_tensor(
        out=a, in0=rowv, in1=a, op=ALU.subtract))
    p.run("vector", lambda e: e.tensor_scalar(
        out=a, in0=a, scalar1=TWOPI, scalar2=None, op0=ALU.mult))
    p.wait("scalar", "vector")
    p.run("scalar", lambda e: e.activation(out=out_sin, in_=a, func=AF.Sin))
    p.wait("vector", "scalar")
    p.run("vector", lambda e: e.tensor_scalar(
        out=b, in0=rowv, scalar1=0.25, scalar2=None, op0=ALU.add))
    p.run("vector", lambda e: e.tensor_scalar(
        out=a, in0=b, scalar1=MAGIC, scalar2=MAGIC, op0=ALU.add,
        op1=ALU.subtract))
    p.run("vector", lambda e: e.tensor_tensor(
        out=a, in0=b, in1=a, op=ALU.subtract))
    p.run("vector", lambda e: e.tensor_scalar(
        out=a, in0=a, scalar1=TWOPI, scalar2=None, op0=ALU.mult))
    p.wait("scalar", "vector")
    p.run("scalar", lambda e: e.activation(out=out_cos, in_=a, func=AF.Sin))
    p.wait("vector", "scalar")


def rows_to_cols(p, ct, row_ap, out_cols, nj, ps, bounce):
    n = nj * 128
    assert n == 512
    p.wait("gpsimd", "vector", "scalar", "tensor", "sync")
    p.dma("gpsimd", bounce.rearrange("(a b) -> a b", a=1), row_ap[0:1, 0:n])
    p.wait("sync", "gpsimd")
    p.dma("sync", out_cols[:, 0:nj],
          bounce.rearrange("(j q) -> q j", q=128), slow=True)


def build_launch_C(dbg=()):
    F = 4 * D
    nc = bass.Bass("TRN2", target_bir_lowering=False)
    cd = decl_consts(nc)
    x_in = nc.dram_tensor("x", [D, T], F32, kind="ExternalInput").ap()
    h_in = nc.dram_tensor("h", [F, T], F32, kind="ExternalInput").ap()
    w2_in = nc.dram_tensor("w2", [D, F], F32, kind="ExternalInput").ap()
    w2t_in = nc.dram_tensor("w2t", [F, D], F32, kind="ExternalInput").ap()
    b2_in = nc.dram_tensor("b2", [D], F32, kind="ExternalInput").ap()
    gmax_in = nc.dram_tensor("gmax", [8, 1], F32, kind="ExternalInput").ap()
    lr_in = nc.dram_tensor("Lr", [NS4], F32, kind="ExternalInput").ap()
    li_in = nc.dram_tensor("Li", [NS4], F32, kind="ExternalInput").ap()
    cr_in = nc.dram_tensor("Cr", [D, NS4], F32, kind="ExternalInput").ap()
    ci_in = nc.dram_tensor("Ci", [D, NS4], F32, kind="ExternalInput").ap()
    pd_in = nc.dram_tensor("pD", [D], F32, kind="ExternalInput").ap()
    lg_in = nc.dram_tensor("lg", [D], F32, kind="ExternalInput").ap()
    lb_in = nc.dram_tensor("lb", [D], F32, kind="ExternalInput").ap()
    x5_out = nc.dram_tensor("x5", [D, T], F32, kind="ExternalOutput").ap()
    bounce = nc.dram_tensor("i_bounce", [512], F32, kind="Internal").ap()
    xln2_out = nc.dram_tensor("xln2", [D, T], F32, kind="ExternalOutput").ap()
    mx_out = nc.dram_tensor("mx", [1, 1], F32, kind="ExternalOutput").ap()

    dr = {}
    for nm, sh, dt in [("x4", [D, T], F32), ("t1", [D, T], F32),
                       ("u", [D, T], BF16), ("ut", [T, D], BF16),
                       ("ys", [D, T], BF16), ("junk", [D, T], BF16)]:
        kind = "ExternalOutput" if nm in dbg else "Internal"
        dr[nm] = nc.dram_tensor("i_" + nm, sh, dt, kind=kind).ap()
    dbg_out = None
    if dbg:
        dbg_out = nc.dram_tensor("dbgC", [128, 1024], F32,
                                 kind="ExternalOutput").ap()

    p = Prog(nc)
    ct = load_const_tiles(p, cd, ("ones_col_bf", "ones_row", "eye_bf",
                                  "eye_f32", "ramp", "revramp", "oh8f"))
    psA = p.psum("psC", [128, 3584], F32)
    PS = [psA[:, 512 * i:512 * (i + 1)] for i in range(7)]
    psb16 = p.psum("psCb", [128, 1024], BF16)
    srow = p.sbuf("srowC", [1, 16], F32)
    gmax_sb = p.sbuf("gmaxC", [8, 1], F32)
    p.dma("sync", gmax_sb[:, :], gmax_in[:, :])
    rmax = p.sbuf("rmaxC", [128, 1], F32)

    with p.scope():
        qw, sbq, bterm, s_col, rs_col = emit_w2_quant(
            p, ct, w2_in, w2t_in, b2_in, gmax_sb, srow, PS)
        emit_qh_w2_resid(p, ct, h_in, x_in, dr["x4"], qw, sbq, bterm,
                         rs_col, PS)

    lg_c = load_vec_cols(p, "lgc", lg_in, D, ct, PS[6])
    lb_c = load_vec_cols(p, "lbc", lb_in, D, ct, PS[6])
    with p.scope():
        sb = ln_scratch(p)
        emit_ln(p, ct, dr["x4"], dr["junk"], sb, PS, dst_f32_dram=dr["t1"])
    with p.scope():
        sb = ln_scratch(p)
        sb["utt"] = p.sbuf("utt", [128, 128], BF16)
        emit_ln(p, ct, dr["t1"], dr["u"], sb, PS, gb=(lg_c, lb_c),
                ut_dram=dr["ut"], psb=psb16)

    Pre_bf = [p.sbuf(f"Preb{t}", [128, 512], BF16) for t in range(NNT)]
    Pim_bf = [p.sbuf(f"Pimb{t}", [128, 512], BF16) for t in range(NNT)]
    nPre_bf = [p.sbuf(f"nPreb{t}", [128, 512], BF16) for t in range(NNT)]
    nPim_bf = [p.sbuf(f"nPimb{t}", [128, 512], BF16) for t in range(NNT)]
    nlam_c = p.sbuf("nlam_c", [128, 2 * NNT], F32)
    plam_c = p.sbuf("plam_c", [128, 2 * NNT], F32)
    Cer = [p.sbuf(f"Cer{t}", [128, D], BF16) for t in range(NNT)]
    Cei = [p.sbuf(f"Cei{t}", [128, D], BF16) for t in range(NNT)]
    Kb = [p.sbuf(f"Kb{j}", [128, GCH], F32) for j in range(NDT)]
    c0_c = p.sbuf("c0c", [128, NDT], F32)
    WinR = p.sbuf("WinR", [128, 512], BF16)
    WinI = p.sbuf("WinI", [128, 512], BF16)
    pd_c = load_vec_cols(p, "pdc", pd_in, D, ct, PS[6])
    c3s = contextlib.ExitStack()
    p.scopes.append(c3s)
    lam = p.sbuf("lam", [1, 2048], F32)
    lr_row = lam[0:1, 0:512]
    li2p_row = lam[0:1, 512:1024]
    tmp2 = p.sbuf("tmp2", [1, 4096], F32)
    p.dma("sync", lam[0:1, 1024:1536], lr_in.rearrange("(a b) -> a b", a=1))
    p.dma("sync", lam[0:1, 1536:2048], li_in.rearrange("(a b) -> a b", a=1))
    p.wait("scalar", "sync")
    p.run("scalar", lambda e: e.activation(out=tmp2[0:1, 0:512],
                                           in_=lam[0:1, 1024:1536],
                                           func=AF.Exp))
    p.run("scalar", lambda e: e.activation(out=tmp2[0:1, 512:1024],
                                           in_=lam[0:1, 1536:2048],
                                           func=AF.Exp))
    p.wait("vector", "scalar")
    p.run("vector", lambda e: e.tensor_scalar(
        out=lr_row, in0=tmp2[0:1, 0:512], scalar1=-1.0, scalar2=None,
        op0=ALU.mult))
    p.run("vector", lambda e: e.tensor_scalar(
        out=li2p_row, in0=tmp2[0:1, 512:1024], scalar1=float(1.0 / TWOPI),
        scalar2=None, op0=ALU.mult))
    lr_c = p.sbuf("lr_c", [128, NNT], F32)
    li2p_c = p.sbuf("li2p_c", [128, NNT], F32)
    rows_to_cols(p, ct, lr_row, lr_c, NNT, PS[6], bounce)
    rows_to_cols(p, ct, li2p_row, li2p_c, NNT, PS[6], bounce)
    p.wait("vector", "sync")
    p.wait("scalar", "sync")
    p.settle()
    if dbg_out is not None:
        p.wait("gpsimd", "vector", "scalar", "tensor", "sync")
        p.dma("gpsimd", dbg_out[0:1, 156:668], lr_row)
        p.dma("gpsimd", dbg_out[0:128, 136:140], lr_c[:, :])
        p.dma("gpsimd", dbg_out[0:1, 668:1024], lam[0:1, 1024:1380])
        p.wait("gpsimd", "gpsimd")

    ramp_rep = p.sbuf("ramp_rep", [128, 512], F32)
    p.wait("tensor", "sync", "vector")
    p.run("tensor", lambda e: e.matmul(
        out=PS[5][:, 0:512], lhsT=ct["ones_row"][:, :], rhs=ct["ramp"][0:1, :],
        start=True, stop=True))
    p.wait("vector", "tensor")
    p.run("vector", lambda e: e.tensor_copy(out=ramp_rep[:, :],
                                            in_=PS[5][:, 0:512]))

    with p.scope():
        Pre = [p.sbuf(f"Pre{t}", [128, 512], F32) for t in range(NNT)]
        Pim = [p.sbuf(f"Pim{t}", [128, 512], F32) for t in range(NNT)]
        mag = p.sbuf("mag", [128, 512], F32)
        mcy = p.sbuf("mcy", [128, 512], F32)
        tfr = p.sbuf("tfr", [128, 512], F32)
        sn = p.sbuf("sn", [128, 512], F32)
        cn = p.sbuf("cn", [128, 512], F32)
        for t in range(NNT):
            p.run("vector", lambda e, t=t: e.tensor_scalar(
                out=mag[:, :], in0=ramp_rep[:, :], scalar1=lr_c[:, t:t + 1],
                scalar2=None, op0=ALU.mult))
            p.wait("scalar", "vector")
            p.run("scalar", lambda e: e.activation(out=mag[:, :],
                                                   in_=mag[:, :], func=AF.Exp))
            p.run("vector", lambda e, t=t: e.tensor_scalar(
                out=mcy[:, :], in0=ramp_rep[:, :], scalar1=li2p_c[:, t:t + 1],
                scalar2=None, op0=ALU.mult))
            p.run("vector", lambda e: e.tensor_scalar(
                out=tfr[:, :], in0=mcy[:, :], scalar1=MAGIC, scalar2=MAGIC,
                op0=ALU.add, op1=ALU.subtract))
            p.run("vector", lambda e: e.tensor_tensor(
                out=tfr[:, :], in0=mcy[:, :], in1=tfr[:, :], op=ALU.subtract))
            p.run("vector", lambda e: e.tensor_scalar(
                out=tfr[:, :], in0=tfr[:, :], scalar1=TWOPI, scalar2=None,
                op0=ALU.mult))
            p.wait("scalar", "vector")
            p.run("scalar", lambda e: e.activation(out=sn[:, :], in_=tfr[:, :],
                                                   func=AF.Sin))
            p.wait("vector", "scalar")
            p.run("vector", lambda e: e.tensor_scalar(
                out=mcy[:, :], in0=mcy[:, :], scalar1=0.25, scalar2=None,
                op0=ALU.add))
            p.run("vector", lambda e: e.tensor_scalar(
                out=tfr[:, :], in0=mcy[:, :], scalar1=MAGIC, scalar2=MAGIC,
                op0=ALU.add, op1=ALU.subtract))
            p.run("vector", lambda e: e.tensor_tensor(
                out=tfr[:, :], in0=mcy[:, :], in1=tfr[:, :], op=ALU.subtract))
            p.run("vector", lambda e: e.tensor_scalar(
                out=tfr[:, :], in0=tfr[:, :], scalar1=TWOPI, scalar2=None,
                op0=ALU.mult))
            p.wait("scalar", "vector")
            p.run("scalar", lambda e: e.activation(out=cn[:, :], in_=tfr[:, :],
                                                   func=AF.Sin))
            p.wait("vector", "scalar")
            p.run("vector", lambda e, t=t: e.tensor_tensor(
                out=Pre[t][:, :], in0=mag[:, :], in1=cn[:, :], op=ALU.mult))
            p.run("vector", lambda e, t=t: e.tensor_tensor(
                out=Pim[t][:, :], in0=mag[:, :], in1=sn[:, :], op=ALU.mult))
            p.run("vector", lambda e, t=t: e.tensor_copy(
                out=Pre_bf[t][:, :], in_=Pre[t][:, :]))
            p.run("vector", lambda e, t=t: e.tensor_copy(
                out=Pim_bf[t][:, :], in_=Pim[t][:, :]))
            p.run("vector", lambda e, t=t: e.tensor_scalar(
                out=nPre_bf[t][:, :], in0=Pre[t][:, :], scalar1=-1.0,
                scalar2=None, op0=ALU.mult))
            p.run("vector", lambda e, t=t: e.tensor_scalar(
                out=nPim_bf[t][:, :], in0=Pim[t][:, :], scalar1=-1.0,
                scalar2=None, op0=ALU.mult))
            p.runw("vector", lambda e, t=t: e.tensor_scalar(
                out=nlam_c[:, 2 * t:2 * t + 1], in0=Pre[t][:, 128:129],
                scalar1=-1.0, scalar2=None, op0=ALU.mult))
            p.runw("vector", lambda e, t=t: e.tensor_scalar(
                out=nlam_c[:, 2 * t + 1:2 * t + 2], in0=Pim[t][:, 128:129],
                scalar1=-1.0, scalar2=None, op0=ALU.mult))
            p.runw("vector", lambda e, t=t: e.tensor_copy(
                out=plam_c[:, 2 * t:2 * t + 1], in_=Pre[t][:, 128:129]))
            p.runw("vector", lambda e, t=t: e.tensor_copy(
                out=plam_c[:, 2 * t + 1:2 * t + 2], in_=Pim[t][:, 128:129]))
        p.settle()

    wr_c = p.sbuf("wr_c", [128, NNT], F32)
    wi_c = p.sbuf("wi_c", [128, NNT], F32)
    nwi_c = p.sbuf("nwi_c", [128, NNT], F32)
    with p.scope():
        rw = p.sbuf("rw", [1, 4608], F32)
        elr = rw[0:1, 0:512]
        cosi = rw[0:1, 512:1024]
        sini = rw[0:1, 1024:1536]
        er = rw[0:1, 1536:2048]
        ei = rw[0:1, 2048:2560]
        den = rw[0:1, 2560:3072]
        wrr = rw[0:1, 3072:3584]
        wir = rw[0:1, 3584:4096]
        p.wait("scalar", "vector")
        p.run("scalar", lambda e: e.activation(out=elr, in_=lr_row,
                                               func=AF.Exp))
        p.wait("vector", "scalar")
        row_frac_sincos(p, li2p_row, tmp2, sini, cosi)
        p.run("vector", lambda e: e.tensor_tensor(
            out=er, in0=elr, in1=cosi, op=ALU.mult))
        p.run("vector", lambda e: e.tensor_scalar(
            out=er, in0=er, scalar1=-1.0, scalar2=None, op0=ALU.add))
        p.run("vector", lambda e: e.tensor_tensor(
            out=ei, in0=elr, in1=sini, op=ALU.mult))
        p.run("vector", lambda e: e.tensor_scalar(
            out=den, in0=li2p_row, scalar1=TWOPI, scalar2=None, op0=ALU.mult))
        p.run("vector", lambda e: e.tensor_tensor(
            out=wrr, in0=lr_row, in1=lr_row, op=ALU.mult))
        p.run("vector", lambda e: e.scalar_tensor_tensor(
            out=den, in0=den, scalar=1.0, in1=den, op0=ALU.mult,
            op1=ALU.mult))
        p.run("vector", lambda e: e.tensor_tensor(
            out=den, in0=wrr, in1=den, op=ALU.add))
        p.run("vector", lambda e: e.reciprocal(out=den, in_=den))
        li_full = rw[0:1, 4096:4608]
        p.run("vector", lambda e: e.tensor_scalar(
            out=li_full, in0=li2p_row, scalar1=TWOPI, scalar2=None,
            op0=ALU.mult))
        p.run("vector", lambda e: e.tensor_tensor(
            out=wrr, in0=er, in1=lr_row, op=ALU.mult))
        p.run("vector", lambda e: e.tensor_tensor(
            out=tmp2[0:1, 0:512], in0=ei, in1=li_full, op=ALU.mult))
        p.run("vector", lambda e: e.tensor_tensor(
            out=wrr, in0=wrr, in1=tmp2[0:1, 0:512], op=ALU.add))
        p.run("vector", lambda e: e.tensor_tensor(
            out=wrr, in0=wrr, in1=den, op=ALU.mult))
        p.run("vector", lambda e: e.tensor_tensor(
            out=wir, in0=ei, in1=lr_row, op=ALU.mult))
        p.run("vector", lambda e: e.tensor_tensor(
            out=tmp2[0:1, 0:512], in0=er, in1=li_full, op=ALU.mult))
        p.run("vector", lambda e: e.tensor_tensor(
            out=wir, in0=wir, in1=tmp2[0:1, 0:512], op=ALU.subtract))
        p.run("vector", lambda e: e.tensor_tensor(
            out=wir, in0=wir, in1=den, op=ALU.mult))
        rows_to_cols(p, ct, wrr, wr_c, NNT, PS[6], bounce)
        rows_to_cols(p, ct, wir, wi_c, NNT, PS[6], bounce)
        p.wait("vector", "sync")
        p.runw("vector", lambda e: e.tensor_scalar(
            out=nwi_c[:, :], in0=wi_c[:, :], scalar1=-1.0, scalar2=None,
            op0=ALU.mult))
        p.settle()

    with p.scope():
        crs = p.sbuf("crs", [128, 512], F32)
        cis = p.sbuf("cis", [128, 512], F32)
        v1 = p.sbuf("v1C", [128, 128], F32)
        for dt_ in range(NDT):
            p.wait("sync", "tensor", "vector")
            p.dma("sync", crs[:, :], cr_in[128 * dt_:128 * (dt_ + 1), :])
            p.dma("sync", cis[:, :], ci_in[128 * dt_:128 * (dt_ + 1), :])
            for t in range(NNT):
                p.wait("tensor", "sync", "vector")
                p.run("tensor", lambda e, t=t: e.transpose(
                    out=PS[0][0:128, 0:128], in_=crs[:, 128 * t:128 * (t + 1)],
                    identity=ct["eye_f32"][:, :]))
                p.run("tensor", lambda e, t=t: e.transpose(
                    out=PS[1][0:128, 0:128], in_=cis[:, 128 * t:128 * (t + 1)],
                    identity=ct["eye_f32"][:, :]))
                p.wait("vector", "tensor")
                dsl = slice(128 * dt_, 128 * (dt_ + 1))
                p.run("vector", lambda e, t=t: e.tensor_scalar(
                    out=v1[:, :], in0=PS[0][0:128, 0:128],
                    scalar1=wr_c[:, t:t + 1], scalar2=None, op0=ALU.mult))
                p.run("vector", lambda e, t=t, dsl=dsl: e.scalar_tensor_tensor(
                    out=Cer[t][:, dsl], in0=PS[1][0:128, 0:128],
                    scalar=nwi_c[:, t:t + 1], in1=v1[:, :], op0=ALU.mult,
                    op1=ALU.add))
                p.run("vector", lambda e, t=t: e.tensor_scalar(
                    out=v1[:, :], in0=PS[0][0:128, 0:128],
                    scalar1=wi_c[:, t:t + 1], scalar2=None, op0=ALU.mult))
                p.run("vector", lambda e, t=t, dsl=dsl: e.scalar_tensor_tensor(
                    out=Cei[t][:, dsl], in0=PS[1][0:128, 0:128],
                    scalar=wr_c[:, t:t + 1], in1=v1[:, :], op0=ALU.mult,
                    op1=ALU.add))

    for j in range(NDT):
        p.wait("tensor", "vector")
        for t in range(NNT):
            p.run("tensor", lambda e, t=t, j=j, st=(t == 0):
                  e.matmul(out=PS[2][:, 0:GCH],
                           lhsT=Cer[t][:, 128 * j:128 * (j + 1)],
                           rhs=Pre_bf[t][:, 0:GCH], start=st, stop=False))
            p.run("tensor", lambda e, t=t, j=j, sp=(t == NNT - 1):
                  e.matmul(out=PS[2][:, 0:GCH],
                           lhsT=Cei[t][:, 128 * j:128 * (j + 1)],
                           rhs=nPim_bf[t][:, 0:GCH], start=False, stop=sp))
        p.wait("vector", "tensor")
        p.run("vector", lambda e, j=j: e.tensor_copy(out=Kb[j][:, :],
                                                     in_=PS[2][:, 0:GCH]))
        p.runw("vector", lambda e, j=j: e.tensor_tensor(
            out=c0_c[:, j:j + 1], in0=Kb[j][:, 0:1], in1=pd_c[:, j:j + 1],
            op=ALU.add))
    p.settle()

    with p.scope():
        mt = p.sbuf("mtW", [128, 512], F32)
        ft = p.sbuf("ftW", [128, 512], F32)
        st_ = p.sbuf("stW", [128, 512], F32)
        p.wait("tensor", "vector", "sync")
        p.run("tensor", lambda e: e.matmul(
            out=PS[3][:, 0:512], lhsT=ct["revramp"][:, :], rhs=lr_row,
            start=True, stop=True))
        p.wait("scalar", "tensor")
        p.run("scalar", lambda e: e.activation(out=mt[:, :],
                                               in_=PS[3][:, 0:512],
                                               func=AF.Exp))
        p.wait("tensor", "scalar")
        p.run("tensor", lambda e: e.matmul(
            out=PS[3][:, 0:512], lhsT=ct["revramp"][:, :], rhs=li2p_row,
            start=True, stop=True))
        p.wait("vector", "tensor", "scalar")
        p.run("vector", lambda e: e.tensor_scalar(
            out=ft[:, :], in0=PS[3][:, 0:512], scalar1=MAGIC, scalar2=MAGIC,
            op0=ALU.add, op1=ALU.subtract))
        p.run("vector", lambda e: e.tensor_tensor(
            out=ft[:, :], in0=PS[3][:, 0:512], in1=ft[:, :], op=ALU.subtract))
        p.run("vector", lambda e: e.tensor_scalar(
            out=ft[:, :], in0=ft[:, :], scalar1=TWOPI, scalar2=None,
            op0=ALU.mult))
        p.wait("scalar", "vector")
        p.run("scalar", lambda e: e.activation(out=st_[:, :], in_=ft[:, :],
                                               func=AF.Sin))
        p.wait("vector", "scalar")
        p.run("vector", lambda e: e.tensor_tensor(
            out=WinI[:, :], in0=mt[:, :], in1=st_[:, :], op=ALU.mult))
        p.run("vector", lambda e: e.tensor_scalar(
            out=ft[:, :], in0=PS[3][:, 0:512], scalar1=0.25, scalar2=None,
            op0=ALU.add))
        p.run("vector", lambda e: e.tensor_scalar(
            out=st_[:, :], in0=ft[:, :], scalar1=MAGIC, scalar2=MAGIC,
            op0=ALU.add, op1=ALU.subtract))
        p.run("vector", lambda e: e.tensor_tensor(
            out=ft[:, :], in0=ft[:, :], in1=st_[:, :], op=ALU.subtract))
        p.run("vector", lambda e: e.tensor_scalar(
            out=ft[:, :], in0=ft[:, :], scalar1=TWOPI, scalar2=None,
            op0=ALU.mult))
        p.wait("scalar", "vector")
        p.run("scalar", lambda e: e.activation(out=st_[:, :], in_=ft[:, :],
                                               func=AF.Sin))
        p.wait("vector", "scalar")
        p.run("vector", lambda e: e.tensor_tensor(
            out=WinR[:, :], in0=mt[:, :], in1=st_[:, :], op=ALU.mult))

    p.barrier()

    if dbg_out is not None:
        p.wait("gpsimd", "vector", "scalar", "tensor", "sync")
        p.dma("gpsimd", dbg_out[0:128, 140:144], wr_c[:, :])
        p.dma("gpsimd", dbg_out[0:128, 144:148], wi_c[:, :])
        p.dma("gpsimd", dbg_out[0:128, 284:412], Kb[0][:, :])
        p.wait("gpsimd", "gpsimd")

    p.barrier()
    p.scopes.pop()
    c3s.close()

    with p.scope():
        utc = p.sbuf("utc", [128, D], BF16)
        Psb = [[p.sbuf(f"Psb{t}{ri}", [128, D], F32) for ri in range(2)]
               for t in range(NNT)]
        Sa = [[p.sbuf(f"Sa{t}{ri}", [128, D], F32) for ri in range(2)]
              for t in range(NNT)]
        Sb = [[p.sbuf(f"Sb{t}{ri}", [128, D], F32) for ri in range(2)]
              for t in range(NNT)]
        Mb = [[p.sbuf(f"Mb{t}{c4}", [128, D], BF16) for c4 in range(4)]
              for t in range(NNT)]
        ytb = p.sbuf("ytb", [128, D], BF16)
        yev = p.sbuf("yev", [128, 128], BF16)
        ztile = p.sbuf("ztile", [128, 128], BF16)
        p.run("vector", lambda e: e.memset(ztile[:, :], 0.0))
        for t in range(NNT):
            for ri in range(2):
                p.run("vector", lambda e, t=t, ri=ri:
                      e.memset(Sa[t][ri][:, :], 0.0))
        comp_w = None
        for c in range(NCH):
            S_old = Sa if c % 2 == 0 else Sb
            S_new = Sb if c % 2 == 0 else Sa
            p.barrier()
            p.dma("sync", utc[:, :], dr["ut"][GCH * c:GCH * (c + 1), :])
            for t in range(NNT):
                for ri, W in ((0, WinR), (1, WinI)):
                    p.wait("tensor", "sync", "vector")
                    for hf in range(2):
                        p.run("tensor", lambda e, t=t, W=W, hf=hf:
                              e.matmul(out=PS[4 + hf][:, 0:512],
                                       lhsT=W[:, 128 * t:128 * (t + 1)],
                                       rhs=utc[:, 512 * hf:512 * (hf + 1)],
                                       start=True, stop=True))
                    p.wait("vector", "tensor")
                    for hf in range(2):
                        p.run("vector", lambda e, t=t, ri=ri, hf=hf:
                              e.tensor_copy(
                                  out=Psb[t][ri][:, 512 * hf:512 * (hf + 1)],
                                  in_=PS[4 + hf][:, 0:512]))
            if c > 0:
                for t in range(NNT):
                    p.run("vector", lambda e, t=t, S=S_old: e.tensor_tensor(
                        out=Mb[t][0][:, :], in0=S[t][0][:, :],
                        in1=Cer[t][:, :], op=ALU.mult))
                    p.run("vector", lambda e, t=t, S=S_old: e.tensor_tensor(
                        out=Mb[t][1][:, :], in0=S[t][1][:, :],
                        in1=Cei[t][:, :], op=ALU.mult))
                    p.run("vector", lambda e, t=t, S=S_old: e.tensor_tensor(
                        out=Mb[t][2][:, :], in0=S[t][1][:, :],
                        in1=Cer[t][:, :], op=ALU.mult))
                    p.run("vector", lambda e, t=t, S=S_old: e.tensor_tensor(
                        out=Mb[t][3][:, :], in0=S[t][0][:, :],
                        in1=Cei[t][:, :], op=ALU.mult))
                p.wait("tensor", "vector")
                for t in range(NNT):
                    for c4, Wt in ((0, Pre_bf[t]), (1, nPre_bf[t]),
                                   (2, nPim_bf[t]), (3, nPim_bf[t])):
                        for hf in range(2):
                            p.run("tensor", lambda e, t=t, c4=c4, Wt=Wt,
                                  hf=hf, st=(t == 0 and c4 == 0),
                                  sp=(t == NNT - 1 and c4 == 3):
                                  e.matmul(
                                      out=PS[2 + hf][:, 0:512],
                                      lhsT=Wt[:, 1:129],
                                      rhs=Mb[t][c4][:, 512 * hf:512 * (hf + 1)],
                                      start=st, stop=sp))
                p.wait("vector", "tensor")
                for hf in range(2):
                    p.run("vector", lambda e, hf=hf: e.tensor_copy(
                        out=ytb[:, 512 * hf:512 * (hf + 1)],
                        in_=PS[2 + hf][:, 0:512]))
                for dt_ in range(NDT):
                    b = dt_ % 2
                    p.wait("tensor", "vector", "gpsimd")
                    p.run("tensor", lambda e, dt_=dt_, b=b: e.transpose(
                        out=psb16[:, 128 * b:128 * b + 128],
                        in_=ytb[:, 128 * dt_:128 * (dt_ + 1)],
                        identity=ct["eye_bf"][:, :]))
                    p.wait("vector", "tensor")
                    p.run("vector", lambda e, b=b: e.tensor_copy(
                        out=yev[:, :], in_=psb16[:, 128 * b:128 * b + 128]))
                    p.wait("gpsimd", "vector")
                    p.dma("gpsimd",
                          dr["ys"][128 * dt_:128 * (dt_ + 1),
                                   GCH * c:GCH * (c + 1)], yev[:, :])
            else:
                p.wait("gpsimd", "vector")
                for dt_ in range(NDT):
                    p.dma("gpsimd",
                          dr["ys"][128 * dt_:128 * (dt_ + 1), 0:GCH],
                          ztile[:, :])
            for t in range(NNT):
                p.run("vector", lambda e, t=t, S=S_old, N=S_new:
                      e.scalar_tensor_tensor(
                          out=N[t][0][:, :], in0=S[t][0][:, :],
                          scalar=plam_c[:, 2 * t:2 * t + 1],
                          in1=Psb[t][0][:, :],
                          op0=ALU.mult, op1=ALU.add))
                p.run("vector", lambda e, t=t, S=S_old, N=S_new:
                      e.scalar_tensor_tensor(
                          out=N[t][0][:, :], in0=S[t][1][:, :],
                          scalar=nlam_c[:, 2 * t + 1:2 * t + 2],
                          in1=N[t][0][:, :], op0=ALU.mult, op1=ALU.add))
                p.run("vector", lambda e, t=t, S=S_old, N=S_new:
                      e.scalar_tensor_tensor(
                          out=N[t][1][:, :], in0=S[t][1][:, :],
                          scalar=plam_c[:, 2 * t:2 * t + 1],
                          in1=Psb[t][1][:, :],
                          op0=ALU.mult, op1=ALU.add))
                p.run("vector", lambda e, t=t, S=S_old, N=S_new:
                      e.scalar_tensor_tensor(
                          out=N[t][1][:, :], in0=S[t][0][:, :],
                          scalar=plam_c[:, 2 * t + 1:2 * t + 2],
                          in1=N[t][1][:, :],
                          op0=ALU.mult, op1=ALU.add))

    if dbg_out is not None:
        p.wait("gpsimd", "vector", "scalar", "tensor", "sync")
        p.dma("gpsimd", dbg_out[0:128, 0:128], Kb[0][:, :])
        p.dma("gpsimd", dbg_out[0:128, 128:136], c0_c[:, :])
        p.dma("gpsimd", dbg_out[0:128, 148:156], nlam_c[:, :])
        p.wait("vector", "gpsimd")
        p.runw("vector", lambda e: e.memset(srow[0:1, 8:9], 42.0))
        p.wait("gpsimd", "vector")
        p.dma("gpsimd", dbg_out[0:1, 5:6], srow[0:1, 8:9])

    with p.scope():
        accv = p.sbuf("accv", [128, T], F32)
        accg = p.sbuf("accg", [128, T], F32)
        uk = p.sbuf("ukC", [128, T], BF16)
        x4k = p.sbuf("x4k", [128, T], F32)
        ysk = p.sbuf("ysk", [128, T], BF16)
        for k in range(NDT):
            p.barrier()
            p.dma("sync", uk[:, :], dr["u"][128 * k:128 * (k + 1), :])
            p.dma("sync", x4k[:, :], dr["x4"][128 * k:128 * (k + 1), :])
            p.dma("sync", ysk[:, :], dr["ys"][128 * k:128 * (k + 1), :])
            p.wait("vector", "sync")
            p.wait("gpsimd", "sync")
            p.run("vector", lambda e, k=k: e.scalar_tensor_tensor(
                out=accv[:, :], in0=uk[:, :], scalar=c0_c[:, k:k + 1],
                in1=x4k[:, :], op0=ALU.mult, op1=ALU.add))
            p.run("vector", lambda e: e.tensor_tensor(
                out=accv[:, :], in0=accv[:, :], in1=ysk[:, :], op=ALU.add))

            useg = uk[:, :].rearrange("p (c q) -> p c q", q=GCH)
            av = accv[:, :].rearrange("p (c q) -> p c q", q=GCH)
            ag = accg[:, :].rearrange("p (c q) -> p c q", q=GCH)
            for l in range(1, GCH):
                w_ = GCH - l
                eng = "vector"
                acc_ = av if eng == "vector" else ag
                p.run(eng, lambda e, l=l, w_=w_, acc_=acc_, k=k:
                      e.scalar_tensor_tensor(
                          out=acc_[:, :, l:GCH], in0=useg[:, :, 0:w_],
                          scalar=Kb[k][:, l:l + 1], in1=acc_[:, :, l:GCH],
                          op0=ALU.mult, op1=ALU.add))
            p.wait("gpsimd", "vector")
            p.dma("gpsimd", x5_out[128 * k:128 * (k + 1), :], accv[:, :])

    with p.scope():
        sb = ln_scratch(p)
        emit_ln(p, ct, x5_out, dr["junk"], sb, PS, dst_f32_dram=xln2_out,
                rmax=rmax)
        p.wait("tensor", "vector", "sync")
        p.run("tensor", lambda e: e.transpose(
            out=PS[0][0:1, 0:128], in_=rmax[:, 0:1],
            identity=ct["eye_f32"][:, :]))
        p.wait("vector", "tensor")
        p.runw("vector", lambda e: e.reduce_max(
            out=srow[0:1, 4:5], in_=PS[0][0:1, 0:128], axis=AX.X))
        p.wait("gpsimd", "vector")
        p.dma("gpsimd", mx_out[0:1, 0:1], srow[0:1, 4:5])
    p.finalize()
    return nc

_CACHE = {}


def _inputs_A(inputs, b):
    x = inputs["L0_attn_hidden_states"]
    m = {"x_in": np.ascontiguousarray(x[b].T)}
    m.update(const_inputs())
    m["gate_w"] = np.ascontiguousarray(inputs["L0_ffn_gate_w"])
    m["Wu"] = np.ascontiguousarray(inputs["L0_ffn_Wu"])
    m["Wv"] = np.ascontiguousarray(inputs["L0_ffn_Wv"])
    m["Wo"] = np.ascontiguousarray(inputs["L0_ffn_Wo"])
    for nm in ["W_xz", "W_dt", "b_dt", "A_log", "W_B", "W_C", "D_skip",
               "W_out", "v_th"]:
        m[nm] = np.ascontiguousarray(inputs["L0_attn_" + nm])
    return m


def kernel(**inputs):
    n = 8
    cores = list(range(n))
    if "A" not in _CACHE:
        _CACHE["A"] = build_launch_A()
        _CACHE["B"] = build_launch_B()
        _CACHE["C"] = build_launch_C()
        _CACHE["E"] = build_launch_E()
    ci = const_inputs()
    w1 = np.ascontiguousarray(inputs["L1_ffn_w1"])
    w1t = np.ascontiguousarray(w1.T)
    b1 = np.ascontiguousarray(inputs["L1_ffn_b1"])
    w2 = np.ascontiguousarray(inputs["L1_ffn_w2"])
    w2t = np.ascontiguousarray(w2.T)
    b2 = np.ascontiguousarray(inputs["L1_ffn_b2"])

    resA = run_bass_kernel_spmd(_CACHE["A"],
                                [_inputs_A(inputs, b) for b in range(n)],
                                core_ids=cores)
    m1 = np.stack([resA.results[b]["mx"][0] for b in range(n)])

    resB = run_bass_kernel_spmd(_CACHE["B"], [
        {"xln": resA.results[b]["xln1"], "w1": w1, "w1t": w1t, "b1": b1,
         "gmax": m1, **ci} for b in range(n)], core_ids=cores)
    m2 = np.stack([resB.results[b]["mx"][0] for b in range(n)])

    resC = run_bass_kernel_spmd(_CACHE["C"], [
        {"x": resA.results[b]["x3"], "h": resB.results[b]["h"],
         "w2": w2, "w2t": w2t, "b2": b2, "gmax": m2,
         "Lr": inputs["L1_attn_Lambda_real"],
         "Li": inputs["L1_attn_Lambda_imag"],
         "Cr": inputs["L1_attn_C_real"], "Ci": inputs["L1_attn_C_imag"],
         "pD": inputs["L1_attn_param_D"], "lg": inputs["L1_attn_ln_gamma"],
         "lb": inputs["L1_attn_ln_beta"], **ci} for b in range(n)],
        core_ids=cores)
    m3 = np.stack([resC.results[b]["mx"][0] for b in range(n)])

    resD = run_bass_kernel_spmd(_CACHE["B"], [
        {"xln": resC.results[b]["xln2"], "w1": w1, "w1t": w1t, "b1": b1,
         "gmax": m3, **ci} for b in range(n)], core_ids=cores)
    m4 = np.stack([resD.results[b]["mx"][0] for b in range(n)])

    resE = run_bass_kernel_spmd(_CACHE["E"], [
        {"x": resC.results[b]["x5"], "h": resD.results[b]["h"],
         "w2": w2, "w2t": w2t, "b2": b2, "gmax": m4, **ci}
        for b in range(n)], core_ids=cores)
    return np.stack([resE.results[b]["y"].T for b in range(n)])


# version 03 (final)
# speedup vs baseline: 1.3768x; 1.0242x over previous
import contextlib
import numpy as np
import ml_dtypes

import concourse.bass as bass
import concourse.mybir as mybir
from concourse.bass_utils import run_bass_kernel_spmd

F32 = mybir.dt.float32
BF16 = mybir.dt.bfloat16
AF = mybir.ActivationFunctionType
ALU = mybir.AluOpType
AX = mybir.AxisListType

MAGIC = float(np.float32(1.5 * 2.0 ** 23))
D = 1024
T = 4096
NDT = 8
NCC = T // 512
LN_EPS = 1e-5
K_STEEP = 10.0
V_TH_MIN = 0.1
QEPS = 1e-8

ENGS = ("sync", "tensor", "scalar", "vector", "gpsimd")


class SemC:
    __slots__ = ("h", "v")

    def __init__(self, h):
        self.h = h
        self.v = 0


class Prog:
    def __init__(self, nc):
        self.nc = nc
        self.es = contextlib.ExitStack()
        self.es.enter_context(nc.cleanup_on_exit())
        self.scopes = [self.es]
        self.ops = {e: [] for e in ENGS}
        self._fence = {"sync": 0, "gpsimd": 0}
        self.s = {e: SemC(self.es.enter_context(nc.semaphore(f"sem_{e}")))
                  for e in ENGS}
        self._setl = self.es.enter_context(
            nc.sbuf_tensor("settle_buf", [128, 512], F32))

    def sbuf(self, name, shape, dt=F32):
        self._uid = getattr(self, "_uid", 0) + 1
        return self.scopes[-1].enter_context(
            self.nc.sbuf_tensor(f"{name}_{self._uid}", shape, dt))

    def psum(self, name, shape, dt=F32):
        self._uid = getattr(self, "_uid", 0) + 1
        return self.scopes[-1].enter_context(
            self.nc.psum_tensor(f"{name}_{self._uid}", shape, dt))

    @contextlib.contextmanager
    def scope(self):
        st = contextlib.ExitStack()
        self.scopes.append(st)
        try:
            yield
        finally:
            self.barrier()
            self.scopes.pop()
            st.close()

    def on(self, eng, fn):
        self.ops[eng].append(fn)

    def wait(self, eng, *prods):
        for pe in prods:
            sc = self.s[pe]
            if sc.v > 0:
                self.on(eng, lambda e, h=sc.h, t=sc.v: e.wait_ge(h, t))
                if pe in ("sync", "gpsimd"):
                    self._fence[pe] = sc.v

    _fence = None

    def dma(self, eng, out, in_, slow=False):
        sc = self.s[eng]
        fv = self._fence.get(eng)
        if fv:
            self.on(eng, lambda e, h=sc.h, t=fv: e.wait_ge(h, t))
            self._fence[eng] = 0
        sc.v += 16
        if slow:
            self.on(eng, lambda e, o=out, i=in_, h=sc.h:
                    e.dma_start(out=o, in_=i,
                                allow_slow_non_contiguous=True).then_inc(h, 16))
        else:
            self.on(eng, lambda e, o=out, i=in_, h=sc.h:
                    e.dma_start(out=o, in_=i).then_inc(h, 16))

    def run(self, eng, fn):
        sc = self.s[eng]
        if eng in ("vector", "scalar", "gpsimd") and sc.v > 0:
            self.on(eng, lambda e, h=sc.h, t=sc.v: e.wait_ge(h, t))
        sc.v += 1
        self.on(eng, lambda e, f=fn, h=sc.h: f(e).then_inc(h, 1))

    def wait_at(self, eng, pe, val):
        if val > 0:
            sc = self.s[pe]
            self.on(eng, lambda e, h=sc.h, t=val: e.wait_ge(h, t))

    def runf(self, eng, fn):
        sc = self.s[eng]
        sc.v += 1
        self.on(eng, lambda e, f=fn, h=sc.h: f(e).then_inc(h, 1))

    def runw(self, eng, fn):
        sc = self.s[eng]
        if sc.v > 0:
            self.on(eng, lambda e, h=sc.h, t=sc.v: e.wait_ge(h, t))
        sc.v += 1
        self.on(eng, lambda e, f=fn, h=sc.h: f(e).then_inc(h, 1))

    def settle(self):
        sc = self.s["vector"]
        if sc.v > 0:
            self.on("vector", lambda e, h=sc.h, t=sc.v: e.wait_ge(h, t))
        sc.v += 1
        self.on("vector", lambda e, h=sc.h, t=self._setl:
                e.memset(t[:, :], 0.0).then_inc(h, 1))

    def sbuf_root(self, name, shape, dt=F32):
        self._uid = getattr(self, "_uid", 0) + 1
        return self.es.enter_context(
            self.nc.sbuf_tensor(f"{name}_{self._uid}", shape, dt))

    def barrier(self):
        for eng in ENGS:
            self.wait(eng, *[o for o in ENGS if o != eng])

    def finalize(self, final_eng="gpsimd"):
        self.wait(final_eng, *[o for o in ENGS if o != final_eng])
        sc = self.s[final_eng]
        if sc.v > 0:
            self.on(final_eng, lambda e, h=sc.h, t=sc.v: e.wait_ge(h, t))
        with self.nc.Block() as blk:
            for name in ENGS:
                fns = self.ops[name]
                if not fns:
                    continue

                def runner(e, fns=fns):
                    for f in fns:
                        f(e)
                getattr(blk, name)(runner)
        self.es.close()


CONST = {}


def _consts():
    if not CONST:
        CONST["ones_col_bf"] = np.ones((128, 1), ml_dtypes.bfloat16)
        CONST["ones_row"] = np.ones((1, 128), np.float32)
        CONST["eye_bf"] = np.eye(128, dtype=ml_dtypes.bfloat16)
        CONST["eye_f32"] = np.eye(128, dtype=np.float32)
        CONST["ramp"] = np.arange(512, dtype=np.float32)[None, :]
        CONST["revramp"] = np.arange(127, -1, -1, dtype=np.float32)[None, :]
        oh16 = np.zeros((16, 16 * 128), ml_dtypes.bfloat16)
        for n in range(16):
            oh16[n, 128 * n:128 * (n + 1)] = 1.0
        CONST["oh16"] = oh16
        oh8 = np.zeros((8, 8 * 128), ml_dtypes.bfloat16)
        for n in range(8):
            oh8[n, 128 * n:128 * (n + 1)] = 1.0
        CONST["oh8"] = oh8
        oh32f = np.zeros((32, 32 * 128), np.float32)
        for n in range(32):
            oh32f[n, 128 * n:128 * (n + 1)] = 1.0
        CONST["oh32f"] = oh32f
        CONST["oh8f"] = oh8.astype(np.float32)
    return CONST


def decl_consts(nc):
    out = {}
    for k, v in _consts().items():
        dt = BF16 if v.dtype == ml_dtypes.bfloat16 else F32
        out[k] = nc.dram_tensor("c_" + k, list(v.shape), dt,
                                kind="ExternalInput").ap()
    return out


def const_inputs():
    return {"c_" + k: np.ascontiguousarray(v) for k, v in _consts().items()}


CT_SHAPES = {"ones_col_bf": ([128, 1], BF16), "ones_row": ([1, 128], F32),
             "eye_bf": ([128, 128], BF16), "eye_f32": ([128, 128], F32),
             "ramp": ([1, 512], F32), "revramp": ([1, 128], F32),
             "oh16": ([16, 2048], BF16), "oh8": ([8, 1024], BF16),
             "oh32f": ([32, 4096], F32), "oh8f": ([8, 1024], F32)}


def load_const_tiles(p, cd, names=("ones_col_bf", "ones_row", "eye_bf",
                                   "eye_f32", "ramp", "revramp", "oh16",
                                   "oh8")):
    t = {}
    for k in names:
        sh, dt = CT_SHAPES[k]
        t[k] = p.sbuf("ct_" + k, sh, dt)
        p.dma("sync", t[k][:, :], cd[k][:, :])
    return t


def load_w_bf(p, name, dram, kdim, mdim, stage):
    nk = -(-kdim // 128)
    tiles = []
    for i in range(nk):
        rows = min(128, kdim - i * 128)
        bft = p.sbuf(f"{name}{i}", [128, mdim], BF16)
        p.wait("sync", "vector")
        p.dma("sync", stage[0:rows, 0:mdim], dram[i * 128:i * 128 + rows, :])
        p.wait("vector", "sync")
        p.run("vector", lambda e, a=bft, b=stage, r=rows, m=mdim:
              e.tensor_copy(out=a[0:r, 0:m], in_=b[0:r, 0:m]))
        tiles.append(bft)
    return tiles


def load_vec_cols(p, name, dram, n, ct=None, ps=None):
    nj = n // 128
    t = p.sbuf(name, [128, nj], F32)
    tr = p.sbuf(name + "r", [nj, 128], F32)
    p.dma("sync", tr[0:nj, 0:128], dram.rearrange("(j q) -> j q", q=128))
    p.wait("tensor", "sync", "vector", "scalar")
    p.run("tensor", lambda e: e.transpose(
        out=ps[0:128, 0:nj], in_=tr[0:nj, 0:128],
        identity=ct["eye_f32"][0:nj, 0:nj]))
    p.wait("vector", "tensor")
    p.run("vector", lambda e: e.tensor_copy(out=t[:, :], in_=ps[0:128, 0:nj]))
    p.settle()
    return t


def emit_ln(p, ct, src_dram, dst_bf_dram, sb, PS, wfull_sb=None,
            gate_tiles=None, dst_f32_dram=None, rmax=None, gb=None,
            ut_dram=None, psb=None):
    xa, xb, xq, o32 = sb["fa"], sb["fb"], sb["fq"], sb["fo"]
    srow, tmax = sb["srow"], sb["tmax"]
    ps_stat, ps_arep, ps_brep, ps_g = PS[3], PS[4], PS[5], PS[6]

    for cc in range(NCC):
        csl = slice(cc * 512, (cc + 1) * 512)
        p.barrier()
        for k in range(NDT):
            p.dma("sync", xa[k][:, :], src_dram[128 * k:128 * (k + 1), csl])
        p.wait("vector", "sync")
        p.wait("scalar", "sync")
        for k in range(NDT):
            p.run("vector", lambda e, a=xb[k], b=xa[k]:
                  e.tensor_copy(out=a[:, :], in_=b[:, :]))
            p.run("scalar", lambda e, a=xq[k], b=xa[k]:
                  e.activation(out=a[:, :], in_=b[:, :], func=AF.Square))
        p.wait("tensor", "vector", "scalar")
        for k in range(NDT):
            p.run("tensor", lambda e, r=xb[k], w=ct["ones_col_bf"],
                  st=(k == 0), sp=(k == NDT - 1):
                  e.matmul(out=ps_stat[0:1, 0:512], lhsT=w[:, :], rhs=r[:, :],
                           start=st, stop=sp))
        for k in range(NDT):
            p.run("tensor", lambda e, r=xq[k], w=ct["ones_col_bf"],
                  st=(k == 0), sp=(k == NDT - 1):
                  e.matmul(out=ps_stat[32:33, 0:512], lhsT=w[:, :], rhs=r[:, :],
                           start=st, stop=sp))
        m_row = srow[0:1, 0:512]
        v_row = srow[0:1, 512:1024]
        a_row = srow[0:1, 1024:1536]
        b_row = srow[0:1, 1536:2048]
        p.wait("vector", "tensor")
        p.run("vector", lambda e: e.tensor_scalar(
            out=m_row, in0=ps_stat[0:1, 0:512], scalar1=1.0 / D, scalar2=None,
            op0=ALU.mult))
        p.run("vector", lambda e: e.tensor_tensor(
            out=v_row, in0=m_row, in1=m_row, op=ALU.mult))
        p.run("vector", lambda e: e.scalar_tensor_tensor(
            out=v_row, in0=ps_stat[32:33, 0:512], scalar=1.0 / D, in1=v_row,
            op0=ALU.mult, op1=ALU.subtract))
        p.run("vector", lambda e: e.tensor_scalar(
            out=v_row, in0=v_row, scalar1=float(LN_EPS), scalar2=None,
            op0=ALU.add))
        p.wait("scalar", "vector")
        p.run("scalar", lambda e: e.activation(
            out=v_row, in_=v_row, func=AF.Sqrt))
        p.wait("vector", "scalar")
        p.run("vector", lambda e: e.reciprocal(out=a_row, in_=v_row))
        p.run("vector", lambda e: e.scalar_tensor_tensor(
            out=b_row, in0=m_row, scalar=-1.0, in1=a_row,
            op0=ALU.mult, op1=ALU.mult))
        p.wait("tensor", "vector")
        p.run("tensor", lambda e: e.matmul(
            out=ps_arep[:, 0:512], lhsT=ct["ones_row"][:, :], rhs=a_row,
            start=True, stop=True))
        p.run("tensor", lambda e: e.matmul(
            out=ps_brep[:, 0:512], lhsT=ct["ones_row"][:, :], rhs=b_row,
            start=True, stop=True))
        p.wait("vector", "tensor", "gpsimd")
        for k in range(NDT):
            p.run("vector", lambda e, xk=xa[k], o=o32[k]:
                  e.tensor_tensor(out=o[:, :], in0=xk[:, :],
                                  in1=ps_arep[:, 0:512], op=ALU.mult))
            p.run("vector", lambda e, o=o32[k]:
                  e.tensor_tensor(out=o[:, :], in0=o[:, :],
                                  in1=ps_brep[:, 0:512], op=ALU.add))
        if gb is not None:
            g_c, b_c = gb
            for k in range(NDT):
                p.run("vector", lambda e, o=o32[k], k=k: e.tensor_scalar(
                    out=o[:, :], in0=o[:, :], scalar1=g_c[:, k:k + 1],
                    scalar2=b_c[:, k:k + 1], op0=ALU.mult, op1=ALU.add))
        p.wait("scalar", "vector")
        for k in range(NDT):
            p.run("scalar", lambda e, o=o32[k], b=xb[k]:
                  e.activation(out=b[:, :], in_=o[:, :], func=AF.Copy))
        if ut_dram is not None:
            for k in range(NDT):
                for s4 in range(4):
                    b = (k * 4 + s4) % 2
                    p.wait("tensor", "scalar", "vector", "gpsimd")
                    p.run("tensor", lambda e, k=k, s4=s4, b=b: e.transpose(
                        out=psb[:, 128 * b:128 * b + 128],
                        in_=xb[k][:, 128 * s4:128 * (s4 + 1)],
                        identity=ct["eye_bf"][:, :]))
                    p.wait("vector", "tensor")
                    p.run("vector", lambda e, b=b: e.tensor_copy(
                        out=sb["utt"][:, 0:128],
                        in_=psb[:, 128 * b:128 * b + 128]))
                    p.wait("gpsimd", "vector")
                    p.dma("gpsimd",
                          ut_dram[cc * 512 + 128 * s4:cc * 512 + 128 * s4 + 128,
                                  128 * k:128 * (k + 1)],
                          sb["utt"][:, 0:128])
        p.wait("gpsimd", "vector", "scalar")
        for k in range(NDT):
            p.dma("gpsimd", dst_bf_dram[128 * k:128 * (k + 1), csl], xb[k][:, :])
            if dst_f32_dram is not None:
                p.dma("gpsimd", dst_f32_dram[128 * k:128 * (k + 1), csl],
                      o32[k][:, :])
        if rmax is not None:
            for k in range(NDT):
                p.runw("vector", lambda e, o=o32[k]:
                       e.reduce_max(out=tmax[:, 0:1], in_=o[:, :], axis=AX.X))
                if cc == 0 and k == 0:
                    p.runw("vector", lambda e:
                           e.tensor_copy(out=rmax[:, 0:1], in_=tmax[:, 0:1]))
                else:
                    p.runw("vector", lambda e:
                           e.tensor_tensor(out=rmax[:, 0:1], in0=rmax[:, 0:1],
                                           in1=tmax[:, 0:1], op=ALU.max))
        if wfull_sb is not None:
            gl, gt, gm, gw, gs = sb["gl"], sb["gt"], sb["gm"], sb["gw"], sb["gs"]
            p.wait("tensor", "vector")
            for k in range(NDT):
                p.run("tensor", lambda e, w=gate_tiles[k], r=o32[k],
                      st=(k == 0), sp=(k == NDT - 1):
                      e.matmul(out=ps_g[0:8, 0:512], lhsT=w[:, 0:8],
                               rhs=r[:, :], start=st, stop=sp))
            p.wait("vector", "tensor")
            p.run("vector", lambda e: e.tensor_copy(out=gl[:, :],
                                                    in_=ps_g[0:8, 0:512]))
            p.wait("tensor", "vector")
            for j in range(4):
                p.run("tensor", lambda e, j=j: e.transpose(
                    out=ps_g[0:128, 8 * j:8 * j + 8],
                    in_=gl[0:8, 128 * j:128 * (j + 1)],
                    identity=ct["eye_f32"][0:8, 0:8]))
            p.wait("vector", "tensor")
            p.run("vector", lambda e: e.tensor_copy(out=gt[:, :],
                                                    in_=ps_g[0:128, 0:32]))
            for j in range(4):
                jsl = slice(8 * j, 8 * j + 8)
                p.runw("vector", lambda e, jsl=jsl: e.max(
                    out=gm[:, jsl], in_=gt[:, jsl]))
            for j in range(4):
                m1 = gm[:, 8 * j + 0:8 * j + 1]
                m2 = gm[:, 8 * j + 1:8 * j + 2]
                d12, w1, w2 = gs[:, 0:1], gs[:, 1:2], gs[:, 2:3]
                t8 = gs[:, 8:16]
                jsl = slice(8 * j, 8 * j + 8)
                p.runw("vector", lambda e, m1=m1, m2=m2, d12=d12:
                       e.tensor_tensor(out=d12, in0=m1, in1=m2,
                                       op=ALU.subtract))
                p.wait("scalar", "vector")
                p.runw("scalar", lambda e, d12=d12, w1=w1:
                       e.activation(out=w1, in_=d12, func=AF.Sigmoid))
                p.wait("vector", "scalar")
                p.runw("vector", lambda e, w1=w1, w2=w2: e.tensor_scalar(
                    out=w2, in0=w1, scalar1=-1.0, scalar2=1.0,
                    op0=ALU.mult, op1=ALU.add))
                p.runw("vector", lambda e, jsl=jsl, m1=m1, w1=w1:
                       e.tensor_scalar(out=gw[:, jsl], in0=gt[:, jsl],
                                       scalar1=m1, scalar2=w1,
                                       op0=ALU.is_equal, op1=ALU.mult))
                p.runw("vector", lambda e, jsl=jsl, m2=m2, w2=w2, t8=t8:
                       e.tensor_scalar(out=t8, in0=gt[:, jsl],
                                       scalar1=m2, scalar2=w2,
                                       op0=ALU.is_equal, op1=ALU.mult))
                p.runw("vector", lambda e, jsl=jsl, t8=t8:
                       e.tensor_tensor(out=gw[:, jsl], in0=gw[:, jsl],
                                       in1=t8, op=ALU.add))
            p.wait("tensor", "vector")
            for j in range(4):
                p.run("tensor", lambda e, j=j: e.transpose(
                    out=ps_g[0:8, 128 * j:128 * (j + 1)],
                    in_=gw[:, 8 * j:8 * j + 8],
                    identity=ct["eye_f32"][:, :]))
            p.wait("vector", "tensor")
            p.run("vector", lambda e, csl=csl: e.tensor_copy(
                out=wfull_sb[0:8, csl], in_=ps_g[0:8, 0:512]))


def emit_uv(p, ct, xbf_dram, wu_t, wv_t, wfull_sb, sw_dram, sb, PS):
    xcc, su, sw = sb["xbcc"], sb["su"], sb["sw"]
    su2 = [su, sb["su1"]]
    sw2 = [sw, sb["sw1"]]
    for cc in range(NCC):
        csl = slice(cc * 512, (cc + 1) * 512)
        p.barrier()
        for k in range(NDT):
            p.dma("sync", xcc[k][:, :], xbf_dram[128 * k:128 * (k + 1), csl])
        p.wait("tensor", "sync")
        silu_done = {}
        evac_done = {}
        dma_done = {}
        for h in range(16):
            e_idx = h // 2
            q = h % 2
            ps_u, ps_v, ps_w = PS[3 * q], PS[3 * q + 1], PS[3 * q + 2]
            suh, swh = su2[q], sw2[q]
            p.wait_at("tensor", "scalar", silu_done.get(h - 2, 0))
            p.wait_at("tensor", "vector", evac_done.get(h - 2, 0))
            for k in range(NDT):
                p.run("tensor", lambda e, w=wu_t[h][k], r=xcc[k], ps_u=ps_u,
                      st=(k == 0), sp=(k == NDT - 1):
                      e.matmul(out=ps_u[:, 0:512], lhsT=w[:, :], rhs=r[:, :],
                               start=st, stop=sp))
            for k in range(NDT):
                p.run("tensor", lambda e, w=wv_t[h][k], r=xcc[k], ps_v=ps_v,
                      st=(k == 0), sp=(k == NDT - 1):
                      e.matmul(out=ps_v[:, 0:512], lhsT=w[:, :], rhs=r[:, :],
                               start=st, stop=sp))
            p.run("tensor", lambda e, ei=e_idx, csl=csl, ps_w=ps_w: e.matmul(
                out=ps_w[:, 0:512], lhsT=ct["oh8"][:, 128 * ei:128 * (ei + 1)],
                rhs=wfull_sb[0:8, csl], start=True, stop=True))
            mm_done = p.s["tensor"].v
            p.wait_at("scalar", "tensor", mm_done)
            p.wait_at("scalar", "vector", evac_done.get(h - 2, 0))
            p.run("scalar", lambda e, ps_u=ps_u, suh=suh: e.activation(
                out=suh[:, :], in_=ps_u[:, 0:512], func=AF.Silu))
            silu_done[h] = p.s["scalar"].v
            p.wait_at("vector", "scalar", silu_done[h])
            p.wait_at("vector", "gpsimd", dma_done.get(h - 2, 0))
            p.run("vector", lambda e, ps_v=ps_v, suh=suh: e.tensor_tensor(
                out=suh[:, :], in0=suh[:, :], in1=ps_v[:, 0:512], op=ALU.mult))
            p.run("vector", lambda e, ps_w=ps_w, suh=suh, swh=swh:
                  e.tensor_tensor(
                      out=swh[:, :], in0=suh[:, :], in1=ps_w[:, 0:512],
                      op=ALU.mult))
            evac_done[h] = p.s["vector"].v
            p.wait("gpsimd", "vector")
            p.dma("gpsimd", sw_dram[128 * h:128 * (h + 1), csl], swh[:, :])
            dma_done[h] = p.s["gpsimd"].v


def emit_matvec_out(p, ct, h_dram, n_h, w_t, x_dram, out_dram, sb, PS,
                    scale=0.5, sub_dram=None):
    hcc = sb["hcc"]
    xcc1 = sb["fa"]
    occ = sb["fo"]
    for cc in range(NCC):
        csl = slice(cc * 512, (cc + 1) * 512)
        p.barrier()
        for h in range(n_h):
            p.dma("sync", hcc[h][:, :], h_dram[128 * h:128 * (h + 1), csl])
        for k in range(NDT):
            p.dma("sync", xcc1[k][:, :], x_dram[128 * k:128 * (k + 1), csl])
        p.wait("tensor", "sync")
        evac_done = {}
        for j in range(NDT):
            pj = PS[j % 7]
            p.wait_at("tensor", "vector", evac_done.get(j - 6, 0))
            for h in range(n_h):
                p.run("tensor", lambda e, w=w_t[j][h], r=hcc[h], pj=pj,
                      st=(h == 0), sp=(h == n_h - 1):
                      e.matmul(out=pj[:, 0:512], lhsT=w, rhs=r[:, :],
                               start=st, stop=sp))
            p.wait("vector", "tensor", "sync")
            p.run("vector", lambda e, j=j, pj=pj: e.scalar_tensor_tensor(
                out=occ[j][:, :], in0=pj[:, 0:512], scalar=float(scale),
                in1=xcc1[j][:, :], op0=ALU.mult, op1=ALU.add))
            evac_done[j] = p.s["vector"].v
        if sub_dram is not None:
            p.wait("sync", "vector")
            for j in range(NDT):
                p.dma("sync", xcc1[j][:, :],
                      sub_dram[128 * j:128 * (j + 1), csl])
            p.wait("vector", "sync")
            for j in range(NDT):
                p.run("vector", lambda e, j=j: e.tensor_tensor(
                    out=occ[j][:, :], in0=occ[j][:, :], in1=xcc1[j][:, :],
                    op=ALU.subtract))
        p.wait("gpsimd", "vector")
        for j in range(NDT):
            p.dma("gpsimd", out_dram[128 * j:128 * (j + 1), csl], occ[j][:, :])


def emit_linear(p, ct, xbf_dram, w_t, n_out, sb, PS, evac):
    xcc = sb["xbcc"]
    nps = 4
    for cc in range(NCC):
        p.barrier()
        csl = slice(cc * 512, (cc + 1) * 512)
        for k in range(NDT):
            p.dma("sync", xcc[k][:, :], xbf_dram[128 * k:128 * (k + 1), csl])
        p.wait("tensor", "sync")
        for j in range(n_out):
            pj = PS[j % nps]
            if j >= nps:
                p.wait("tensor", "vector", "scalar")
            for k in range(NDT):
                p.run("tensor", lambda e, w=w_t[j][k], r=xcc[k], pj=pj,
                      st=(k == 0), sp=(k == NDT - 1):
                      e.matmul(out=pj[:, 0:512], lhsT=w, rhs=r[:, :],
                               start=st, stop=sp))
            evac(p, cc, j, pj)



def ln_scratch(p):
    sb = {}
    sb["fa"] = [p.sbuf(f"fa{k}", [128, 512], F32) for k in range(NDT)]
    sb["fb"] = [p.sbuf(f"fb{k}", [128, 512], BF16) for k in range(NDT)]
    sb["fq"] = [p.sbuf(f"fq{k}", [128, 512], BF16) for k in range(NDT)]
    sb["fo"] = [p.sbuf(f"fo{k}", [128, 512], F32) for k in range(NDT)]
    sb["srow"] = p.sbuf("srow", [1, 2048], F32)
    sb["tmax"] = p.sbuf("tmax", [128, 1], F32)
    return sb


def gate_scratch(p, sb):
    sb["gl"] = p.sbuf("gl", [8, 512], F32)
    sb["gt"] = p.sbuf("gt", [128, 32], F32)
    sb["gm"] = p.sbuf("gm", [128, 32], F32)
    sb["gw"] = p.sbuf("gw", [128, 32], F32)
    sb["gs"] = p.sbuf("gs", [128, 16], F32)
    return sb


def uv_scratch(p):
    sb = {}
    sb["xbcc"] = [p.sbuf(f"xbcc{k}", [128, 512], BF16) for k in range(NDT)]
    sb["su"] = p.sbuf("su", [128, 512], BF16)
    sb["sw"] = p.sbuf("sw", [128, 512], BF16)
    sb["su1"] = p.sbuf("su1", [128, 512], BF16)
    sb["sw1"] = p.sbuf("sw1", [128, 512], BF16)
    return sb


def mv_scratch(p, n_h):
    sb = {}
    sb["hcc"] = [p.sbuf(f"hcc{h}", [128, 512], BF16) for h in range(n_h)]
    sb["fa"] = [p.sbuf(f"va{k}", [128, 512], F32) for k in range(NDT)]
    sb["fo"] = [p.sbuf(f"vo{k}", [128, 512], F32) for k in range(NDT)]
    return sb


def build_launch_A(upto=99, dbg=()):
    nc = bass.Bass("TRN2", target_bir_lowering=False)
    cd = decl_consts(nc)
    x_in = nc.dram_tensor("x_in", [D, T], F32, kind="ExternalInput").ap()
    w = {}
    for nm, sh in [("gate_w", [D, 8]), ("Wu", [8, D, 256]), ("Wv", [8, D, 256]),
                   ("Wo", [8, 256, D]), ("W_xz", [D, 2 * D]), ("W_dt", [D, D]),
                   ("b_dt", [D]), ("A_log", [D, 16]), ("W_B", [D, 16]),
                   ("W_C", [D, 16]), ("D_skip", [D]), ("W_out", [D, D]),
                   ("v_th", [D])]:
        w[nm] = nc.dram_tensor(nm, sh, F32, kind="ExternalInput").ap()
    x3_out = nc.dram_tensor("x3", [D, T], F32, kind="ExternalOutput").ap()
    xln1_out = nc.dram_tensor("xln1", [D, T], F32, kind="ExternalOutput").ap()
    mx_out = nc.dram_tensor("mx", [1, 1], F32, kind="ExternalOutput").ap()

    dr = {}
    for nm, sh, dt in [("xln0", [D, T], BF16), ("sw", [2 * D, T], BF16),
                       ("x1", [D, T], F32), ("xlnm", [D, T], BF16),
                       ("xlnm32", [D, T], F32), ("xssm", [D, T], BF16),
                       ("z", [D, T], BF16), ("dt", [D, T], F32),
                       ("dtx", [D, T], BF16), ("yg", [D, T], BF16),
                       ("x2", [D, T], F32), ("xlnjunk", [D, T], BF16),
                       ("wf", [8, T], BF16)]:
        kind = "ExternalOutput" if nm in dbg else "Internal"
        dr[nm] = nc.dram_tensor("i_" + nm, sh, dt, kind=kind).ap()

    p = Prog(nc)
    ct = load_const_tiles(p, cd)

    psA = p.psum("psA", [128, 3584], F32)
    PS = [psA[:, 512 * i:512 * (i + 1)] for i in range(7)]
    psb16 = p.psum("psAb", [128, 1024], BF16)

    gate_f = []
    for i in range(NDT):
        gt_ = p.sbuf(f"gwf{i}", [128, 8], F32)
        p.dma("sync", gt_[:, :], w["gate_w"][128 * i:128 * (i + 1), :])
        gate_f.append(gt_)
    rmax = p.sbuf("rmax", [128, 1], F32)
    mrow = p.sbuf("mrow", [1, 8], F32)

    def ffn_block(x_dram, sw_dram, out_dram, pfix):
        with p.scope():
            wfull = p.sbuf("wfull", [8, T], BF16)
            with p.scope():
                sb = gate_scratch(p, ln_scratch(p))
                emit_ln(p, ct, x_dram, dr["xln0"], sb, PS, wfull_sb=wfull,
                        gate_tiles=gate_f)
            p.wait("gpsimd", "vector")
            p.dma("gpsimd", dr["wf"][0:8, :], wfull[0:8, :])
            with p.scope():
                stage = p.sbuf("wstage", [128, 2048], F32)
                sb = uv_scratch(p)
                wu_sb = [load_w_bf(p, f"{pfix}u{h}",
                                   w["Wu"][h // 2, :, (h % 2) * 128:
                                           (h % 2) * 128 + 128],
                                   D, 128, stage) for h in range(16)]
                wv_sb = [load_w_bf(p, f"{pfix}v{h}",
                                   w["Wv"][h // 2, :, (h % 2) * 128:
                                           (h % 2) * 128 + 128],
                                   D, 128, stage) for h in range(16)]
                emit_uv(p, ct, dr["xln0"], wu_sb, wv_sb, wfull, sw_dram, sb, PS)
        with p.scope():
            stage = p.sbuf("wstage", [128, 2048], F32)
            sb = mv_scratch(p, 16)
            wo_sb = [load_w_bf(p, f"{pfix}o{h}",
                               w["Wo"][h // 2, (h % 2) * 128:
                                       (h % 2) * 128 + 128, :],
                               128, D, stage)[0] for h in range(16)]
            wo_t = [[wo_sb[h][:, 128 * j:128 * (j + 1)] for h in range(16)]
                    for j in range(NDT)]
            emit_matvec_out(p, ct, sw_dram, 16, wo_t, x_dram, out_dram, sb, PS,
                            scale=0.5)

    if upto >= 1:
        ffn_block(x_in, dr["sw"], dr["x1"], "f")

    if upto < 2:
        p.finalize()
        return nc
    with p.scope():
        sb = ln_scratch(p)
        emit_ln(p, ct, dr["x1"], dr["xlnm"], sb, PS, dst_f32_dram=dr["xlnm32"])
    if upto < 3:
        p.finalize()
        return nc
    with p.scope():
        stage = p.sbuf("wstage", [128, 2048], F32)
        sb = uv_scratch(p)
        wxz_sb = load_w_bf(p, "wxz", w["W_xz"], D, 2 * D, stage)
        wxz_t = [[wxz_sb[k][:, 128 * j:128 * (j + 1)] for k in range(NDT)]
                 for j in range(16)]

        def evac_xz(p_, cc, j, pj):
            csl = slice(cc * 512, (cc + 1) * 512)
            dst = dr["xssm"] if j < 8 else dr["z"]
            jj = j % 8
            tile = sb["su"]
            p_.wait("vector", "tensor", "gpsimd")
            p_.run("vector", lambda e: e.tensor_copy(out=tile[:, :],
                                                     in_=pj[:, 0:512]))
            p_.wait("gpsimd", "vector")
            p_.dma("gpsimd", dst[128 * jj:128 * (jj + 1), csl], tile[:, :])

        emit_linear(p, ct, dr["xlnm"], wxz_t, 16, sb, PS, evac_xz)

    if upto < 4:
        p.finalize()
        return nc
    with p.scope():
        stage = p.sbuf("wstage", [128, 1024], F32)
        sb = uv_scratch(p)
        t32l = [p.sbuf(f"t32{j}", [128, 512], F32) for j in range(2)]
        tbfl = [p.sbuf(f"tbf{j}", [128, 512], BF16) for j in range(2)]
        wdt_sb = load_w_bf(p, "wdt", w["W_dt"], D, D, stage)
        wdt_t = [[wdt_sb[k][:, 128 * j:128 * (j + 1)] for k in range(NDT)]
                 for j in range(NDT)]
        bdt_c = load_vec_cols(p, "bdt", w["b_dt"], D, ct, PS[6])

        def evac_dt(p_, cc, j, pj):
            csl = slice(cc * 512, (cc + 1) * 512)
            t32 = t32l[j % 2]
            tbf = tbfl[j % 2]
            p_.wait("scalar", "tensor", "gpsimd")
            p_.run("scalar", lambda e: e.activation(
                out=t32[:, :], in_=pj[:, 0:512], func=AF.Exp,
                bias=bdt_c[:, j:j + 1]))
            p_.run("scalar", lambda e: e.activation(
                out=t32[:, :], in_=t32[:, :], func=AF.Ln, bias=1.0))
            p_.wait("vector", "scalar", "gpsimd")
            xj = sb["xbcc"][j]
            p_.run("vector", lambda e, xj=xj: e.tensor_tensor(
                out=tbf[:, :], in0=t32[:, :], in1=xj[:, :],
                op=ALU.mult))
            p_.wait("gpsimd", "vector", "scalar")
            p_.dma("gpsimd", dr["dt"][128 * j:128 * (j + 1), csl], t32[:, :])
            p_.dma("gpsimd", dr["dtx"][128 * j:128 * (j + 1), csl], tbf[:, :])

        emit_linear(p, ct, dr["xssm"], wdt_t, NDT, sb, PS, evac_dt)

    a_sb = []
    for k in range(NDT):
        at = p.sbuf(f"aA{k}", [128, 16], F32)
        p.dma("sync", at[:, :], w["A_log"][128 * k:128 * (k + 1), :])
        p.wait("scalar", "sync")
        p.run("scalar", lambda e, at=at: e.activation(
            out=at[:, :], in_=at[:, :], func=AF.Exp))
        p.wait("vector", "scalar")
        p.run("vector", lambda e, at=at: e.tensor_scalar(
            out=at[:, :], in0=at[:, :], scalar1=-1.0, scalar2=None,
            op0=ALU.mult))
        a_sb.append(at)
    dsk_c = load_vec_cols(p, "dsk", w["D_skip"], D, ct, PS[6])
    vth_c = load_vec_cols(p, "vth", w["v_th"], D, ct, PS[6])
    p.wait("vector", "sync")
    p.run("vector", lambda e: e.tensor_scalar(
        out=vth_c[:, :], in0=vth_c[:, :], scalar1=float(V_TH_MIN),
        scalar2=-K_STEEP, op0=ALU.max, op1=ALU.mult))

    if upto < 5:
        p.finalize()
        return nc
    with p.scope():
        bm_sb = p.sbuf("bm", [16, T], BF16)
        cm_sb = p.sbuf("cm", [16, T], BF16)
        with p.scope():
            stage = p.sbuf("wstage", [128, 1024], F32)
            sbx = uv_scratch(p)
            wb_sb = load_w_bf(p, "wB", w["W_B"], D, 16, stage)
            wc_sb = load_w_bf(p, "wC", w["W_C"], D, 16, stage)
            for cc in range(NCC):
                csl = slice(cc * 512, (cc + 1) * 512)
                p.barrier()
                for k in range(NDT):
                    p.dma("sync", sbx["xbcc"][k][:, :],
                          dr["xlnm"][128 * k:128 * (k + 1), csl])
                p.wait("tensor", "sync")
                for k in range(NDT):
                    p.run("tensor", lambda e, k=k, st=(k == 0),
                          sp=(k == NDT - 1):
                          e.matmul(out=PS[0][0:16, 0:512],
                                   lhsT=wb_sb[k][:, 0:16],
                                   rhs=sbx["xbcc"][k][:, :], start=st, stop=sp))
                for k in range(NDT):
                    p.run("tensor", lambda e, k=k, st=(k == 0),
                          sp=(k == NDT - 1):
                          e.matmul(out=PS[1][0:16, 0:512],
                                   lhsT=wc_sb[k][:, 0:16],
                                   rhs=sbx["xbcc"][k][:, :], start=st, stop=sp))
                p.wait("vector", "tensor")
                p.run("vector", lambda e, csl=csl: e.tensor_copy(
                    out=bm_sb[0:16, csl], in_=PS[0][0:16, 0:512]))
                p.run("vector", lambda e, csl=csl: e.tensor_copy(
                    out=cm_sb[0:16, csl], in_=PS[1][0:16, 0:512]))

        TC = 1024
        NSC = T // TC
        if upto >= 5.5:
          with p.scope():
            dtc = [p.sbuf(f"dtc{k}", [128, TC], F32) for k in range(NDT)]
            dxc = [p.sbuf(f"dxc{k}", [128, TC], BF16) for k in range(NDT)]
            brep = [p.sbuf(f"brep{n}", [128, TC], BF16) for n in range(16)]
            crep = [p.sbuf(f"crep{n}", [128, TC], BF16) for n in range(16)]
            dec2 = [p.sbuf(f"dec{i}", [128, TC], F32) for i in range(2)]
            inp = p.sbuf("inp", [128, TC], F32)
            scn = p.sbuf("scn", [128, TC], F32)
            yk = [p.sbuf(f"ykk{k}", [128, TC], F32) for k in range(NDT)]
            st8 = p.sbuf("st8", [128, 128], F32)
            xsk = p.sbuf("xsk", [128, TC], BF16)
            zk = p.sbuf("zk", [128, TC], BF16)
            spk = p.sbuf("spk", [128, TC], F32)
            p.wait("vector", "sync")
            p.run("vector", lambda e: e.memset(st8[:, :], 0.0))
            dec_rd = [0, 0]
            for cc in range(NSC):
                csl = slice(cc * TC, (cc + 1) * TC)
                p.barrier()
                for k in range(NDT):
                    p.dma("sync", dtc[k][:, :],
                          dr["dt"][128 * k:128 * (k + 1), csl])
                    p.dma("sync", dxc[k][:, :],
                          dr["dtx"][128 * k:128 * (k + 1), csl])
                p.wait("tensor", "vector")
                for n in range(16 if upto >= 5.58 else 0):
                    for half in range(2):
                        hs = slice(cc * TC + half * 512,
                                   cc * TC + half * 512 + 512)
                        p.wait("tensor", "vector")
                        p.run("tensor", lambda e, n=n, hs=hs, b=(n % 3):
                              e.matmul(out=PS[b][:, 0:512],
                                       lhsT=ct["oh16"][:, 128 * n:128 * (n + 1)],
                                       rhs=bm_sb[0:16, hs],
                                       start=True, stop=True))
                        p.run("tensor", lambda e, n=n, hs=hs, b=3 + (n % 3):
                              e.matmul(out=PS[b][:, 0:512],
                                       lhsT=ct["oh16"][:, 128 * n:128 * (n + 1)],
                                       rhs=cm_sb[0:16, hs],
                                       start=True, stop=True))
                        p.wait("vector", "tensor")
                        p.run("vector", lambda e, n=n, half=half, b=(n % 3):
                              e.tensor_copy(
                                  out=brep[n][:, 512 * half:512 * half + 512],
                                  in_=PS[b][:, 0:512]))
                        p.run("vector", lambda e, n=n, half=half, b=3 + (n % 3):
                              e.tensor_copy(
                                  out=crep[n][:, 512 * half:512 * half + 512],
                                  in_=PS[b][:, 0:512]))
                p.wait("vector", "sync")
                p.wait("scalar", "sync", "vector")
                for k in range(NDT):
                    if upto < 5.7:
                        break
                    for n in range(16):
                        sc_col = k * 16 + n
                        pb = sc_col % 2
                        dec = dec2[pb]
                        p.wait_at("scalar", "vector", dec_rd[pb])
                        p.run("scalar", lambda e, k=k, n=n, dec=dec:
                              e.activation(
                                  out=dec[:, :], in_=dtc[k][:, :], func=AF.Exp,
                                  scale=a_sb[k][:, n:n + 1]))
                        exp_done = p.s["scalar"].v
                        p.runf("vector", lambda e, k=k, n=n: e.tensor_tensor(
                            out=inp[:, :], in0=dxc[k][:, :], in1=brep[n][:, :],
                            op=ALU.mult))
                        p.wait_at("vector", "scalar", exp_done)
                        p.runf("vector", lambda e, c=sc_col, dec=dec:
                              e.tensor_tensor_scan(
                                  out=scn[:, :], data0=dec[:, :],
                                  data1=inp[:, :], initial=st8[:, c:c + 1],
                                  op0=ALU.mult, op1=ALU.add))
                        dec_rd[pb] = p.s["vector"].v
                        p.runw("vector", lambda e, c=sc_col: e.tensor_copy(
                            out=st8[:, c:c + 1], in_=scn[:, TC - 1:TC]))
                        if upto < 5.8:
                            continue
                        p.runf("vector", lambda e, n=n: e.tensor_tensor(
                            out=scn[:, :], in0=scn[:, :], in1=crep[n][:, :],
                            op=ALU.mult))
                        if n == 0:
                            p.runf("vector", lambda e, k=k: e.tensor_copy(
                                out=yk[k][:, :], in_=scn[:, :]))
                        else:
                            p.runf("vector", lambda e, k=k: e.tensor_tensor(
                                out=yk[k][:, :], in0=yk[k][:, :],
                                in1=scn[:, :], op=ALU.add))
                    if upto < 5.9:
                        continue
                    p.wait("sync", "vector", "gpsimd")
                    p.dma("sync", xsk[:, :],
                          dr["xssm"][128 * k:128 * (k + 1), csl])
                    p.dma("sync", zk[:, :], dr["z"][128 * k:128 * (k + 1), csl])
                    p.wait("vector", "sync")
                    p.runf("vector", lambda e, k=k: e.scalar_tensor_tensor(
                        out=yk[k][:, :], in0=xsk[:, :],
                        scalar=dsk_c[:, k:k + 1], in1=yk[k][:, :],
                        op0=ALU.mult, op1=ALU.add))
                    p.wait("scalar", "vector")
                    p.run("scalar", lambda e, k=k: e.activation(
                        out=spk[:, :], in_=yk[k][:, :], func=AF.Sigmoid,
                        scale=K_STEEP, bias=vth_c[:, k:k + 1]))
                    p.runf("scalar", lambda e: e.activation(
                        out=zk[:, :], in_=zk[:, :], func=AF.Silu))
                    p.wait("vector", "scalar", "gpsimd")
                    p.runf("vector", lambda e, k=k: e.tensor_tensor(
                        out=yk[k][:, :], in0=yk[k][:, :], in1=spk[:, :],
                        op=ALU.mult))
                    p.runf("vector", lambda e, k=k: e.tensor_tensor(
                        out=xsk[:, :], in0=yk[k][:, :], in1=zk[:, :],
                        op=ALU.mult))
                    p.wait("gpsimd", "vector")
                    p.dma("gpsimd", dr["yg"][128 * k:128 * (k + 1), csl],
                          xsk[:, :])

    if upto < 6:
        p.finalize()
        return nc
    with p.scope():
        stage = p.sbuf("wstage", [128, 1024], F32)
        sb = mv_scratch(p, NDT)
        wout_sb = load_w_bf(p, "wout", w["W_out"], D, D, stage)
        wout_t = [[wout_sb[k][:, 128 * j:128 * (j + 1)] for k in range(NDT)]
                  for j in range(NDT)]
        emit_matvec_out(p, ct, dr["yg"], NDT, wout_t, dr["x1"], dr["x2"],
                        sb, PS, scale=1.0, sub_dram=dr["xlnm32"])

    if upto < 7:
        p.finalize()
        return nc
    ffn_block(dr["x2"], dr["sw"], x3_out, "g")

    if upto < 8:
        p.finalize()
        return nc
    with p.scope():
        sb = ln_scratch(p)
        emit_ln(p, ct, x3_out, dr["xlnjunk"], sb, PS, dst_f32_dram=xln1_out,
                rmax=rmax)
        p.wait("tensor", "vector")
        p.run("tensor", lambda e: e.transpose(
            out=PS[0][0:1, 0:128], in_=rmax[:, 0:1],
            identity=ct["eye_f32"][:, :]))
        p.wait("vector", "tensor")
        p.runw("vector", lambda e: e.reduce_max(
            out=mrow[0:1, 0:1], in_=PS[0][0:1, 0:128], axis=AX.X))
        p.wait("gpsimd", "vector")
        p.dma("gpsimd", mx_out[0:1, 0:1], mrow[0:1, 0:1])

    p.finalize()
    return nc





def emit_scalar_prep(p, ct, gmax_sb, srow, ps):
    s_col = p.sbuf("s_col", [128, 2], F32)
    p.wait("tensor", "sync", "vector")
    p.run("tensor", lambda e: e.transpose(
        out=ps[0:1, 0:8], in_=gmax_sb[0:8, 0:1],
        identity=ct["eye_f32"][0:8, 0:8]))
    p.wait("vector", "tensor")
    p.runw("vector", lambda e: e.reduce_max(
        out=srow[0:1, 0:1], in_=ps[0:1, 0:8], axis=AX.X))
    p.runw("vector", lambda e: e.tensor_scalar(
        out=srow[0:1, 0:1], in0=srow[0:1, 0:1], scalar1=float(QEPS),
        scalar2=1.0 / 255.0, op0=ALU.max, op1=ALU.mult))
    p.runw("vector", lambda e: e.reciprocal(
        out=srow[0:1, 1:2], in_=srow[0:1, 0:1]))
    p.wait("tensor", "vector")
    p.run("tensor", lambda e: e.matmul(
        out=ps[0:128, 8:10], lhsT=ct["ones_row"][:, :], rhs=srow[0:1, 0:2],
        start=True, stop=True))
    p.wait("vector", "tensor")
    p.runw("vector", lambda e: e.tensor_copy(out=s_col[:, 0:2],
                                             in_=ps[0:128, 8:10]))
    p.settle()
    return s_col[:, 0:1], s_col[:, 1:2]


def build_launch_B(uptoB=99):
    F = 4 * D
    NFT = F // 128
    nc = bass.Bass("TRN2", target_bir_lowering=False)
    cd = decl_consts(nc)
    xln_in = nc.dram_tensor("xln", [D, T], F32, kind="ExternalInput").ap()
    w1_in = nc.dram_tensor("w1", [F, D], F32, kind="ExternalInput").ap()
    w1t_in = nc.dram_tensor("w1t", [D, F], F32, kind="ExternalInput").ap()
    b1_in = nc.dram_tensor("b1", [F], F32, kind="ExternalInput").ap()
    gmax_in = nc.dram_tensor("gmax", [8, 1], F32, kind="ExternalInput").ap()
    h_out = nc.dram_tensor("h", [F, T], F32, kind="ExternalOutput").ap()
    mx_out = nc.dram_tensor("mx", [1, 1], F32, kind="ExternalOutput").ap()
    dbg_out = nc.dram_tensor("dbgB", [128, 128], F32, kind="ExternalOutput").ap()
    dbq_out = nc.dram_tensor("dbqB", [128, 512], F32, kind="ExternalOutput").ap()


    p = Prog(nc)
    ct = load_const_tiles(p, cd, ("ones_col_bf", "ones_row", "eye_f32",
                                  "oh32f"))
    psA = p.psum("psB", [128, 3584], F32)
    PS = [psA[:, 512 * i:512 * (i + 1)] for i in range(7)]
    psb16 = p.psum("psBb", [128, 1024], BF16)
    srow = p.sbuf("srowB", [1, 16], F32)
    gmax_sb = p.sbuf("gmaxB", [8, 1], F32)
    p.dma("sync", gmax_sb[:, :], gmax_in[:, :])
    s_colB, rs_col = emit_scalar_prep(p, ct, gmax_sb, srow, PS[6])

    sw_c = p.sbuf("sw_c", [128, NFT], F32)
    swr_c = p.sbuf("swr_c", [128, NFT], F32)
    sbq = p.sbuf("sbq", [128, NFT], F32)
    bterm = p.sbuf("bterm", [128, NFT], F32)
    b1c = load_vec_cols(p, "b1c", b1_in, F, ct, PS[6])
    with p.scope():
        stage = p.sbuf("wstageB", [128, 1024], F32)
        for i in range(NFT):
            p.barrier()
            p.dma("sync", stage[:, 0:D], w1_in[128 * i:128 * (i + 1), :])
            p.barrier()
            if i == 0:
                p.dma("gpsimd", dbq_out[:, :], stage[:, 0:512])
                p.barrier()
            p.runw("vector", lambda e, i=i, st=stage: e.reduce_max(
                out=sw_c[:, i:i + 1], in_=st[:, 0:D], axis=AX.X,
                apply_absolute_value=True))
        tq1 = p.sbuf("tq1", [128, NFT], F32)
        tq2 = p.sbuf("tq2", [128, NFT], F32)
        p.wait("vector", "sync")
        p.runw("vector", lambda e: e.tensor_scalar(
            out=tq1[:, :], in0=sw_c[:, :], scalar1=float(QEPS),
            scalar2=1.0 / 127.0, op0=ALU.max, op1=ALU.mult))
        p.runw("vector", lambda e: e.tensor_copy(out=sw_c[:, :],
                                                 in_=tq1[:, :]))
        p.wait("gpsimd", "vector")
        p.dma("gpsimd", dbg_out[0:128, 98:128], sw_c[:, 0:30])
        p.runw("vector", lambda e: e.reciprocal(out=swr_c[:, :],
                                                in_=sw_c[:, :]))
        p.runw("vector", lambda e: e.tensor_scalar(
            out=sbq[:, :], in0=sw_c[:, :], scalar1=s_colB, scalar2=None,
            op0=ALU.mult))
        p.runw("vector", lambda e: e.reciprocal(out=tq1[:, :],
                                                in_=sbq[:, :]))
        p.runw("vector", lambda e: e.tensor_tensor(
            out=tq2[:, :], in0=b1c[:, :], in1=tq1[:, :], op=ALU.mult))
        p.runw("vector", lambda e: e.tensor_scalar(
            out=tq1[:, :], in0=tq2[:, :], scalar1=MAGIC, scalar2=MAGIC,
            op0=ALU.add, op1=ALU.subtract))
        p.runw("vector", lambda e: e.tensor_tensor(
            out=bterm[:, :], in0=sbq[:, :], in1=tq1[:, :], op=ALU.mult))

    if uptoB < 2:
        p.finalize()
        return nc
    qw = [p.sbuf(f"qw1t{i}", [128, F], BF16) for i in range(NDT)]
    with p.scope():
        stage = p.sbuf("wstageB2", [128, F], F32)
        rep = p.sbuf("repB", [128, 512], F32)
        r32 = p.sbuf("r32B", [NFT, 128], F32)
        if uptoB >= 2.2:
            p.wait("tensor", "vector", "sync")
            p.run("tensor", lambda e: e.transpose(
                out=PS[0][0:NFT, 0:128], in_=swr_c[:, :],
                identity=ct["eye_f32"][:, :]))
            p.wait("vector", "tensor")
            p.run("vector", lambda e: e.tensor_copy(out=r32[:, :],
                                                    in_=PS[0][0:NFT, 0:128]))
            p.settle()
        for i in range(NDT):
            t = qw[i]
            p.wait("sync", "vector")
            for q4 in range(4):
                p.dma("sync", stage[:, 1024 * q4:1024 * (q4 + 1)],
                      w1t_in[128 * i:128 * (i + 1),
                             1024 * q4:1024 * (q4 + 1)])
            for fc in range(F // 512):
                p.barrier()
                if uptoB >= 2.4:
                    for s4 in range(4):
                        fr = fc * 4 + s4
                        p.run("tensor", lambda e, fr=fr, s4=s4: e.matmul(
                            out=PS[1][:, 128 * s4:128 * (s4 + 1)],
                            lhsT=ct["oh32f"][:, 128 * fr:128 * (fr + 1)],
                            rhs=r32[0:NFT, :], start=True, stop=True))
                    p.wait("vector", "tensor")
                if uptoB >= 2.6:
                    fsl = slice(512 * fc, 512 * (fc + 1))
                    p.run("vector", lambda e, fsl=fsl, st=stage:
                          e.tensor_tensor(
                              out=rep[:, :], in0=st[:, fsl],
                              in1=PS[1][:, 0:512], op=ALU.mult))
                    p.run("vector", lambda e, t=t, fsl=fsl: e.tensor_scalar(
                        out=t[:, fsl], in0=rep[:, :], scalar1=MAGIC,
                        scalar2=MAGIC, op0=ALU.add, op1=ALU.subtract))

    if uptoB < 3:
        p.finalize()
        return nc
    hmax = p.sbuf("hmax", [128, 1], F32)
    p.wait("vector", "sync")
    p.run("vector", lambda e: e.memset(hmax[:, :], 0.0))
    with p.scope():
        xcc = [p.sbuf(f"qx{k}", [128, 512], F32) for k in range(NDT)]
        qx = [p.sbuf(f"qxb{k}", [128, 512], BF16) for k in range(NDT)]
        hcc2 = [p.sbuf(f"hccB{q}", [128, 512], F32) for q in range(2)]
        tm = p.sbuf("tmB", [128, 1], F32)
        for cc in range(NCC):
            csl = slice(cc * 512, (cc + 1) * 512)
            p.barrier()
            for k in range(NDT):
                p.dma("sync", xcc[k][:, :], xln_in[128 * k:128 * (k + 1), csl])
            p.barrier()
            for k in range(NDT):
                p.run("vector", lambda e, k=k: e.tensor_scalar(
                    out=xcc[k][:, :], in0=xcc[k][:, :], scalar1=rs_col,
                    scalar2=MAGIC, op0=ALU.mult, op1=ALU.add))
                p.run("vector", lambda e, k=k: e.tensor_scalar(
                    out=xcc[k][:, :], in0=xcc[k][:, :], scalar1=MAGIC,
                    scalar2=0.0, op0=ALU.subtract, op1=ALU.max))
                p.run("vector", lambda e, k=k: e.tensor_scalar(
                    out=qx[k][:, :], in0=xcc[k][:, :], scalar1=255.0,
                    scalar2=None, op0=ALU.min))

            quant_done = p.s["vector"].v
            p.wait("tensor", "scalar")
            evac_done = {}
            dma_done = {}
            for j in range(NFT):
                pj = PS[j % 7]
                hq = hcc2[j % 2]
                p.wait_at("tensor", "vector",
                          max(quant_done, evac_done.get(j - 6, 0)))
                for k in range(NDT):
                    p.run("tensor", lambda e, w=qw[k], j=j, k=k, pj=pj,
                          st=(k == 0), sp=(k == NDT - 1):
                          e.matmul(out=pj[:, 0:512],
                                   lhsT=w[:, 128 * j:128 * (j + 1)],
                                   rhs=qx[k][:, :], start=st, stop=sp))
                p.wait("vector", "tensor")
                p.wait_at("vector", "gpsimd", dma_done.get(j - 2, 0))
                p.run("vector", lambda e, j=j, pj=pj, hq=hq: e.tensor_scalar(
                    out=hq[:, :], in0=pj[:, 0:512], scalar1=sbq[:, j:j + 1],
                    scalar2=bterm[:, j:j + 1], op0=ALU.mult, op1=ALU.add))
                p.run("vector", lambda e, hq=hq: e.tensor_scalar(
                    out=hq[:, :], in0=hq[:, :], scalar1=0.0, scalar2=None,
                    op0=ALU.max))
                p.runw("vector", lambda e, hq=hq: e.reduce_max(
                    out=tm[:, 0:1], in_=hq[:, :], axis=AX.X))
                p.runw("vector", lambda e: e.tensor_tensor(
                    out=hmax[:, 0:1], in0=hmax[:, 0:1], in1=tm[:, 0:1],
                    op=ALU.max))
                evac_done[j] = p.s["vector"].v
                p.wait("gpsimd", "vector")
                p.dma("gpsimd", h_out[128 * j:128 * (j + 1), csl], hq[:, :])
                dma_done[j] = p.s["gpsimd"].v
    p.wait("gpsimd", "vector", "scalar", "tensor", "sync")
    p.dma("gpsimd", dbg_out[0:128, 0:2], s_colB.tensor.ap()[:, 0:2])
    p.dma("gpsimd", dbg_out[0:128, 2:34], sbq[:, :])
    p.dma("gpsimd", dbg_out[0:128, 34:66], bterm[:, :])
    p.dma("gpsimd", dbg_out[0:128, 66:98], sw_c[:, :])
    p.wait("tensor", "vector", "sync")
    p.run("tensor", lambda e: e.transpose(
        out=PS[0][0:1, 0:128], in_=hmax[:, 0:1], identity=ct["eye_f32"][:, :]))
    p.wait("vector", "tensor")
    p.runw("vector", lambda e: e.reduce_max(
        out=srow[0:1, 2:3], in_=PS[0][0:1, 0:128], axis=AX.X))
    p.wait("gpsimd", "vector")
    p.dma("gpsimd", mx_out[0:1, 0:1], srow[0:1, 2:3])
    p.finalize()
    return nc


def emit_w2_quant(p, ct, w2_in, w2t_in, b2_in, gmax_sb, srow, PS):
    F = 4 * D
    NFT = F // 128
    s_col, rs_col = emit_scalar_prep(p, ct, gmax_sb, srow, PS[6])
    sw_c = p.sbuf("sw2c", [128, NDT], F32)
    swr_c = p.sbuf("sw2r", [128, NDT], F32)
    sbq = p.sbuf("sbq2", [128, NDT], F32)
    bterm = p.sbuf("bterm2", [128, NDT], F32)
    b2c = load_vec_cols(p, "b2c", b2_in, D, ct, PS[6])
    with p.scope():
        stage = p.sbuf("wstageC", [128, F], F32)
        for i in range(NDT):
            p.wait("sync", "vector")
            p.dma("sync", stage[:, :], w2_in[128 * i:128 * (i + 1), :])
            p.wait("vector", "sync")
            p.runw("vector", lambda e, i=i, st=stage: e.reduce_max(
                out=sw_c[:, i:i + 1], in_=st[:, :], axis=AX.X,
                apply_absolute_value=True))
        p.wait("vector", "sync")
        p.runw("vector", lambda e: e.tensor_scalar(
            out=sw_c[:, :], in0=sw_c[:, :], scalar1=float(QEPS),
            scalar2=1.0 / 127.0, op0=ALU.max, op1=ALU.mult))
        p.runw("vector", lambda e: e.reciprocal(out=swr_c[:, :],
                                                in_=sw_c[:, :]))
        p.runw("vector", lambda e: e.tensor_scalar(
            out=sbq[:, :], in0=sw_c[:, :], scalar1=s_col, scalar2=None,
            op0=ALU.mult))
        p.runw("vector", lambda e: e.reciprocal(out=bterm[:, :],
                                                in_=sbq[:, :]))
        p.runw("vector", lambda e: e.tensor_tensor(
            out=bterm[:, :], in0=b2c[:, :], in1=bterm[:, :], op=ALU.mult))
        p.runw("vector", lambda e: e.tensor_scalar(
            out=bterm[:, :], in0=bterm[:, :], scalar1=MAGIC, scalar2=MAGIC,
            op0=ALU.add, op1=ALU.subtract))
        p.runw("vector", lambda e: e.tensor_tensor(
            out=bterm[:, :], in0=sbq[:, :], in1=bterm[:, :], op=ALU.mult))
        p.settle()

    qw = [p.sbuf(f"qw2t{i}", [128, D], BF16) for i in range(F // 128)]
    with p.scope():
        stage = p.sbuf("wstageC2", [128, D], F32)
        rep = p.sbuf("repC", [128, D], F32)
        r8 = p.sbuf("r8C", [NDT, 128], F32)
        p.wait("tensor", "vector", "sync")
        p.run("tensor", lambda e: e.transpose(
            out=PS[0][0:NDT, 0:128], in_=swr_c[:, :],
            identity=ct["eye_f32"][:, :]))
        p.wait("vector", "tensor")
        p.run("vector", lambda e: e.tensor_copy(out=r8[:, :],
                                                in_=PS[0][0:NDT, 0:128]))
        p.settle()
        p.wait("tensor", "vector")
        for s8 in range(NDT):
            b = s8 % 2
            p.wait("tensor", "vector")
            p.run("tensor", lambda e, s8=s8, b=b: e.matmul(
                out=PS[b][:, 0:128],
                lhsT=ct["oh8f"][:, 128 * s8:128 * (s8 + 1)],
                rhs=r8[0:NDT, :], start=True, stop=True))
            p.wait("vector", "tensor")
            p.run("vector", lambda e, s8=s8, b=b: e.tensor_copy(
                out=rep[:, 128 * s8:128 * (s8 + 1)], in_=PS[b][:, 0:128]))
        for i in range(F // 128):
            t = qw[i]
            p.wait("sync", "vector")
            p.dma("sync", stage[:, :], w2t_in[128 * i:128 * (i + 1), :])
            p.wait("vector", "sync")
            p.run("vector", lambda e, t=t, st=stage: e.tensor_tensor(
                out=st[:, :], in0=st[:, :], in1=rep[:, :], op=ALU.mult))
            p.run("vector", lambda e, t=t, st=stage: e.tensor_scalar(
                out=t[:, :], in0=st[:, :], scalar1=MAGIC, scalar2=MAGIC,
                op0=ALU.add, op1=ALU.subtract))
    return qw, sbq, bterm, s_col, rs_col


def emit_qh_w2_resid(p, ct, h_in, x_in, out_dram, qw, sbq, bterm, s_q, PS,
                     scale=0.5):
    F = 4 * D
    NFT = F // 128
    with p.scope():
        hcc = [p.sbuf(f"hC{k}", [128, 512], F32) for k in range(NFT)]
        qh = [p.sbuf(f"qhC{k}", [128, 512], BF16) for k in range(NFT)]
        occ = [p.sbuf(f"oC{j}", [128, 512], F32) for j in range(NDT)]
        xcc = [p.sbuf(f"xC{j}", [128, 512], F32) for j in range(NDT)]
        for cc in range(NCC):
            csl = slice(cc * 512, (cc + 1) * 512)
            p.barrier()
            for k in range(NFT):
                p.dma("sync", hcc[k][:, :], h_in[128 * k:128 * (k + 1), csl])
            for j in range(NDT):
                p.dma("sync", xcc[j][:, :], x_in[128 * j:128 * (j + 1), csl])
            p.wait("vector", "sync")
            for k in range(NFT):
                p.run("vector", lambda e, k=k: e.tensor_scalar(
                    out=hcc[k][:, :], in0=hcc[k][:, :], scalar1=s_q,
                    scalar2=MAGIC, op0=ALU.mult, op1=ALU.add))
                p.run("vector", lambda e, k=k: e.tensor_scalar(
                    out=qh[k][:, :], in0=hcc[k][:, :], scalar1=MAGIC,
                    scalar2=None, op0=ALU.subtract))
            quant_done = p.s["vector"].v
            p.wait("tensor", "scalar")
            evac_done = {}
            for j in range(NDT):
                pj = PS[j % 7]
                p.wait_at("tensor", "vector",
                          max(quant_done, evac_done.get(j - 6, 0)))
                for k in range(NFT):
                    p.run("tensor", lambda e, w=qw[k], j=j, k=k, pj=pj,
                          st=(k == 0), sp=(k == NFT - 1):
                          e.matmul(out=pj[:, 0:512],
                                   lhsT=w[:, 128 * j:128 * (j + 1)],
                                   rhs=qh[k][:, :], start=st, stop=sp))
                p.wait("vector", "tensor")
                p.run("vector", lambda e, j=j, pj=pj: e.tensor_scalar(
                    out=occ[j][:, :], in0=pj[:, 0:512],
                    scalar1=sbq[:, j:j + 1], scalar2=bterm[:, j:j + 1],
                    op0=ALU.mult, op1=ALU.add))
                p.run("vector", lambda e, j=j: e.scalar_tensor_tensor(
                    out=occ[j][:, :], in0=occ[j][:, :], scalar=float(scale),
                    in1=xcc[j][:, :], op0=ALU.mult, op1=ALU.add))
                evac_done[j] = p.s["vector"].v
            p.wait("gpsimd", "vector")
            for j in range(NDT):
                p.dma("gpsimd", out_dram[128 * j:128 * (j + 1), csl],
                      occ[j][:, :])


def build_launch_E():
    F = 4 * D
    nc = bass.Bass("TRN2", target_bir_lowering=False)
    cd = decl_consts(nc)
    x_in = nc.dram_tensor("x", [D, T], F32, kind="ExternalInput").ap()
    h_in = nc.dram_tensor("h", [F, T], F32, kind="ExternalInput").ap()
    w2_in = nc.dram_tensor("w2", [D, F], F32, kind="ExternalInput").ap()
    w2t_in = nc.dram_tensor("w2t", [F, D], F32, kind="ExternalInput").ap()
    b2_in = nc.dram_tensor("b2", [D], F32, kind="ExternalInput").ap()
    gmax_in = nc.dram_tensor("gmax", [8, 1], F32, kind="ExternalInput").ap()
    y_out = nc.dram_tensor("y", [D, T], F32, kind="ExternalOutput").ap()

    p = Prog(nc)
    ct = load_const_tiles(p, cd, ("ones_col_bf", "ones_row", "eye_f32",
                                  "oh8f"))
    psA = p.psum("psE", [128, 3584], F32)
    PS = [psA[:, 512 * i:512 * (i + 1)] for i in range(7)]
    psb16 = p.psum("psEb", [128, 1024], BF16)
    srow = p.sbuf("srowE", [1, 16], F32)
    gmax_sb = p.sbuf("gmaxE", [8, 1], F32)
    p.dma("sync", gmax_sb[:, :], gmax_in[:, :])
    qw, sbq, bterm, s_col, rs_col = emit_w2_quant(
        p, ct, w2_in, w2t_in, b2_in, gmax_sb, srow, PS)
    emit_qh_w2_resid(p, ct, h_in, x_in, y_out, qw, sbq, bterm, rs_col, PS)
    p.finalize()
    return nc




TWOPI = float(np.float32(2.0 * np.pi))
NS4 = 512
NNT = NS4 // 128
GCH = 128
NCH = T // GCH


def row_frac_sincos(p, rowv, tmp, out_sin, out_cos, width=512):
    a, b = tmp[0:1, 0:width], tmp[0:1, width:2 * width]
    p.run("vector", lambda e: e.tensor_scalar(
        out=a, in0=rowv, scalar1=MAGIC, scalar2=MAGIC, op0=ALU.add,
        op1=ALU.subtract))
    p.run("vector", lambda e: e.tensor_tensor(
        out=a, in0=rowv, in1=a, op=ALU.subtract))
    p.run("vector", lambda e: e.tensor_scalar(
        out=a, in0=a, scalar1=TWOPI, scalar2=None, op0=ALU.mult))
    p.wait("scalar", "vector")
    p.run("scalar", lambda e: e.activation(out=out_sin, in_=a, func=AF.Sin))
    p.wait("vector", "scalar")
    p.run("vector", lambda e: e.tensor_scalar(
        out=b, in0=rowv, scalar1=0.25, scalar2=None, op0=ALU.add))
    p.run("vector", lambda e: e.tensor_scalar(
        out=a, in0=b, scalar1=MAGIC, scalar2=MAGIC, op0=ALU.add,
        op1=ALU.subtract))
    p.run("vector", lambda e: e.tensor_tensor(
        out=a, in0=b, in1=a, op=ALU.subtract))
    p.run("vector", lambda e: e.tensor_scalar(
        out=a, in0=a, scalar1=TWOPI, scalar2=None, op0=ALU.mult))
    p.wait("scalar", "vector")
    p.run("scalar", lambda e: e.activation(out=out_cos, in_=a, func=AF.Sin))
    p.wait("vector", "scalar")


def rows_to_cols(p, ct, row_ap, out_cols, nj, ps, bounce):
    n = nj * 128
    assert n == 512
    p.wait("gpsimd", "vector", "scalar", "tensor", "sync")
    p.dma("gpsimd", bounce.rearrange("(a b) -> a b", a=1), row_ap[0:1, 0:n])
    p.wait("sync", "gpsimd")
    p.dma("sync", out_cols[:, 0:nj],
          bounce.rearrange("(j q) -> q j", q=128), slow=True)


def build_launch_C(dbg=()):
    F = 4 * D
    nc = bass.Bass("TRN2", target_bir_lowering=False)
    cd = decl_consts(nc)
    x_in = nc.dram_tensor("x", [D, T], F32, kind="ExternalInput").ap()
    h_in = nc.dram_tensor("h", [F, T], F32, kind="ExternalInput").ap()
    w2_in = nc.dram_tensor("w2", [D, F], F32, kind="ExternalInput").ap()
    w2t_in = nc.dram_tensor("w2t", [F, D], F32, kind="ExternalInput").ap()
    b2_in = nc.dram_tensor("b2", [D], F32, kind="ExternalInput").ap()
    gmax_in = nc.dram_tensor("gmax", [8, 1], F32, kind="ExternalInput").ap()
    lr_in = nc.dram_tensor("Lr", [NS4], F32, kind="ExternalInput").ap()
    li_in = nc.dram_tensor("Li", [NS4], F32, kind="ExternalInput").ap()
    cr_in = nc.dram_tensor("Cr", [D, NS4], F32, kind="ExternalInput").ap()
    ci_in = nc.dram_tensor("Ci", [D, NS4], F32, kind="ExternalInput").ap()
    pd_in = nc.dram_tensor("pD", [D], F32, kind="ExternalInput").ap()
    lg_in = nc.dram_tensor("lg", [D], F32, kind="ExternalInput").ap()
    lb_in = nc.dram_tensor("lb", [D], F32, kind="ExternalInput").ap()
    x5_out = nc.dram_tensor("x5", [D, T], F32, kind="ExternalOutput").ap()
    bounce = nc.dram_tensor("i_bounce", [512], F32, kind="Internal").ap()
    xln2_out = nc.dram_tensor("xln2", [D, T], F32, kind="ExternalOutput").ap()
    mx_out = nc.dram_tensor("mx", [1, 1], F32, kind="ExternalOutput").ap()

    dr = {}
    for nm, sh, dt in [("x4", [D, T], F32), ("t1", [D, T], F32),
                       ("u", [D, T], BF16), ("ut", [T, D], BF16),
                       ("ys", [D, T], BF16), ("junk", [D, T], BF16)]:
        kind = "ExternalOutput" if nm in dbg else "Internal"
        dr[nm] = nc.dram_tensor("i_" + nm, sh, dt, kind=kind).ap()
    dbg_out = None
    if dbg:
        dbg_out = nc.dram_tensor("dbgC", [128, 1024], F32,
                                 kind="ExternalOutput").ap()

    p = Prog(nc)
    ct = load_const_tiles(p, cd, ("ones_col_bf", "ones_row", "eye_bf",
                                  "eye_f32", "ramp", "revramp", "oh8f"))
    psA = p.psum("psC", [128, 3584], F32)
    PS = [psA[:, 512 * i:512 * (i + 1)] for i in range(7)]
    psb16 = p.psum("psCb", [128, 1024], BF16)
    srow = p.sbuf("srowC", [1, 16], F32)
    gmax_sb = p.sbuf("gmaxC", [8, 1], F32)
    p.dma("sync", gmax_sb[:, :], gmax_in[:, :])
    rmax = p.sbuf("rmaxC", [128, 1], F32)

    with p.scope():
        qw, sbq, bterm, s_col, rs_col = emit_w2_quant(
            p, ct, w2_in, w2t_in, b2_in, gmax_sb, srow, PS)
        emit_qh_w2_resid(p, ct, h_in, x_in, dr["x4"], qw, sbq, bterm,
                         rs_col, PS)

    lg_c = load_vec_cols(p, "lgc", lg_in, D, ct, PS[6])
    lb_c = load_vec_cols(p, "lbc", lb_in, D, ct, PS[6])
    with p.scope():
        sb = ln_scratch(p)
        emit_ln(p, ct, dr["x4"], dr["junk"], sb, PS, dst_f32_dram=dr["t1"])
    with p.scope():
        sb = ln_scratch(p)
        sb["utt"] = p.sbuf("utt", [128, 128], BF16)
        emit_ln(p, ct, dr["t1"], dr["u"], sb, PS, gb=(lg_c, lb_c),
                ut_dram=dr["ut"], psb=psb16)

    Pre_bf = [p.sbuf(f"Preb{t}", [128, 512], BF16) for t in range(NNT)]
    Pim_bf = [p.sbuf(f"Pimb{t}", [128, 512], BF16) for t in range(NNT)]
    nPre_bf = [p.sbuf(f"nPreb{t}", [128, 512], BF16) for t in range(NNT)]
    nPim_bf = [p.sbuf(f"nPimb{t}", [128, 512], BF16) for t in range(NNT)]
    nlam_c = p.sbuf("nlam_c", [128, 2 * NNT], F32)
    plam_c = p.sbuf("plam_c", [128, 2 * NNT], F32)
    Cer = [p.sbuf(f"Cer{t}", [128, D], BF16) for t in range(NNT)]
    Cei = [p.sbuf(f"Cei{t}", [128, D], BF16) for t in range(NNT)]
    Kb = [p.sbuf(f"Kb{j}", [128, GCH], F32) for j in range(NDT)]
    c0_c = p.sbuf("c0c", [128, NDT], F32)
    WinR = p.sbuf("WinR", [128, 512], BF16)
    WinI = p.sbuf("WinI", [128, 512], BF16)
    pd_c = load_vec_cols(p, "pdc", pd_in, D, ct, PS[6])
    c3s = contextlib.ExitStack()
    p.scopes.append(c3s)
    lam = p.sbuf("lam", [1, 2048], F32)
    lr_row = lam[0:1, 0:512]
    li2p_row = lam[0:1, 512:1024]
    tmp2 = p.sbuf("tmp2", [1, 4096], F32)
    p.dma("sync", lam[0:1, 1024:1536], lr_in.rearrange("(a b) -> a b", a=1))
    p.dma("sync", lam[0:1, 1536:2048], li_in.rearrange("(a b) -> a b", a=1))
    p.wait("scalar", "sync")
    p.run("scalar", lambda e: e.activation(out=tmp2[0:1, 0:512],
                                           in_=lam[0:1, 1024:1536],
                                           func=AF.Exp))
    p.run("scalar", lambda e: e.activation(out=tmp2[0:1, 512:1024],
                                           in_=lam[0:1, 1536:2048],
                                           func=AF.Exp))
    p.wait("vector", "scalar")
    p.run("vector", lambda e: e.tensor_scalar(
        out=lr_row, in0=tmp2[0:1, 0:512], scalar1=-1.0, scalar2=None,
        op0=ALU.mult))
    p.run("vector", lambda e: e.tensor_scalar(
        out=li2p_row, in0=tmp2[0:1, 512:1024], scalar1=float(1.0 / TWOPI),
        scalar2=None, op0=ALU.mult))
    lr_c = p.sbuf("lr_c", [128, NNT], F32)
    li2p_c = p.sbuf("li2p_c", [128, NNT], F32)
    rows_to_cols(p, ct, lr_row, lr_c, NNT, PS[6], bounce)
    rows_to_cols(p, ct, li2p_row, li2p_c, NNT, PS[6], bounce)
    p.wait("vector", "sync")
    p.wait("scalar", "sync")
    p.settle()
    if dbg_out is not None:
        p.wait("gpsimd", "vector", "scalar", "tensor", "sync")
        p.dma("gpsimd", dbg_out[0:1, 156:668], lr_row)
        p.dma("gpsimd", dbg_out[0:128, 136:140], lr_c[:, :])
        p.dma("gpsimd", dbg_out[0:1, 668:1024], lam[0:1, 1024:1380])
        p.wait("gpsimd", "gpsimd")

    ramp_rep = p.sbuf("ramp_rep", [128, 512], F32)
    p.wait("tensor", "sync", "vector")
    p.run("tensor", lambda e: e.matmul(
        out=PS[5][:, 0:512], lhsT=ct["ones_row"][:, :], rhs=ct["ramp"][0:1, :],
        start=True, stop=True))
    p.wait("vector", "tensor")
    p.run("vector", lambda e: e.tensor_copy(out=ramp_rep[:, :],
                                            in_=PS[5][:, 0:512]))

    with p.scope():
        Pre = [p.sbuf(f"Pre{t}", [128, 512], F32) for t in range(NNT)]
        Pim = [p.sbuf(f"Pim{t}", [128, 512], F32) for t in range(NNT)]
        mag = p.sbuf("mag", [128, 512], F32)
        mcy = p.sbuf("mcy", [128, 512], F32)
        tfr = p.sbuf("tfr", [128, 512], F32)
        sn = p.sbuf("sn", [128, 512], F32)
        cn = p.sbuf("cn", [128, 512], F32)
        for t in range(NNT):
            p.run("vector", lambda e, t=t: e.tensor_scalar(
                out=mag[:, :], in0=ramp_rep[:, :], scalar1=lr_c[:, t:t + 1],
                scalar2=None, op0=ALU.mult))
            p.wait("scalar", "vector")
            p.run("scalar", lambda e: e.activation(out=mag[:, :],
                                                   in_=mag[:, :], func=AF.Exp))
            p.run("vector", lambda e, t=t: e.tensor_scalar(
                out=mcy[:, :], in0=ramp_rep[:, :], scalar1=li2p_c[:, t:t + 1],
                scalar2=None, op0=ALU.mult))
            p.run("vector", lambda e: e.tensor_scalar(
                out=tfr[:, :], in0=mcy[:, :], scalar1=MAGIC, scalar2=MAGIC,
                op0=ALU.add, op1=ALU.subtract))
            p.run("vector", lambda e: e.tensor_tensor(
                out=tfr[:, :], in0=mcy[:, :], in1=tfr[:, :], op=ALU.subtract))
            p.run("vector", lambda e: e.tensor_scalar(
                out=tfr[:, :], in0=tfr[:, :], scalar1=TWOPI, scalar2=None,
                op0=ALU.mult))
            p.wait("scalar", "vector")
            p.run("scalar", lambda e: e.activation(out=sn[:, :], in_=tfr[:, :],
                                                   func=AF.Sin))
            p.wait("vector", "scalar")
            p.run("vector", lambda e: e.tensor_scalar(
                out=mcy[:, :], in0=mcy[:, :], scalar1=0.25, scalar2=None,
                op0=ALU.add))
            p.run("vector", lambda e: e.tensor_scalar(
                out=tfr[:, :], in0=mcy[:, :], scalar1=MAGIC, scalar2=MAGIC,
                op0=ALU.add, op1=ALU.subtract))
            p.run("vector", lambda e: e.tensor_tensor(
                out=tfr[:, :], in0=mcy[:, :], in1=tfr[:, :], op=ALU.subtract))
            p.run("vector", lambda e: e.tensor_scalar(
                out=tfr[:, :], in0=tfr[:, :], scalar1=TWOPI, scalar2=None,
                op0=ALU.mult))
            p.wait("scalar", "vector")
            p.run("scalar", lambda e: e.activation(out=cn[:, :], in_=tfr[:, :],
                                                   func=AF.Sin))
            p.wait("vector", "scalar")
            p.run("vector", lambda e, t=t: e.tensor_tensor(
                out=Pre[t][:, :], in0=mag[:, :], in1=cn[:, :], op=ALU.mult))
            p.run("vector", lambda e, t=t: e.tensor_tensor(
                out=Pim[t][:, :], in0=mag[:, :], in1=sn[:, :], op=ALU.mult))
            p.run("vector", lambda e, t=t: e.tensor_copy(
                out=Pre_bf[t][:, :], in_=Pre[t][:, :]))
            p.run("vector", lambda e, t=t: e.tensor_copy(
                out=Pim_bf[t][:, :], in_=Pim[t][:, :]))
            p.run("vector", lambda e, t=t: e.tensor_scalar(
                out=nPre_bf[t][:, :], in0=Pre[t][:, :], scalar1=-1.0,
                scalar2=None, op0=ALU.mult))
            p.run("vector", lambda e, t=t: e.tensor_scalar(
                out=nPim_bf[t][:, :], in0=Pim[t][:, :], scalar1=-1.0,
                scalar2=None, op0=ALU.mult))
            p.runw("vector", lambda e, t=t: e.tensor_scalar(
                out=nlam_c[:, 2 * t:2 * t + 1], in0=Pre[t][:, 128:129],
                scalar1=-1.0, scalar2=None, op0=ALU.mult))
            p.runw("vector", lambda e, t=t: e.tensor_scalar(
                out=nlam_c[:, 2 * t + 1:2 * t + 2], in0=Pim[t][:, 128:129],
                scalar1=-1.0, scalar2=None, op0=ALU.mult))
            p.runw("vector", lambda e, t=t: e.tensor_copy(
                out=plam_c[:, 2 * t:2 * t + 1], in_=Pre[t][:, 128:129]))
            p.runw("vector", lambda e, t=t: e.tensor_copy(
                out=plam_c[:, 2 * t + 1:2 * t + 2], in_=Pim[t][:, 128:129]))
        p.settle()

    wr_c = p.sbuf("wr_c", [128, NNT], F32)
    wi_c = p.sbuf("wi_c", [128, NNT], F32)
    nwi_c = p.sbuf("nwi_c", [128, NNT], F32)
    with p.scope():
        rw = p.sbuf("rw", [1, 4608], F32)
        elr = rw[0:1, 0:512]
        cosi = rw[0:1, 512:1024]
        sini = rw[0:1, 1024:1536]
        er = rw[0:1, 1536:2048]
        ei = rw[0:1, 2048:2560]
        den = rw[0:1, 2560:3072]
        wrr = rw[0:1, 3072:3584]
        wir = rw[0:1, 3584:4096]
        p.wait("scalar", "vector")
        p.run("scalar", lambda e: e.activation(out=elr, in_=lr_row,
                                               func=AF.Exp))
        p.wait("vector", "scalar")
        row_frac_sincos(p, li2p_row, tmp2, sini, cosi)
        p.run("vector", lambda e: e.tensor_tensor(
            out=er, in0=elr, in1=cosi, op=ALU.mult))
        p.run("vector", lambda e: e.tensor_scalar(
            out=er, in0=er, scalar1=-1.0, scalar2=None, op0=ALU.add))
        p.run("vector", lambda e: e.tensor_tensor(
            out=ei, in0=elr, in1=sini, op=ALU.mult))
        p.run("vector", lambda e: e.tensor_scalar(
            out=den, in0=li2p_row, scalar1=TWOPI, scalar2=None, op0=ALU.mult))
        p.run("vector", lambda e: e.tensor_tensor(
            out=wrr, in0=lr_row, in1=lr_row, op=ALU.mult))
        p.run("vector", lambda e: e.scalar_tensor_tensor(
            out=den, in0=den, scalar=1.0, in1=den, op0=ALU.mult,
            op1=ALU.mult))
        p.run("vector", lambda e: e.tensor_tensor(
            out=den, in0=wrr, in1=den, op=ALU.add))
        p.run("vector", lambda e: e.reciprocal(out=den, in_=den))
        li_full = rw[0:1, 4096:4608]
        p.run("vector", lambda e: e.tensor_scalar(
            out=li_full, in0=li2p_row, scalar1=TWOPI, scalar2=None,
            op0=ALU.mult))
        p.run("vector", lambda e: e.tensor_tensor(
            out=wrr, in0=er, in1=lr_row, op=ALU.mult))
        p.run("vector", lambda e: e.tensor_tensor(
            out=tmp2[0:1, 0:512], in0=ei, in1=li_full, op=ALU.mult))
        p.run("vector", lambda e: e.tensor_tensor(
            out=wrr, in0=wrr, in1=tmp2[0:1, 0:512], op=ALU.add))
        p.run("vector", lambda e: e.tensor_tensor(
            out=wrr, in0=wrr, in1=den, op=ALU.mult))
        p.run("vector", lambda e: e.tensor_tensor(
            out=wir, in0=ei, in1=lr_row, op=ALU.mult))
        p.run("vector", lambda e: e.tensor_tensor(
            out=tmp2[0:1, 0:512], in0=er, in1=li_full, op=ALU.mult))
        p.run("vector", lambda e: e.tensor_tensor(
            out=wir, in0=wir, in1=tmp2[0:1, 0:512], op=ALU.subtract))
        p.run("vector", lambda e: e.tensor_tensor(
            out=wir, in0=wir, in1=den, op=ALU.mult))
        rows_to_cols(p, ct, wrr, wr_c, NNT, PS[6], bounce)
        rows_to_cols(p, ct, wir, wi_c, NNT, PS[6], bounce)
        p.wait("vector", "sync")
        p.runw("vector", lambda e: e.tensor_scalar(
            out=nwi_c[:, :], in0=wi_c[:, :], scalar1=-1.0, scalar2=None,
            op0=ALU.mult))
        p.settle()

    with p.scope():
        crs = p.sbuf("crs", [128, 512], F32)
        cis = p.sbuf("cis", [128, 512], F32)
        v1 = p.sbuf("v1C", [128, 128], F32)
        for dt_ in range(NDT):
            p.wait("sync", "tensor", "vector")
            p.dma("sync", crs[:, :], cr_in[128 * dt_:128 * (dt_ + 1), :])
            p.dma("sync", cis[:, :], ci_in[128 * dt_:128 * (dt_ + 1), :])
            for t in range(NNT):
                p.wait("tensor", "sync", "vector")
                p.run("tensor", lambda e, t=t: e.transpose(
                    out=PS[0][0:128, 0:128], in_=crs[:, 128 * t:128 * (t + 1)],
                    identity=ct["eye_f32"][:, :]))
                p.run("tensor", lambda e, t=t: e.transpose(
                    out=PS[1][0:128, 0:128], in_=cis[:, 128 * t:128 * (t + 1)],
                    identity=ct["eye_f32"][:, :]))
                p.wait("vector", "tensor")
                dsl = slice(128 * dt_, 128 * (dt_ + 1))
                p.run("vector", lambda e, t=t: e.tensor_scalar(
                    out=v1[:, :], in0=PS[0][0:128, 0:128],
                    scalar1=wr_c[:, t:t + 1], scalar2=None, op0=ALU.mult))
                p.run("vector", lambda e, t=t, dsl=dsl: e.scalar_tensor_tensor(
                    out=Cer[t][:, dsl], in0=PS[1][0:128, 0:128],
                    scalar=nwi_c[:, t:t + 1], in1=v1[:, :], op0=ALU.mult,
                    op1=ALU.add))
                p.run("vector", lambda e, t=t: e.tensor_scalar(
                    out=v1[:, :], in0=PS[0][0:128, 0:128],
                    scalar1=wi_c[:, t:t + 1], scalar2=None, op0=ALU.mult))
                p.run("vector", lambda e, t=t, dsl=dsl: e.scalar_tensor_tensor(
                    out=Cei[t][:, dsl], in0=PS[1][0:128, 0:128],
                    scalar=wr_c[:, t:t + 1], in1=v1[:, :], op0=ALU.mult,
                    op1=ALU.add))

    for j in range(NDT):
        p.wait("tensor", "vector")
        for t in range(NNT):
            p.run("tensor", lambda e, t=t, j=j, st=(t == 0):
                  e.matmul(out=PS[2][:, 0:GCH],
                           lhsT=Cer[t][:, 128 * j:128 * (j + 1)],
                           rhs=Pre_bf[t][:, 0:GCH], start=st, stop=False))
            p.run("tensor", lambda e, t=t, j=j, sp=(t == NNT - 1):
                  e.matmul(out=PS[2][:, 0:GCH],
                           lhsT=Cei[t][:, 128 * j:128 * (j + 1)],
                           rhs=nPim_bf[t][:, 0:GCH], start=False, stop=sp))
        p.wait("vector", "tensor")
        p.run("vector", lambda e, j=j: e.tensor_copy(out=Kb[j][:, :],
                                                     in_=PS[2][:, 0:GCH]))
        p.runw("vector", lambda e, j=j: e.tensor_tensor(
            out=c0_c[:, j:j + 1], in0=Kb[j][:, 0:1], in1=pd_c[:, j:j + 1],
            op=ALU.add))
    p.settle()

    with p.scope():
        mt = p.sbuf("mtW", [128, 512], F32)
        ft = p.sbuf("ftW", [128, 512], F32)
        st_ = p.sbuf("stW", [128, 512], F32)
        p.wait("tensor", "vector", "sync")
        p.run("tensor", lambda e: e.matmul(
            out=PS[3][:, 0:512], lhsT=ct["revramp"][:, :], rhs=lr_row,
            start=True, stop=True))
        p.wait("scalar", "tensor")
        p.run("scalar", lambda e: e.activation(out=mt[:, :],
                                               in_=PS[3][:, 0:512],
                                               func=AF.Exp))
        p.wait("tensor", "scalar")
        p.run("tensor", lambda e: e.matmul(
            out=PS[3][:, 0:512], lhsT=ct["revramp"][:, :], rhs=li2p_row,
            start=True, stop=True))
        p.wait("vector", "tensor", "scalar")
        p.run("vector", lambda e: e.tensor_scalar(
            out=ft[:, :], in0=PS[3][:, 0:512], scalar1=MAGIC, scalar2=MAGIC,
            op0=ALU.add, op1=ALU.subtract))
        p.run("vector", lambda e: e.tensor_tensor(
            out=ft[:, :], in0=PS[3][:, 0:512], in1=ft[:, :], op=ALU.subtract))
        p.run("vector", lambda e: e.tensor_scalar(
            out=ft[:, :], in0=ft[:, :], scalar1=TWOPI, scalar2=None,
            op0=ALU.mult))
        p.wait("scalar", "vector")
        p.run("scalar", lambda e: e.activation(out=st_[:, :], in_=ft[:, :],
                                               func=AF.Sin))
        p.wait("vector", "scalar")
        p.run("vector", lambda e: e.tensor_tensor(
            out=WinI[:, :], in0=mt[:, :], in1=st_[:, :], op=ALU.mult))
        p.run("vector", lambda e: e.tensor_scalar(
            out=ft[:, :], in0=PS[3][:, 0:512], scalar1=0.25, scalar2=None,
            op0=ALU.add))
        p.run("vector", lambda e: e.tensor_scalar(
            out=st_[:, :], in0=ft[:, :], scalar1=MAGIC, scalar2=MAGIC,
            op0=ALU.add, op1=ALU.subtract))
        p.run("vector", lambda e: e.tensor_tensor(
            out=ft[:, :], in0=ft[:, :], in1=st_[:, :], op=ALU.subtract))
        p.run("vector", lambda e: e.tensor_scalar(
            out=ft[:, :], in0=ft[:, :], scalar1=TWOPI, scalar2=None,
            op0=ALU.mult))
        p.wait("scalar", "vector")
        p.run("scalar", lambda e: e.activation(out=st_[:, :], in_=ft[:, :],
                                               func=AF.Sin))
        p.wait("vector", "scalar")
        p.run("vector", lambda e: e.tensor_tensor(
            out=WinR[:, :], in0=mt[:, :], in1=st_[:, :], op=ALU.mult))

    p.barrier()

    if dbg_out is not None:
        p.wait("gpsimd", "vector", "scalar", "tensor", "sync")
        p.dma("gpsimd", dbg_out[0:128, 140:144], wr_c[:, :])
        p.dma("gpsimd", dbg_out[0:128, 144:148], wi_c[:, :])
        p.dma("gpsimd", dbg_out[0:128, 284:412], Kb[0][:, :])
        p.wait("gpsimd", "gpsimd")

    p.barrier()
    p.scopes.pop()
    c3s.close()

    with p.scope():
        utc = p.sbuf("utc", [128, D], BF16)
        Psb = [[p.sbuf(f"Psb{t}{ri}", [128, D], F32) for ri in range(2)]
               for t in range(NNT)]
        Sa = [[p.sbuf(f"Sa{t}{ri}", [128, D], F32) for ri in range(2)]
              for t in range(NNT)]
        Sb = [[p.sbuf(f"Sb{t}{ri}", [128, D], F32) for ri in range(2)]
              for t in range(NNT)]
        Mb = [[p.sbuf(f"Mb{t}{c4}", [128, D], BF16) for c4 in range(4)]
              for t in range(NNT)]
        ytb = p.sbuf("ytb", [128, D], BF16)
        yev2 = [p.sbuf(f"yev{q}", [128, 128], BF16) for q in range(2)]
        ztile = p.sbuf("ztile", [128, 128], BF16)
        p.run("vector", lambda e: e.memset(ztile[:, :], 0.0))
        for t in range(NNT):
            for ri in range(2):
                p.run("vector", lambda e, t=t, ri=ri:
                      e.memset(Sa[t][ri][:, :], 0.0))
        comp_w = None
        for c in range(NCH):
            S_old = Sa if c % 2 == 0 else Sb
            S_new = Sb if c % 2 == 0 else Sa
            p.barrier()
            p.dma("sync", utc[:, :], dr["ut"][GCH * c:GCH * (c + 1), :])
            p.wait("tensor", "sync")
            ip_done = {}
            for g, (t, ri, W) in enumerate(
                    [(t, ri, W) for t in range(NNT)
                     for ri, W in ((0, WinR), (1, WinI))]):
                bp = [(4, 5), (0, 1)][g % 2]
                p.wait_at("tensor", "vector", ip_done.get(g - 2, 0))
                for hf in range(2):
                    p.run("tensor", lambda e, t=t, W=W, hf=hf, bp=bp:
                          e.matmul(out=PS[bp[hf]][:, 0:512],
                                   lhsT=W[:, 128 * t:128 * (t + 1)],
                                   rhs=utc[:, 512 * hf:512 * (hf + 1)],
                                   start=True, stop=True))
                p.wait("vector", "tensor")
                for hf in range(2):
                    p.run("vector", lambda e, t=t, ri=ri, hf=hf, bp=bp:
                          e.tensor_copy(
                              out=Psb[t][ri][:, 512 * hf:512 * (hf + 1)],
                              in_=PS[bp[hf]][:, 0:512]))
                ip_done[g] = p.s["vector"].v
            if c > 0:
                for t in range(NNT):
                    p.run("vector", lambda e, t=t, S=S_old: e.tensor_tensor(
                        out=Mb[t][0][:, :], in0=S[t][0][:, :],
                        in1=Cer[t][:, :], op=ALU.mult))
                    p.run("vector", lambda e, t=t, S=S_old: e.tensor_tensor(
                        out=Mb[t][1][:, :], in0=S[t][1][:, :],
                        in1=Cei[t][:, :], op=ALU.mult))
                    p.run("vector", lambda e, t=t, S=S_old: e.tensor_tensor(
                        out=Mb[t][2][:, :], in0=S[t][1][:, :],
                        in1=Cer[t][:, :], op=ALU.mult))
                    p.run("vector", lambda e, t=t, S=S_old: e.tensor_tensor(
                        out=Mb[t][3][:, :], in0=S[t][0][:, :],
                        in1=Cei[t][:, :], op=ALU.mult))
                p.wait("tensor", "vector")
                for t in range(NNT):
                    for c4, Wt in ((0, Pre_bf[t]), (1, nPre_bf[t]),
                                   (2, nPim_bf[t]), (3, nPim_bf[t])):
                        for hf in range(2):
                            p.run("tensor", lambda e, t=t, c4=c4, Wt=Wt,
                                  hf=hf, st=(t == 0 and c4 == 0),
                                  sp=(t == NNT - 1 and c4 == 3):
                                  e.matmul(
                                      out=PS[2 + hf][:, 0:512],
                                      lhsT=Wt[:, 1:129],
                                      rhs=Mb[t][c4][:, 512 * hf:512 * (hf + 1)],
                                      start=st, stop=sp))
                p.wait("vector", "tensor")
                for hf in range(2):
                    p.run("vector", lambda e, hf=hf: e.tensor_copy(
                        out=ytb[:, 512 * hf:512 * (hf + 1)],
                        in_=PS[2 + hf][:, 0:512]))
                dm_done = {}
                for dt_ in range(NDT):
                    b = dt_ % 2
                    p.wait("tensor", "vector")
                    p.run("tensor", lambda e, dt_=dt_, b=b: e.transpose(
                        out=psb16[:, 128 * b:128 * b + 128],
                        in_=ytb[:, 128 * dt_:128 * (dt_ + 1)],
                        identity=ct["eye_bf"][:, :]))
                    p.wait("vector", "tensor")
                    p.wait_at("vector", "gpsimd", dm_done.get(dt_ - 2, 0))
                    p.run("vector", lambda e, b=b, q=dt_ % 2: e.tensor_copy(
                        out=yev2[q][:, :],
                        in_=psb16[:, 128 * b:128 * b + 128]))
                    p.wait("gpsimd", "vector")
                    p.dma("gpsimd",
                          dr["ys"][128 * dt_:128 * (dt_ + 1),
                                   GCH * c:GCH * (c + 1)], yev2[dt_ % 2][:, :])
                    dm_done[dt_] = p.s["gpsimd"].v
            else:
                p.wait("gpsimd", "vector")
                for dt_ in range(NDT):
                    p.dma("gpsimd",
                          dr["ys"][128 * dt_:128 * (dt_ + 1), 0:GCH],
                          ztile[:, :])
            for t in range(NNT):
                p.run("vector", lambda e, t=t, S=S_old, N=S_new:
                      e.scalar_tensor_tensor(
                          out=N[t][0][:, :], in0=S[t][0][:, :],
                          scalar=plam_c[:, 2 * t:2 * t + 1],
                          in1=Psb[t][0][:, :],
                          op0=ALU.mult, op1=ALU.add))
                p.run("vector", lambda e, t=t, S=S_old, N=S_new:
                      e.scalar_tensor_tensor(
                          out=N[t][0][:, :], in0=S[t][1][:, :],
                          scalar=nlam_c[:, 2 * t + 1:2 * t + 2],
                          in1=N[t][0][:, :], op0=ALU.mult, op1=ALU.add))
                p.run("vector", lambda e, t=t, S=S_old, N=S_new:
                      e.scalar_tensor_tensor(
                          out=N[t][1][:, :], in0=S[t][1][:, :],
                          scalar=plam_c[:, 2 * t:2 * t + 1],
                          in1=Psb[t][1][:, :],
                          op0=ALU.mult, op1=ALU.add))
                p.run("vector", lambda e, t=t, S=S_old, N=S_new:
                      e.scalar_tensor_tensor(
                          out=N[t][1][:, :], in0=S[t][0][:, :],
                          scalar=plam_c[:, 2 * t + 1:2 * t + 2],
                          in1=N[t][1][:, :],
                          op0=ALU.mult, op1=ALU.add))

    if dbg_out is not None:
        p.wait("gpsimd", "vector", "scalar", "tensor", "sync")
        p.dma("gpsimd", dbg_out[0:128, 0:128], Kb[0][:, :])
        p.dma("gpsimd", dbg_out[0:128, 128:136], c0_c[:, :])
        p.dma("gpsimd", dbg_out[0:128, 148:156], nlam_c[:, :])
        p.wait("vector", "gpsimd")
        p.runw("vector", lambda e: e.memset(srow[0:1, 8:9], 42.0))
        p.wait("gpsimd", "vector")
        p.dma("gpsimd", dbg_out[0:1, 5:6], srow[0:1, 8:9])

    with p.scope():
        accv = p.sbuf("accv", [128, T], F32)
        accg = p.sbuf("accg", [128, T], F32)
        uk = p.sbuf("ukC", [128, T], BF16)
        x4k = p.sbuf("x4k", [128, T], F32)
        ysk = p.sbuf("ysk", [128, T], BF16)
        for k in range(NDT):
            p.barrier()
            p.dma("sync", uk[:, :], dr["u"][128 * k:128 * (k + 1), :])
            p.dma("sync", x4k[:, :], dr["x4"][128 * k:128 * (k + 1), :])
            p.dma("sync", ysk[:, :], dr["ys"][128 * k:128 * (k + 1), :])
            p.wait("vector", "sync")
            p.wait("gpsimd", "sync")
            p.run("vector", lambda e, k=k: e.scalar_tensor_tensor(
                out=accv[:, :], in0=uk[:, :], scalar=c0_c[:, k:k + 1],
                in1=x4k[:, :], op0=ALU.mult, op1=ALU.add))
            p.run("vector", lambda e: e.tensor_tensor(
                out=accv[:, :], in0=accv[:, :], in1=ysk[:, :], op=ALU.add))

            useg = uk[:, :].rearrange("p (c q) -> p c q", q=GCH)
            av = accv[:, :].rearrange("p (c q) -> p c q", q=GCH)
            ag = accg[:, :].rearrange("p (c q) -> p c q", q=GCH)
            for l in range(1, GCH):
                w_ = GCH - l
                eng = "vector"
                acc_ = av if eng == "vector" else ag
                p.run(eng, lambda e, l=l, w_=w_, acc_=acc_, k=k:
                      e.scalar_tensor_tensor(
                          out=acc_[:, :, l:GCH], in0=useg[:, :, 0:w_],
                          scalar=Kb[k][:, l:l + 1], in1=acc_[:, :, l:GCH],
                          op0=ALU.mult, op1=ALU.add))
            p.wait("gpsimd", "vector")
            p.dma("gpsimd", x5_out[128 * k:128 * (k + 1), :], accv[:, :])

    with p.scope():
        sb = ln_scratch(p)
        emit_ln(p, ct, x5_out, dr["junk"], sb, PS, dst_f32_dram=xln2_out,
                rmax=rmax)
        p.wait("tensor", "vector", "sync")
        p.run("tensor", lambda e: e.transpose(
            out=PS[0][0:1, 0:128], in_=rmax[:, 0:1],
            identity=ct["eye_f32"][:, :]))
        p.wait("vector", "tensor")
        p.runw("vector", lambda e: e.reduce_max(
            out=srow[0:1, 4:5], in_=PS[0][0:1, 0:128], axis=AX.X))
        p.wait("gpsimd", "vector")
        p.dma("gpsimd", mx_out[0:1, 0:1], srow[0:1, 4:5])
    p.finalize()
    return nc

_CACHE = {}


def _inputs_A(inputs, b):
    x = inputs["L0_attn_hidden_states"]
    m = {"x_in": np.ascontiguousarray(x[b].T)}
    m.update(const_inputs())
    m["gate_w"] = np.ascontiguousarray(inputs["L0_ffn_gate_w"])
    m["Wu"] = np.ascontiguousarray(inputs["L0_ffn_Wu"])
    m["Wv"] = np.ascontiguousarray(inputs["L0_ffn_Wv"])
    m["Wo"] = np.ascontiguousarray(inputs["L0_ffn_Wo"])
    for nm in ["W_xz", "W_dt", "b_dt", "A_log", "W_B", "W_C", "D_skip",
               "W_out", "v_th"]:
        m[nm] = np.ascontiguousarray(inputs["L0_attn_" + nm])
    return m


def kernel(**inputs):
    n = 8
    cores = list(range(n))
    if "A" not in _CACHE:
        _CACHE["A"] = build_launch_A()
        _CACHE["B"] = build_launch_B()
        _CACHE["C"] = build_launch_C()
        _CACHE["E"] = build_launch_E()
    ci = const_inputs()
    w1 = np.ascontiguousarray(inputs["L1_ffn_w1"])
    w1t = np.ascontiguousarray(w1.T)
    b1 = np.ascontiguousarray(inputs["L1_ffn_b1"])
    w2 = np.ascontiguousarray(inputs["L1_ffn_w2"])
    w2t = np.ascontiguousarray(w2.T)
    b2 = np.ascontiguousarray(inputs["L1_ffn_b2"])

    resA = run_bass_kernel_spmd(_CACHE["A"],
                                [_inputs_A(inputs, b) for b in range(n)],
                                core_ids=cores)
    m1 = np.stack([resA.results[b]["mx"][0] for b in range(n)])

    resB = run_bass_kernel_spmd(_CACHE["B"], [
        {"xln": resA.results[b]["xln1"], "w1": w1, "w1t": w1t, "b1": b1,
         "gmax": m1, **ci} for b in range(n)], core_ids=cores)
    m2 = np.stack([resB.results[b]["mx"][0] for b in range(n)])

    resC = run_bass_kernel_spmd(_CACHE["C"], [
        {"x": resA.results[b]["x3"], "h": resB.results[b]["h"],
         "w2": w2, "w2t": w2t, "b2": b2, "gmax": m2,
         "Lr": inputs["L1_attn_Lambda_real"],
         "Li": inputs["L1_attn_Lambda_imag"],
         "Cr": inputs["L1_attn_C_real"], "Ci": inputs["L1_attn_C_imag"],
         "pD": inputs["L1_attn_param_D"], "lg": inputs["L1_attn_ln_gamma"],
         "lb": inputs["L1_attn_ln_beta"], **ci} for b in range(n)],
        core_ids=cores)
    m3 = np.stack([resC.results[b]["mx"][0] for b in range(n)])

    resD = run_bass_kernel_spmd(_CACHE["B"], [
        {"xln": resC.results[b]["xln2"], "w1": w1, "w1t": w1t, "b1": b1,
         "gmax": m3, **ci} for b in range(n)], core_ids=cores)
    m4 = np.stack([resD.results[b]["mx"][0] for b in range(n)])

    resE = run_bass_kernel_spmd(_CACHE["E"], [
        {"x": resC.results[b]["x5"], "h": resD.results[b]["h"],
         "w2": w2, "w2t": w2t, "b2": b2, "gmax": m4, **ci}
        for b in range(n)], core_ids=cores)
    return np.stack([resE.results[b]["y"].T for b in range(n)])
